# version 13
# baseline (speedup 1.0000x reference)
import numpy as np
import ml_dtypes
from contextlib import ExitStack

import concourse.bass as bass
import concourse.mybir as mybir
from concourse.bass_utils import run_bass_kernel_spmd

F32 = mybir.dt.float32
BF16 = mybir.dt.bfloat16
I32 = mybir.dt.int32
AF = mybir.ActivationFunctionType
ALU = mybir.AluOpType
AX = mybir.AxisListType


class Cfg:
    def __init__(self, S=8192):
        self.B = 2; self.S = S; self.D = 1024; self.H = 16; self.DH = 64
        self.E = 8; self.HID = 2048; self.CORES = 8
        self.GC = self.CORES // self.B
        self.HPC = self.H // self.GC
        self.DL = self.HPC * self.DH
        self.TQ = S // self.GC
        self.SW = 512
        self.NS = S // self.SW
        self.NKB = S // 128
        self.CAPT = self.B * S // self.E
        self.CSLOT = 384
        self.XROW = self.CSLOT + 2
        self.AUGW = self.D + 128
        self.NIT = 28
        self.DCH = self.D // 128
        self.TQS = self.TQ // 128
        self.XRT = self.CSLOT + 1 + self.TQS + 1
        self.GST = self.TQ // 512
        self.HT = self.HID // 128


class Emit:
    ENGS = ("gp", "sy", "pe", "ac", "dv")
    BMAP = {"gp": "gpsimd", "sy": "sync", "pe": "tensor", "ac": "scalar",
            "dv": "vector"}

    def __init__(self, nc):
        self.nc = nc
        self.cnt = {}
        self.sems = {}
        self.cur = None
        self.eng = None
        self._eobj = dict(gp=nc.gpsimd, sy=nc.sync, pe=nc.tensor,
                          ac=nc.scalar, dv=nc.vector)
        self._lastwait = {}

    def sem(self, name):
        if name not in self.sems:
            self.sems[name] = self.nc.alloc_semaphore(name)
            self.cnt[name] = 0
        return self.sems[name]

    def phase(self, block, fn):
        base = dict(self.cnt)
        final = None
        for name in self.ENGS:
            self.cnt = dict(base)
            self.cur = name
            self.eng = self._eobj[name]
            self._lastwait = {}
            getattr(block, self.BMAP[name])(lambda _e: fn(self))
            final = self.cnt
        self.cnt = final

    def wait(self, sname, val=None):
        if val is None:
            val = self.cnt[sname]
        if val <= 0:
            return
        if self._lastwait.get(sname, -1) >= val:
            return
        self._lastwait[sname] = val
        self.eng.wait_ge(self.sem(sname), val)

    SERIAL = False
    NO_XW = False

    def do(self, engname, fn, sem=None, n=1, waits=()):
        if sem is not None:
            self.sem(sem)
        if self.cur == engname:
            if Emit.SERIAL:
                for s_ in ("PE", "DV", "AC", "GP"):
                    if s_ in self.cnt:
                        self.wait(s_, self.cnt[s_])
            for w in waits:
                if isinstance(w, tuple):
                    self.wait(w[0], w[1])
                else:
                    self.wait(w)
            inst = fn(self.eng)
            if sem is not None and inst is not None:
                inst.then_inc(self.sems[sem], n)
        if sem is not None:
            self.cnt[sem] += n
        return (sem, self.cnt[sem]) if sem is not None else None


def build(cfg: Cfg, dbg=()):
    nc = bass.Bass("TRN2", target_bir_lowering=False)
    c = cfg
    S, D, TQ, DL, HPC, DH = c.S, c.D, c.TQ, c.DL, c.HPC, c.DH
    DHA = DH + 1
    NKB, NS = c.NKB, c.NS

    def T(n, shp, dt=F32, kind="ExternalInput"):
        return nc.dram_tensor(n, shp, dt, kind=kind).ap()

    xT_h = T("xT_h", [D, S])
    x_q = T("x_q", [TQ, D])
    wqT_i = T("wqT_i", [D, DL]); wkT_i = T("wkT_i", [D, DL]); wvT_i = T("wvT_i", [D, DL])
    bq_l = T("bq_l", [2, 128]); bk_l = T("bk_l", [2, 128]); bv_l = T("bv_l", [2, 128])
    bv_r = T("bv_r", [1, DL])
    woT_i = T("woT_i", [D, D]); bo_r = T("bo_r", [1, D])
    wgT_i = T("wgT_i", [D, 16])
    w1T_i = T("w1T_i", [c.E * D, c.HID]); w3T_i = T("w3T_i", [c.E * D, c.HID])
    w2T_i = T("w2T_i", [c.E * c.HID, D])
    idxq = T("idxq", [128, 8], I32)
    y_out = T("y_out", [TQ, D], F32, kind="ExternalOutput")

    xT_d = T("xT_d", [D, S], BF16, "Internal")
    wqT_bf = T("wqT_bf", [D, DL], BF16, "Internal")
    wkT_bf = T("wkT_bf", [D, DL], BF16, "Internal")
    wvT_bf = T("wvT_bf", [D, DL], BF16, "Internal")
    woT_bf = T("woT_bf", [D, D], BF16, "Internal")
    wgT_bf = T("wgT_bf", [D, 16], BF16, "Internal")
    w1T_d = T("w1T_d", [c.E * D, c.HID], BF16, "Internal")
    w3T_d = T("w3T_d", [c.E * D, c.HID], BF16, "Internal")
    w2T_d = T("w2T_d", [c.E * c.HID, D], BF16, "Internal")
    khead_d = T("khead_d", [HPC * 64, S], BF16, "Internal")
    qhat_d = T("qhat_d", [HPC * DHA, S], BF16, "Internal")
    attnT_d = T("attnT_d", [DL, S], BF16, "Internal")
    attnT_full = T("attnT_full", [D, S], BF16, "Internal")
    l_d = T("l_d", [1, S], F32, "Internal")
    rl_d = T("rl_d", [1, S], F32, "Internal")
    sT_dr = T("sT_dr", [c.E, TQ], F32, "Internal")
    s_full = T("s_full", [c.CORES * c.E, TQ], F32, "Internal")
    s_re = T("s_re", [128, c.CORES * TQ // 16], F32, "Internal")
    ln1_aug = T("ln1_aug", [TQ, c.AUGW], BF16, "Internal")
    tokslot_d = T("tokslot_d", [c.E * c.XRT, 1], I32, "Internal")
    xg_d = T("xg_d", [c.E * c.XROW, c.AUGW], BF16, "Internal")
    ysl_d = T("ysl_d", [c.E * c.XROW, D], BF16, "Internal")

    dbg_t = {}
    for name, shp, dt in [("d_attnT", [DL, S], BF16), ("d_ln1", [TQ, D], F32),
                          ("d_sT", [c.E, TQ], F32), ("d_prm", [8, 16], F32),
                          ("d_wl", [c.E, TQ], F32), ("d_slot", [c.E, TQ], F32),
                          ("d_qd", [HPC * DHA, S], BF16),
                          ("d_kd", [HPC * 64, S], BF16),
                          ("d_vd", [DL, S], BF16),
                          ("d_vtok", [128, (S // 128) * DL], BF16)]:
        if name in dbg:
            dbg_t[name] = T(name, shp, dt, "ExternalOutput")

    st = ExitStack()
    with st:
        e = st.enter_context
        sb = lambda n, shp, dt: e(nc.sbuf_tensor(n, shp, dt))

        ones_rf = sb("ones_rf", [1, 128], F32)
        ones_cb = sb("ones_cb", [128, 1], BF16)
        ident_b = sb("ident_b", [128, 128], BF16)
        ident_f = sb("ident_f", [128, 128], F32)
        wmask = sb("wmask", [128, 896], BF16)
        g8 = sb("g8", [128, 8], F32)
        g8t = sb("g8t", [8, 128], F32)
        bqT = sb("bqT", [128, 2], F32)
        bkT = sb("bkT", [128, 2], F32)
        bvT = sb("bvT", [128, 2], F32)
        bo_row = sb("bo_row", [1, D], F32)
        idxq_sb = sb("idxq_sb", [128, c.DCH], I32)
        zrow_b = sb("zrow_b", [128, c.AUGW], BF16)
        zrow_f = sb("zrow_f", [128, D], F32)
        tokrow = sb("tokrow", [128, c.TQS], I32)
        zi32 = sb("zi32", [8, c.XRT], I32)
        stats = sb("stats", [1, 64], F32)
        ln_ms = sb("ln_ms", [128, 6], F32)
        eps_c = sb("eps_c", [128, 1], F32)

        em = Emit(nc)
        for s_ in ["LDG", "XBR", "LDS", "PE", "DV", "AC", "GP", "STO", "IND",
                   "CC1", "CC2", "ATS", "LRL", "SYA", "SYB", "XQG", "XMW",
                   "PRJ", "KH0", "KH1", "CX", "CW", "COG", "BIA", "LN1S",
                   "STD", "SRE", "AGL", "XGZ", "AUGC", "XGL", "WOT", "XB0", "XB1", "XS0", "XS1", "PR0", "PR1", "PRM", "XT0", "XT1", "LDI", "LS0", "LS1", "RL0", "RL1", "AT0", "AT1"]:
            em.sem(s_)
        for i in range(c.E):
            em.sem(f"XW{i}"); em.sem(f"SC{i}"); em.sem(f"YS{i}"); em.sem(f"WC{i}")

        def ln_emit(E, slab, out_bf):
            E.do("dv", lambda v: v.reduce_sum(ln_ms[:, 0:1], slab[:, :], AX.X),
                 sem="DV", waits=[("AC", E.cnt["AC"]), ("DV", E.cnt["DV"])])
            E.do("dv", lambda v: v.tensor_scalar_mul(
                ln_ms[:, 1:2], ln_ms[:, 0:1], -1.0 / D), sem="DV", waits=["DV"])
            E.do("ac", lambda s: s.activation(
                sq_sc[:, :], slab[:, :], AF.Square, bias=ln_ms[:, 1:2],
                accum_out=ln_ms[:, 2:3]), sem="AC", waits=[("DV", E.cnt["DV"])])
            E.do("ac", lambda s: s.activation(
                ln_ms[:, 3:4], ln_ms[:, 2:3], AF.Sqrt, bias=eps_c[:, :],
                scale=1.0 / D), sem="AC", waits=["AC"])
            E.do("dv", lambda v: v.reciprocal(ln_ms[:, 4:5], ln_ms[:, 3:4]),
                 sem="DV", waits=[("AC", E.cnt["AC"])])
            E.do("dv", lambda v: v.tensor_mul(
                ln_ms[:, 5:6], ln_ms[:, 4:5], ln_ms[:, 4:5]), sem="DV", waits=["DV"])
            E.do("dv", lambda v: v.tensor_scalar(
                ln_ms[:, 0:1], ln_ms[:, 2:3], 1.0 / D, 1e-5, ALU.mult, ALU.add),
                sem="DV", waits=["DV"])
            E.do("dv", lambda v: v.tensor_mul(
                ln_ms[:, 5:6], ln_ms[:, 5:6], ln_ms[:, 0:1]), sem="DV", waits=["DV"])
            E.do("dv", lambda v: v.tensor_scalar(
                ln_ms[:, 5:6], ln_ms[:, 5:6], -0.5, 1.5, ALU.mult, ALU.add),
                sem="DV", waits=["DV"])
            E.do("dv", lambda v: v.tensor_mul(
                ln_ms[:, 4:5], ln_ms[:, 4:5], ln_ms[:, 5:6]), sem="DV", waits=["DV"])
            E.do("dv", lambda v: v.tensor_scalar(
                slab[:, :], slab[:, :], ln_ms[:, 1:2], ln_ms[:, 4:5],
                ALU.add, ALU.mult), sem="DV", waits=["DV"])
            if out_bf is not None:
                E.do("dv", lambda v: v.tensor_copy(out_bf[:, :], slab[:, :]),
                     sem="DV", waits=["DV"])

        with nc.Block() as b0:
            def p0(E):
                for t_ in (ones_rf, ones_cb, ident_b, ident_f, wmask, g8, g8t):
                    E.do("dv", lambda v, t_=t_: v.memset(t_[:, :], 1.0), sem="DV")
                E.do("dv", lambda v: v.memset(zrow_b[:, :], 0.0), sem="DV")
                E.do("dv", lambda v: v.memset(zrow_f[:, :], 0.0), sem="DV")
                E.do("dv", lambda v: v.memset(eps_c[:, :], 1e-5), sem="DV")
                gsel = [(ident_b, [[1, 128]], ALU.is_equal, 0, -1),
                        (ident_f, [[1, 128]], ALU.is_equal, 0, -1),
                        (wmask, [[1, 896]], ALU.is_ge, -384, -1),
                        (g8, [[-16, 8]], ALU.is_ge, 0, 1),
                        (g8, [[16, 8]], ALU.is_ge, 15, -1),
                        (g8t, [[1, 128]], ALU.is_ge, 0, -16),
                        (g8t, [[-1, 128]], ALU.is_ge, 15, 16)]
                for t_, pat, op, base, cm in gsel:
                    E.do("gp", lambda g, t_=t_, pat=pat, op=op, base=base, cm=cm:
                         g.affine_select(t_[:, :], t_[:, :], pat, op, 0.0,
                                         base=base, channel_multiplier=cm),
                         sem="GP", waits=["DV", "GP"])
                E.do("sy", lambda s: s.dma_start(bo_row[:, :], bo_r), sem="LDS", n=16)
                E.do("sy", lambda s: s.dma_start(
                    idxq_sb[:, :], idxq), sem="LDI", n=16)
                E.do("gp", lambda g: g.dma_start(xT_d, xT_h), sem="CX", n=16)
                for dst, src in [(wqT_bf, wqT_i), (wkT_bf, wkT_i), (wvT_bf, wvT_i)]:
                    E.do("gp", lambda g, dst=dst, src=src: g.dma_start(dst, src),
                         sem="CW", n=16)
                for dst, src in [(woT_bf, woT_i), (wgT_bf, wgT_i)]:
                    E.do("gp", lambda g, dst=dst, src=src: g.dma_start(dst, src),
                         sem="COG", n=16)
                E.do("gp", lambda g: g.memset(zi32[:, :], 0), sem="GP",
                     waits=["GP"])
                E.do("gp", lambda g: g.iota(
                    tokrow[:, :], [[128, c.TQS]], channel_multiplier=1,
                    allow_small_or_imprecise_dtypes=True), sem="GP")
                zw = [("DV", E.cnt["DV"]), ("GP", E.cnt["GP"])]
                for ei in range(c.E):
                    for rr in range(0, c.XROW, 128):
                        n_ = min(128, c.XROW - rr)
                        E.do("sy", lambda s, ei=ei, rr=rr, n_=n_: s.dma_start(
                            ysl_d[ei * c.XROW + rr:ei * c.XROW + rr + n_, :],
                            zrow_b[0:n_, 0:D]), sem="XGZ", n=16, waits=zw)
                E.do("sy", lambda s: s.dma_start(
                    tokslot_d.rearrange("(a b) o -> a (b o)", b=c.XRT),
                    zi32[:, :]), sem="XGZ", n=16, waits=zw)

            em.phase(b0, p0)

        stA = ExitStack()
        with stA:
            eA = stA.enter_context
            v_tok = eA(nc.sbuf_tensor("v_tok", [128, (S // 128) * DL], BF16))
            st1 = ExitStack()
            with st1:
                e1 = st1.enter_context
                wqT = [e1(nc.sbuf_tensor(f"wqT{i}", [128, DL], BF16))
                       for i in range(c.DCH)]
                wkT = [e1(nc.sbuf_tensor(f"wkT{i}", [128, DL], BF16))
                       for i in range(c.DCH)]
                wvT = [e1(nc.sbuf_tensor(f"wvT{i}", [128, DL], BF16))
                       for i in range(c.DCH)]
                xts = [[e1(nc.sbuf_tensor(f"xts{i}_{j}", [128, 512], BF16))
                        for j in range(c.DCH)] for i in range(2)]
                prj = [e1(nc.sbuf_tensor(f"prj{i}", [128, 512], BF16))
                       for i in range(2)]
                btmp = [e1(nc.sbuf_tensor(f"btmp{i}", [2, 128], F32))
                        for i in range(3)]
                bv_bc = e1(nc.sbuf_tensor("bv_bc", [128, DL], F32))
                bvr_s = e1(nc.sbuf_tensor("bvr_s", [1, DL], F32))
                sqs = e1(nc.sbuf_tensor("sqs", [128, 512], BF16))
                karr = e1(nc.sbuf_tensor("karr", [1, HPC * NS], F32))
                qn_s = e1(nc.sbuf_tensor("qn_s", [1, 512], F32))
                mh_s = e1(nc.sbuf_tensor("mh_s", [1, 512], BF16))
                p_pr = [e1(nc.psum_tensor(f"p_pr{i}", [128, 512], F32))
                        for i in range(2)]
                p_sm = [e1(nc.psum_tensor(f"p_sm{i}", [1, 512], F32))
                        for i in range(2)]
                p_bT = e1(nc.psum_tensor("p_bT", [128, 2], F32))
                p_bv = [e1(nc.psum_tensor(f"p_bv{i}", [128, 256], F32))
                        for i in range(2)]

                with nc.Block() as b1:
                    def p1(E):
                        for i, bin_ in enumerate([bq_l, bk_l, bv_l]):
                            E.do("gp", lambda g, bin_=bin_, i=i: g.dma_start(
                                btmp[i][:, :], bin_), sem="BIA", n=16)
                        E.do("gp", lambda g: g.dma_start(bvr_s[:, :], bv_r),
                             sem="BIA", n=16)
                        for i, (bt, scl) in enumerate([(bqT, 0.125), (bkT, 1.0),
                                                       (bvT, 1.0)]):
                            E.do("pe", lambda t, i=i: t.transpose(
                                p_bT[:, :], btmp[i][:, :], ident_f[0:2, 0:2]),
                                sem="PE", waits=[("BIA", 64), ("GP", E.cnt["GP"]),
                                                 ("DV", E.cnt["DV"])])
                            E.do("dv", lambda v, bt=bt, scl=scl: v.tensor_scalar_mul(
                                bt[:, :], p_bT[:, :], scl), sem="DV", waits=["PE"])
                        for hf in range(2):
                            E.do("pe", lambda t, hf=hf: t.matmul(
                                p_bv[hf][:, 0:128], ones_rf[:, :],
                                bvr_s[0:1, hf * 128:(hf + 1) * 128],
                                start=True, stop=True),
                                sem="PE", waits=[("BIA", 64), ("DV", E.cnt["DV"])])
                            E.do("dv", lambda v, hf=hf: v.tensor_copy(
                                bv_bc[:, hf * 128:(hf + 1) * 128],
                                p_bv[hf][:, 0:128]), sem="DV", waits=["PE"])
                        bvbc_ok = ("DV", E.cnt["DV"])
                        for wbf, wt in [(wqT_bf, wqT), (wkT_bf, wkT),
                                        (wvT_bf, wvT)]:
                            for cc in range(c.DCH):
                                E.do("sy", lambda s, wbf=wbf, wt=wt, cc=cc:
                                     s.dma_start(wt[cc][:, :],
                                                 wbf[cc * 128:(cc + 1) * 128, :]),
                                     sem="SYA", n=16, waits=[("CW", 48)])
                        wT_ok = ("SYA", E.cnt["SYA"])

                        evac_ps = [None, None]
                        prj_free = [None, None]
                        sqs_free = None
                        psm_free = [None, None]
                        pbv_free = [bvbc_ok, bvbc_ok]
                        for wi, kind in enumerate(["k", "q", "v"]):
                            wt = {"k": wkT, "q": wqT, "v": wvT}[kind]
                            bt = {"k": bkT, "q": bqT, "v": bvT}[kind]
                            scl = 0.125 if kind == "q" else 1.0
                            pe_round = {}
                            for tc in range(NS):
                                bsel = tc % 2
                                ldsem = "SYB" if bsel else "SYA"
                                wld = [wT_ok, ("CX", 16)]
                                if tc >= 2:
                                    wld.append(("PE", pe_round[tc - 2]))
                                elif wi > 0:
                                    wld.append(("PE", E.cnt["PE"]))
                                for cc in range(c.DCH):
                                    E.do("sy", lambda s, tc=tc, cc=cc, bsel=bsel:
                                         s.dma_start(
                                             xts[bsel][cc][:, :],
                                             xT_d[cc * 128:(cc + 1) * 128,
                                                  tc * 512:(tc + 1) * 512]),
                                         sem=ldsem, n=16, waits=wld)
                                ldval = (ldsem, E.cnt[ldsem])
                                if kind == "v":
                                    for tb in range(4):
                                        pvi = tb % 2
                                        pv = p_bv[pvi]
                                        mm_w = [ldval, pbv_free[pvi]]
                                        for cc in range(c.DCH):
                                            E.do("pe", lambda t, pv=pv, cc=cc,
                                                 tb=tb, bsel=bsel:
                                                 t.matmul(
                                                     pv[:, :],
                                                     xts[bsel][cc][:, tb * 128:
                                                                   (tb + 1) * 128],
                                                     wvT[cc][:, :],
                                                     start=(cc == 0),
                                                     stop=(cc == c.DCH - 1)),
                                                 sem="PE",
                                                 waits=mm_w if cc == 0 else ())
                                        tg = tc * 4 + tb
                                        E.do("dv", lambda v, pv=pv, tg=tg:
                                             v.scalar_tensor_tensor(
                                                 v_tok[:, tg * DL:(tg + 1) * DL],
                                                 pv[:, :], 0.0, bv_bc[:, :],
                                                 ALU.add, ALU.add),
                                             sem="DV", waits=["PE", bvbc_ok])
                                        pbv_free[pvi] = ("DV", E.cnt["DV"])
                                    pe_round[tc] = E.cnt["PE"]
                                    continue
                                for dt in range(DL // 128):
                                    pidx = (tc * 2 + dt) % 2
                                    pp = p_pr[pidx]
                                    mm_w = [ldval]
                                    if evac_ps[pidx] is not None:
                                        mm_w.append(evac_ps[pidx])
                                    for cc in range(c.DCH):
                                        E.do("pe", lambda t, pp=pp, wt=wt, cc=cc,
                                             dt=dt, bsel=bsel:
                                             t.matmul(
                                                 pp[:, :],
                                                 wt[cc][:, dt * 128:(dt + 1) * 128],
                                                 xts[bsel][cc][:, :],
                                                 start=(cc == 0),
                                                 stop=(cc == c.DCH - 1)),
                                             sem="PE", waits=mm_w if cc == 0 else ())
                                    sidx = (tc * 2 + dt) % 2
                                    sg = prj[sidx]
                                    ew = ["PE"]
                                    if prj_free[sidx] is not None:
                                        ew.append(prj_free[sidx])
                                    ev = E.do("ac", lambda s, pp=pp, sg=sg, bt=bt,
                                              dt=dt, scl=scl:
                                              s.activation(sg[:, :], pp[:, :],
                                                           AF.Identity,
                                                           bias=bt[:, dt:dt + 1],
                                                           scale=scl),
                                              sem="AC", waits=ew)
                                    evac_ps[pidx] = ev
                                    if kind == "k":
                                        dmas = [(khead_d[dt * 128:(dt + 1) * 128,
                                                         tc * 512:(tc + 1) * 512],
                                                 sg[:, :])]
                                    else:
                                        dmas = []
                                        for hh in range(2):
                                            h = dt * 2 + hh
                                            dmas.append(
                                                (qhat_d[h * DHA:h * DHA + 64,
                                                        tc * 512:(tc + 1) * 512],
                                                 sg[hh * 64:(hh + 1) * 64, :]))
                                    prs = "PR0" if sidx == 0 else "PR1"
                                    for dst_, src_ in dmas:
                                        E.do("gp", lambda g, dst_=dst_, src_=src_:
                                             g.dma_start(dst_, src_),
                                             sem=prs, n=16, waits=[ev])
                                    prj_free[sidx] = (prs, E.cnt[prs])
                                    sqw = [ev]
                                    if sqs_free is not None:
                                        sqw.append(sqs_free)
                                    E.do("ac", lambda s, sg=sg: s.activation(
                                        sqs[:, :], sg[:, :], AF.Square),
                                        sem="AC", waits=sqw)
                                    sq_ok = ("AC", E.cnt["AC"])
                                    for hh in range(2):
                                        h = dt * 2 + hh
                                        pb2 = hh
                                        mmw = [sq_ok]
                                        if psm_free[pb2] is not None:
                                            mmw.append(psm_free[pb2])
                                        E.do("pe", lambda t, hh=hh, pb2=pb2:
                                             t.matmul(
                                                 p_sm[pb2][:, :],
                                                 ones_cb[hh * 64:hh * 64 + 64, :],
                                                 sqs[hh * 64:(hh + 1) * 64, :],
                                                 start=True, stop=True),
                                             sem="PE", waits=mmw)
                                        if kind == "k":
                                            E.do("dv", lambda v, h=h, tc=tc, pb2=pb2:
                                                 v.reduce_max(
                                                     karr[0:1, h * NS + tc:
                                                          h * NS + tc + 1],
                                                     p_sm[pb2][:, :], AX.X),
                                                 sem="DV", waits=["PE"])
                                            psm_free[pb2] = ("DV", E.cnt["DV"])
                                        else:
                                            E.do("ac", lambda s, pb2=pb2:
                                                 s.activation(qn_s[:, :],
                                                              p_sm[pb2][:, :],
                                                              AF.Sqrt),
                                                 sem="AC",
                                                 waits=["PE", ("DV", E.cnt["DV"])])
                                            E.do("dv", lambda v, h=h:
                                                 v.tensor_scalar(
                                                     mh_s[:, :], qn_s[:, :],
                                                     stats[0:1, h:h + 1], None,
                                                     ALU.mult),
                                                 sem="DV",
                                                 waits=[("AC", E.cnt["AC"]),
                                                        ("PRM", E.cnt["PRM"])])
                                            E.do("gp", lambda g, h=h, tc=tc:
                                                 g.dma_start(
                                                     qhat_d[h * DHA + 64:
                                                            h * DHA + 65,
                                                            tc * 512:(tc + 1) * 512],
                                                     mh_s[:, :]),
                                                 sem="PRM", n=16,
                                                 waits=[("DV", E.cnt["DV"]),
                                                        ("PRM", E.cnt["PRM"])])
                                            psm_free[pb2] = ("AC", E.cnt["AC"])
                                    sqs_free = ("PE", E.cnt["PE"])
                                pe_round[tc] = E.cnt["PE"]
                            if kind == "k":
                                for h in range(HPC):
                                    E.do("dv", lambda v, h=h: v.reduce_max(
                                        stats[0:1, 32 + h:33 + h],
                                        karr[0:1, h * NS:(h + 1) * NS], AX.X),
                                        sem="DV", waits=["DV"])
                                    E.do("ac", lambda s, h=h: s.activation(
                                        stats[0:1, 16 + h:17 + h],
                                        stats[0:1, 32 + h:33 + h], AF.Sqrt),
                                        sem="AC", waits=[("DV", E.cnt["DV"]),
                                                         ("AC", E.cnt["AC"])])
                                    E.do("dv", lambda v, h=h: v.tensor_scalar_mul(
                                        stats[0:1, h:h + 1],
                                        stats[0:1, 16 + h:17 + h], -1.0),
                                        sem="DV", waits=[("AC", E.cnt["AC"])])
                    em.phase(b1, p1)
            PRJ_ALL = [("PR0", em.cnt["PR0"]), ("PR1", em.cnt["PR1"]),
                       ("PRM", em.cnt["PRM"])]

            st2 = ExitStack()
            with st2:
                e2 = st2.enter_context
                kh_buf = [e2(nc.sbuf_tensor(f"khb{i}", [DHA, S], BF16))
                          for i in range(2)]
                qh_buf = [e2(nc.sbuf_tensor(f"qhb{i}", [DHA, S], BF16))
                          for i in range(2)]
                v_aug = [e2(nc.sbuf_tensor(f"vaug{i}", [128, NKB * DHA], BF16))
                         for i in range(2)]
                pt = [e2(nc.sbuf_tensor(f"pt{i}", [128, 1024], BF16))
                      for i in range(4)]
                raw_h = e2(nc.sbuf_tensor("raw_h", [64, S], F32))
                rlb = [e2(nc.sbuf_tensor(f"rlb{i}", [64, 512], F32))
                       for i in range(2)]
                ltile = e2(nc.sbuf_tensor("ltile", [NS, c.SW], F32))
                l_st = [e2(nc.sbuf_tensor(f"l_st{i}", [1, 512], F32))
                        for i in range(2)]
                rl_st = [e2(nc.sbuf_tensor(f"rlst{i}", [1, 512], F32))
                         for i in range(2)]
                att_st = [e2(nc.sbuf_tensor(f"attst{i}", [64, 512], BF16))
                          for i in range(2)]
                s_ps = [e2(nc.psum_tensor(f"s_ps{i}", [128, 1024], F32))
                        for i in range(3)]
                ot_ps = [e2(nc.psum_tensor(f"ot_ps{i}", [DHA, 512], F32))
                         for i in range(2)]


                with nc.Block() as b2:
                    def p2(E):
                        for i in range(2):
                            E.do("dv", lambda v, i=i: v.memset(
                                v_aug[i][:, :].rearrange("p (k a) -> p k a", a=DHA)
                                [:, :, 64:65], 1.0), sem="DV")
                            E.do("dv", lambda v, i=i: v.memset(
                                kh_buf[i][64:65, :], 1.0), sem="DV")

                        vab_ready = [None, None]
                        ot_evac = [None, None]
                        exp_of_sp = [None, None, None]
                        av_of_pt = [None, None, None, None]
                        attn_of_hb = [None, None]
                        kq_val = {}
                        for h in range(HPC):
                            hb = h % 2
                            vb = h % 2
                            lrl_hs = ("LRL", E.cnt["LRL"])
                            rl_hs = [("RL0", E.cnt["RL0"]), ("RL1", E.cnt["RL1"])]
                            def kq_load(E, hh):
                                hbb = hh % 2
                                ks = "KH0" if hbb == 0 else "KH1"
                                lw = PRJ_ALL + [("DV", E.cnt["DV"])]
                                if attn_of_hb[hbb] is not None:
                                    lw.append(attn_of_hb[hbb])
                                E.do("gp", lambda g, hh=hh, hbb=hbb: g.dma_start(
                                    kh_buf[hbb][0:64, :],
                                    khead_d[hh * 64:(hh + 1) * 64, :]),
                                    sem=ks, n=16, waits=lw)
                                E.do("gp", lambda g, hh=hh, hbb=hbb: g.dma_start(
                                    qh_buf[hbb][:, :],
                                    qhat_d[hh * DHA:(hh + 1) * DHA, :]),
                                    sem=ks, n=16)
                                kq_val[hh] = (ks, E.cnt[ks])
                            if h == 0:
                                kq_load(E, 0)
                                kq_load(E, 1)
                            kq_ok = kq_val[h]
                            cast_q = []
                            wts = [("DV", E.cnt["DV"])]
                            if vab_ready[vb] is not None:
                                wts.append(vab_ready[vb])
                            E.do("dv", lambda v, h=h, vb=vb: v.tensor_copy(
                                v_aug[vb][:, :].rearrange("p (k a) -> p k a", a=DHA)
                                [:, :, 0:64],
                                v_tok[:, :].rearrange("p (k a) -> p k a", a=DL)
                                [:, :, h * 64:(h + 1) * 64]),
                                sem="DV", waits=wts)
                            va_ok = ("DV", E.cnt["DV"])
                            for s_ in range(NS):
                                ot = ot_ps[s_ % 2]
                                npair = (s_ + 1) * 2
                                LAG = 2
                                exp_rec = {}
                                mask_rec = {}

                                def emit_av(kp, E=None):
                                    pb = kp % 4
                                    av_w = [exp_rec[kp]]
                                    if mask_rec.get(kp) is not None:
                                        av_w.append(mask_rec[kp])
                                    if kp == 0 and ot_evac[s_ % 2] is not None:
                                        av_w.append(ot_evac[s_ % 2])
                                    for half in range(2):
                                        kb = 2 * kp + half
                                        E.do("pe", lambda t, kb=kb, pb=pb, half=half,
                                             kp=kp:
                                             t.matmul(
                                                 ot[:, :],
                                                 v_aug[vb][:, kb * DHA:(kb + 1) * DHA],
                                                 pt[pb][:, half * 512:(half + 1) * 512],
                                                 start=(kp == 0 and half == 0),
                                                 stop=(kp == npair - 1 and half == 1)),
                                             sem="PE", waits=av_w if half == 0 else ())
                                    av_of_pt[pb] = ("PE", E.cnt["PE"])

                                for kp in range(npair):
                                    pb = kp % 4
                                    spb = kp % 3
                                    sp = s_ps[spb]
                                    mm_w = [kq_ok, va_ok]
                                    if exp_of_sp[spb] is not None:
                                        mm_w.append(exp_of_sp[spb])
                                    for half in range(2):
                                        kb = 2 * kp + half
                                        E.do("pe", lambda t, sp=sp, hb=hb, kb=kb,
                                             s_=s_, half=half:
                                             t.matmul(sp[:, half * 512:(half + 1) * 512],
                                                      kh_buf[hb][:, kb * 128:(kb + 1) * 128],
                                                      qh_buf[hb][:, s_ * 512:(s_ + 1) * 512],
                                                      start=True, stop=True),
                                             sem="PE", waits=mm_w if half == 0 else ())
                                    smm = ("PE", E.cnt["PE"])
                                    ew = [smm]
                                    if av_of_pt[pb] is not None:
                                        ew.append(av_of_pt[pb])
                                    E.do("ac", lambda s, pb=pb, spb=spb: s.activation(
                                        pt[pb][:, :], s_ps[spb][:, :], AF.Exp),
                                        sem="AC", waits=ew)
                                    exp_of_sp[spb] = ("AC", E.cnt["AC"])
                                    exp_rec[kp] = ("AC", E.cnt["AC"])
                                    md = None
                                    for half in range(2):
                                        kb = 2 * kp + half
                                        m = kb - 4 * s_
                                        if 0 <= m < 4:
                                            E.do("dv", lambda v, pb=pb, half=half, m=m:
                                                 v.tensor_mul(
                                                     pt[pb][:, half * 512:(half + 1) * 512],
                                                     pt[pb][:, half * 512:(half + 1) * 512],
                                                     wmask[:, 384 - 128 * m:
                                                           896 - 128 * m]),
                                                 sem="DV", waits=[exp_rec[kp]])
                                            md = ("DV", E.cnt["DV"])
                                    mask_rec[kp] = md
                                    if kp >= LAG:
                                        emit_av(kp - LAG, E)
                                for kp in range(max(0, npair - LAG), npair):
                                    emit_av(kp, E)
                                otmm = ("PE", E.cnt["PE"])
                                lsm = "LS0" if s_ % 2 == 0 else "LS1"
                                E.do("dv", lambda v, ot=ot, s_=s_: v.tensor_copy(
                                    l_st[s_ % 2][:, :], ot[64:65, :]),
                                    sem="DV", waits=[otmm, (lsm, E.cnt[lsm])])
                                E.do("gp", lambda g, s_=s_: g.dma_start(
                                    l_d[0:1, s_ * 512:(s_ + 1) * 512],
                                    l_st[s_ % 2][:, :]),
                                    sem=lsm, n=16,
                                    waits=[("DV", E.cnt["DV"]), lrl_hs])
                                E.do("dv", lambda v, ot=ot, s_=s_: v.tensor_copy(
                                    raw_h[:, s_ * 512:(s_ + 1) * 512], ot[0:64, :]),
                                    sem="DV", waits=["DV"])
                                ot_evac[s_ % 2] = ("DV", E.cnt["DV"])
                                for _ in range(3):
                                    if cast_q:
                                        dst_, srt_, ei_ = cast_q.pop(0)
                                        E.do("gp", lambda g, dst_=dst_, srt_=srt_:
                                             g.dma_start(dst_, srt_),
                                             sem=f"WC{ei_}", n=16)
                            E.do("gp", lambda g: g.dma_start(
                                ltile[:, :],
                                l_d.rearrange("o (a b) -> (o a) b", b=c.SW)),
                                sem="LRL", n=16,
                                waits=["LRL", "LS0", "LS1", ("DV", E.cnt["DV"])])
                            E.do("dv", lambda v: v.reciprocal(
                                ltile[:, :], ltile[:, :]), sem="DV", waits=["LRL"])
                            E.do("gp", lambda g: g.dma_start(
                                rl_d.rearrange("o (a b) -> (o a) b", b=c.SW),
                                ltile[:, :]), sem="LRL", n=16,
                                waits=[("DV", E.cnt["DV"])] + rl_hs)
                            rl_done = ("LRL", E.cnt["LRL"])
                            while cast_q:
                                dst_, srt_, ei_ = cast_q.pop(0)
                                E.do("gp", lambda g, dst_=dst_, srt_=srt_:
                                     g.dma_start(dst_, srt_),
                                     sem=f"WC{ei_}", n=16)
                            for s_ in range(NS):
                                rb = s_ % 2
                                rsm = "RL0" if rb == 0 else "RL1"
                                E.do("gp", lambda g, s_=s_, rb=rb: g.dma_start(
                                    rlb[rb][:, :],
                                    bass.AP(rl_d.tensor, s_ * 512,
                                            [[0, 64], [1, 512]])),
                                    sem=rsm, n=16,
                                    waits=[rl_done, ("DV", E.cnt["DV"])])
                                rlv = (rsm, E.cnt[rsm])
                                ab = att_st[s_ % 2]
                                asm_ = "AT0" if s_ % 2 == 0 else "AT1"
                                E.do("dv", lambda v, s_=s_, ab=ab, rb=rb:
                                     v.tensor_mul(
                                         ab[:, :],
                                         raw_h[:, s_ * 512:(s_ + 1) * 512],
                                         rlb[rb][:, :]),
                                    sem="DV",
                                    waits=["DV", rlv, (asm_, E.cnt[asm_])])
                                E.do("gp", lambda g, h=h, s_=s_, ab=ab: g.dma_start(
                                    attnT_d[h * 64:(h + 1) * 64,
                                            s_ * 512:(s_ + 1) * 512], ab[:, :]),
                                    sem=asm_, n=16, waits=[("DV", E.cnt["DV"])])
                            vab_ready[vb] = ("PE", E.cnt["PE"])
                            attn_of_hb[hb] = ("PE", E.cnt["PE"])
                            if h + 2 < HPC:
                                kq_load(E, h + 2)
                        if "d_attnT" in dbg_t:
                            E.do("gp", lambda g: g.dma_start(
                                dbg_t["d_attnT"], attnT_d), sem="STO", n=16,
                                waits=["AT0", "AT1"])
                        E.do("gp", lambda g: g.collective_compute(
                            "AllGather", ALU.bypass,
                            replica_groups=[[0, 1, 2, 3], [4, 5, 6, 7]],
                            ins=[attnT_d.opt()], outs=[attnT_full.opt()]),
                            sem="CC1", n=1, waits=["AT0", "AT1"])
                    em.phase(b2, p2)

        stB = ExitStack()
        eB = stB.enter_context
        resid = [eB(nc.sbuf_tensor(f"resid{i}", [128, D], F32))
                 for i in range(c.TQS)]
        sq_sc = eB(nc.sbuf_tensor("sq_sc", [128, D], F32))
        scatT = eB(nc.sbuf_tensor("scatT", [128, c.E * c.TQS], I32))
        gathT = eB(nc.sbuf_tensor("gathT", [128, c.E * c.TQS], I32))
        stC = ExitStack()
        sT_loc = stC.enter_context(nc.sbuf_tensor("sT_loc", [c.E, TQ], F32))

        st3 = ExitStack()
        with st3:
            e3 = st3.enter_context
            attnQ = [e3(nc.sbuf_tensor(f"attnQ{i}", [128, TQ], BF16))
                     for i in range(c.DCH)]
            woT = [e3(nc.sbuf_tensor(f"woT{i}", [128, D], BF16))
                   for i in range(c.DCH)]
            wgT = [e3(nc.sbuf_tensor(f"wgT{i}", [128, 16], BF16))
                   for i in range(c.DCH)]
            lt_sb = [e3(nc.sbuf_tensor(f"lt_sb{i}", [128, TQ], BF16))
                     for i in range(c.DCH)]
            ln1b = e3(nc.sbuf_tensor("ln1b", [128, D], BF16))
            xq_t = [e3(nc.sbuf_tensor(f"xq_t{i}", [128, D], F32)) for i in range(2)]
            bo_bc = e3(nc.sbuf_tensor("bo_bc", [128, D], F32))
            p_o = [e3(nc.psum_tensor(f"p_o{i}", [128, 512], F32)) for i in range(2)]
            p_g = e3(nc.psum_tensor("p_g", [16, 512], F32))
            p_lt = e3(nc.psum_tensor("p_lt", [128, 128], BF16))

            with nc.Block() as b3:
                def p3(E):
                    for cc in range(c.DCH):
                        E.do("sy", lambda s, cc=cc: s.dma_start(
                            woT[cc][:, :], woT_bf[cc * 128:(cc + 1) * 128, :]),
                            sem="WOT", n=16, waits=[("COG", 32)])
                        E.do("sy", lambda s, cc=cc: s.dma_start(
                            wgT[cc][:, 0:16], wgT_bf[cc * 128:(cc + 1) * 128, :]),
                            sem="WOT", n=16, waits=[("COG", 32)])
                    xbr_o = ("WOT", E.cnt["WOT"])
                    for half in range(2):
                        E.do("pe", lambda t, half=half: t.matmul(
                            p_o[half][:, :], ones_rf[:, :],
                            bo_row[0:1, half * 512:(half + 1) * 512],
                            start=True, stop=True), sem="PE", waits=[("LDS", 16)])
                        E.do("dv", lambda v, half=half: v.tensor_copy(
                            bo_bc[:, half * 512:(half + 1) * 512], p_o[half][:, :]),
                            sem="DV", waits=["PE"])
                    bo_ok = ("DV", E.cnt["DV"])
                    for cc in range(c.DCH):
                        E.do("gp", lambda g, cc=cc: g.indirect_dma_start(
                            attnQ[cc][:, :], None,
                            attnT_full.rearrange("(o a) b -> o (a b)", o=1),
                            bass.IndirectOffsetOnAxis(idxq_sb[:, cc:cc + 1], 1)),
                            sem="XQG", n=16, waits=[("CC1", 1), ("LDI", 16),
                                                    ("STO", E.cnt["STO"])])
                    xqg = ("XQG", E.cnt["XQG"])

                    evac_po = [bo_ok, bo_ok]
                    for tt in range(c.TQS):
                        xb = tt % 2
                        xsem = "SYB" if xb else "SYA"
                        E.do("sy", lambda s, tt=tt, xb=xb: s.dma_start(
                            xq_t[xb][:, :], x_q[tt * 128:(tt + 1) * 128, :]),
                            sem=xsem, n=16,
                            waits=[("DV", E.cnt["DV"])] if tt >= 2 else [])
                        xqv = (xsem, E.cnt[xsem])
                        for ch in range(2):
                            pp = p_o[ch]
                            for cc in range(c.DCH):
                                E.do("pe", lambda t, pp=pp, cc=cc, tt=tt, ch=ch:
                                     t.matmul(pp[:, :],
                                              attnQ[cc][:, tt * 128:(tt + 1) * 128],
                                              woT[cc][:, ch * 512:(ch + 1) * 512],
                                              start=(cc == 0), stop=(cc == c.DCH - 1)),
                                     sem="PE",
                                     waits=[xqg, xbr_o, evac_po[ch]] if cc == 0 else ())
                            E.do("dv", lambda v, pp=pp, tt=tt, ch=ch, xb=xb:
                                 v.scalar_tensor_tensor(
                                     resid[tt][:, ch * 512:(ch + 1) * 512], pp[:, :],
                                     0.0, xq_t[xb][:, ch * 512:(ch + 1) * 512],
                                     ALU.add, ALU.add), sem="DV",
                                 waits=["PE", xqv])
                            E.do("dv", lambda v, tt=tt, ch=ch: v.tensor_add(
                                resid[tt][:, ch * 512:(ch + 1) * 512],
                                resid[tt][:, ch * 512:(ch + 1) * 512],
                                bo_bc[:, ch * 512:(ch + 1) * 512]), sem="DV",
                                waits=["DV"])
                            evac_po[ch] = ("DV", E.cnt["DV"])
                        ln_emit(E, resid[tt], ln1b)
                        E.do("gp", lambda g, tt=tt: g.dma_start(
                            ln1_aug[tt * 128:(tt + 1) * 128, 0:D], ln1b[:, :]),
                            sem="LN1S", n=16, waits=[("DV", E.cnt["DV"])])
                        for cc in range(c.DCH):
                            E.do("pe", lambda t, cc=cc: t.transpose(
                                p_lt[:, :], ln1b[:, cc * 128:(cc + 1) * 128],
                                ident_b[:, :]),
                                sem="PE", waits=[("DV", E.cnt["DV"]),
                                                 ("DV", E.cnt["DV"])])
                            E.do("dv", lambda v, cc=cc, tt=tt: v.tensor_copy(
                                lt_sb[cc][:, tt * 128:(tt + 1) * 128], p_lt[:, :]),
                                sem="DV", waits=["PE"])
                        E.do("dv", lambda v: v.engine_nop(), sem="DV",
                             waits=[("LN1S", E.cnt["LN1S"])])
                        if "d_ln1" in dbg_t:
                            E.do("gp", lambda g, tt=tt: g.dma_start(
                                dbg_t["d_ln1"][tt * 128:(tt + 1) * 128, :],
                                resid[tt][:, :]), sem="STO", n=16,
                                waits=[("DV", E.cnt["DV"])])
                    ln1_st = ("LN1S", E.cnt["LN1S"])

                    lt_ok = ("DV", E.cnt["DV"])
                    for gs in range(c.GST):
                        for cc in range(c.DCH):
                            E.do("pe", lambda t, gs=gs, cc=cc: t.matmul(
                                p_g[:, :], wgT[cc][:, 0:16],
                                lt_sb[cc][:, gs * 512:(gs + 1) * 512],
                                start=(cc == 0), stop=(cc == c.DCH - 1)),
                                sem="PE",
                                waits=[lt_ok, ("DV", E.cnt["DV"])] if cc == 0 else ())
                        E.do("dv", lambda v, gs=gs: v.tensor_copy(
                            sT_loc[:, gs * 512:(gs + 1) * 512], p_g[0:8, :]),
                            sem="DV", waits=["PE"])
                    E.do("gp", lambda g: g.dma_start(sT_dr, sT_loc[:, :]),
                         sem="STD", n=16, waits=[("DV", E.cnt["DV"])])
                    E.do("gp", lambda g: g.collective_compute(
                        "AllGather", ALU.bypass,
                        replica_groups=[[0, 1, 2, 3, 4, 5, 6, 7]],
                        ins=[sT_dr.opt()], outs=[s_full.opt()]),
                        sem="CC2", n=1, waits=["STD"])
                    if "d_sT" in dbg_t:
                        E.do("gp", lambda g: g.dma_start(dbg_t["d_sT"], sT_loc[:, :]),
                             sem="STO", n=16, waits=[("DV", E.cnt["DV"])])
                em.phase(b3, p3)

        NW = c.CORES * TQ // 16
        st4 = ExitStack()
        with st4:
            e4 = st4.enter_context
            S_all = e4(nc.sbuf_tensor("S_all", [128, NW], F32))
            scr = e4(nc.sbuf_tensor("scr", [128, NW], F32))
            scr2 = e4(nc.sbuf_tensor("scr2", [128, NW], F32))
            sv8 = e4(nc.sbuf_tensor("sv8", [8, 16], F32))
            prow = e4(nc.sbuf_tensor("prow", [1, 16], F32))
            colp = e4(nc.sbuf_tensor("colp", [128, 4], F32))
            r1x = e4(nc.sbuf_tensor("r1x", [1, 128], F32))
            mk = e4(nc.sbuf_tensor("mk", [c.E, TQ], F32))
            wl = e4(nc.sbuf_tensor("wl", [c.E, TQ], F32))
            ic = e4(nc.sbuf_tensor("ic", [c.E, TQ], F32))
            on8 = e4(nc.sbuf_tensor("on8", [c.E, TQ], F32))
            ok8 = e4(nc.sbuf_tensor("ok8", [c.E, TQ], F32))
            sf8 = e4(nc.sbuf_tensor("sf8", [c.E, TQ], F32))
            gf8 = e4(nc.sbuf_tensor("gf8", [c.E, TQ], F32))
            trb = e4(nc.sbuf_tensor("trb", [128, 8], BF16))
            tsramp = e4(nc.sbuf_tensor("tsramp", [c.E, TQ], F32))
            tsneg = e4(nc.sbuf_tensor("tsneg", [c.E, TQ], F32))
            p_s1 = e4(nc.psum_tensor("p_s1", [128, 8], F32))
            p_s2 = e4(nc.psum_tensor("p_s2", [8, 8], F32))
            p_tr = e4(nc.psum_tensor("p_tr", [1, 128], F32))

            with nc.Block() as b4:
                def p4(E):
                    for hh in range(2):
                        E.do("gp", lambda g, hh=hh: g.dma_start(
                            s_re.rearrange("(e r h) q -> e r h q", e=c.E,
                                           r=c.CORES, h=2)[:, :, hh, :],
                            s_full.rearrange("(r e) (h q) -> r e h q", r=c.CORES,
                                             e=c.E, h=2).transpose([1, 0, 2, 3])
                            [:, :, hh, :]),
                            sem="SRE", n=16, waits=[("CC2", 1)])
                    sal = E.do("gp", lambda g: g.dma_start(
                        S_all[:, :], s_re), sem="SRE", n=16, waits=["SRE"])

                    E.do("dv", lambda v: v.memset(sv8[:, :], 0.0), sem="DV")
                    E.do("dv", lambda v: v.memset(prow[:, :], 0.0), sem="DV")
                    E.do("dv", lambda v: v.memset(colp[:, :], 0.0), sem="DV")
                    E.do("dv", lambda v: v.memset(r1x[:, :], 0.0), sem="DV")

                    def expert_reduce(src_col, dst_slice, op):
                        E.do("pe", lambda t, src_col=src_col: t.transpose(
                            p_tr[:, :], src_col, ident_f[:, :]),
                            sem="PE", waits=[("DV", E.cnt["DV"])])
                        E.do("dv", lambda v: v.tensor_copy(r1x[:, :], p_tr[:, :]),
                             sem="DV", waits=["PE"])
                        E.do("dv", lambda v, op=op: v.tensor_reduce(
                            prow[0:1, 0:8],
                            r1x[0:1, :].rearrange("o (e s) -> o e s", s=16),
                            AX.X, op), sem="DV", waits=["DV"])
                        E.do("pe", lambda t: t.transpose(
                            p_s2[:, 0:1], prow[0:1, 0:8], ident_f[0:1, 0:1]),
                            sem="PE", waits=[("DV", E.cnt["DV"])])
                        E.do("dv", lambda v, dst_slice=dst_slice: v.tensor_copy(
                            dst_slice, p_s2[:, 0:1]), sem="DV", waits=["PE"])

                    E.do("dv", lambda v: v.reduce_max(
                        colp[:, 0:1], S_all[:, :], AX.X), sem="DV",
                        waits=[sal, "DV"])
                    expert_reduce(colp[:, 0:1], sv8[:, 1:2], ALU.max)
                    E.do("dv", lambda v: v.tensor_reduce(
                        colp[:, 0:1], S_all[:, :], AX.X, ALU.min), sem="DV",
                        waits=[sal, ("PE", E.cnt["PE"])])
                    expert_reduce(colp[:, 0:1], sv8[:, 0:1], ALU.min)
                    E.do("dv", lambda v: v.tensor_copy(sv8[:, 4:5], sv8[:, 1:2]),
                         sem="DV", waits=["DV"])
                    E.do("dv", lambda v: v.tensor_scalar_mul(
                        sv8[:, 5:6], sv8[:, 4:5], -1.0), sem="DV", waits=["DV"])

                    for it in range(c.NIT):
                        E.do("dv", lambda v: v.tensor_add(
                            sv8[:, 2:3], sv8[:, 0:1], sv8[:, 1:2]), sem="DV",
                            waits=["DV", ("PE", E.cnt["PE"])])
                        E.do("dv", lambda v: v.tensor_scalar_mul(
                            sv8[:, 2:3], sv8[:, 2:3], 0.5), sem="DV", waits=["DV"])
                        E.do("pe", lambda t: t.matmul(
                            p_s1[:, 0:1], g8t[:, :], sv8[:, 2:3],
                            start=True, stop=True), sem="PE",
                            waits=[("DV", E.cnt["DV"])])
                        E.do("dv", lambda v: v.tensor_copy(
                            colp[:, 2:3], p_s1[:, 0:1]), sem="DV", waits=["PE"])
                        E.do("dv", lambda v: v.tensor_scalar(
                            scr[:, :], S_all[:, :], colp[:, 2:3], 0.0,
                            ALU.is_gt, ALU.add, accum_out=colp[:, 3:4]),
                            sem="DV", waits=["DV"])
                        E.do("pe", lambda t: t.matmul(
                            p_s2[:, 0:1], g8[:, :], colp[:, 3:4],
                            start=True, stop=True), sem="PE",
                            waits=[("DV", E.cnt["DV"])])
                        E.do("dv", lambda v: v.tensor_copy(
                            sv8[:, 8:9], p_s2[:, 0:1]), sem="DV", waits=["PE"])
                        E.do("dv", lambda v: v.tensor_single_scalar(
                            sv8[:, 3:4], sv8[:, 8:9], c.CAPT - 0.5, ALU.is_gt),
                            sem="DV", waits=["DV"])
                        E.do("dv", lambda v: v.tensor_sub(
                            sv8[:, 9:10], sv8[:, 2:3], sv8[:, 0:1]),
                            sem="DV", waits=["DV"])
                        E.do("dv", lambda v: v.tensor_sub(
                            sv8[:, 10:11], sv8[:, 1:2], sv8[:, 2:3]),
                            sem="DV", waits=["DV"])
                        E.do("dv", lambda v: v.tensor_mul(
                            sv8[:, 9:10], sv8[:, 9:10], sv8[:, 3:4]),
                            sem="DV", waits=["DV"])
                        E.do("dv", lambda v: v.tensor_mul(
                            sv8[:, 10:11], sv8[:, 10:11], sv8[:, 3:4]),
                            sem="DV", waits=["DV"])
                        E.do("dv", lambda v: v.tensor_add(
                            sv8[:, 0:1], sv8[:, 0:1], sv8[:, 9:10]),
                            sem="DV", waits=["DV"])
                        E.do("dv", lambda v: v.tensor_add(
                            sv8[:, 1:2], sv8[:, 2:3], sv8[:, 10:11]),
                            sem="DV", waits=["DV"])
                    E.do("dv", lambda v: v.tensor_add(
                        sv8[:, 2:3], sv8[:, 0:1], sv8[:, 1:2]), sem="DV",
                        waits=["DV", ("PE", E.cnt["PE"])])
                    E.do("dv", lambda v: v.tensor_scalar_mul(
                        sv8[:, 2:3], sv8[:, 2:3], 0.5), sem="DV", waits=["DV"])
                    E.do("pe", lambda t: t.matmul(
                        p_s1[:, 0:1], g8t[:, :], sv8[:, 2:3], start=True, stop=True),
                        sem="PE", waits=[("DV", E.cnt["DV"])])
                    E.do("pe", lambda t: t.matmul(
                        p_s1[:, 1:2], g8t[:, :], sv8[:, 5:6], start=True, stop=True),
                        sem="PE", waits=[("DV", E.cnt["DV"])])
                    E.do("dv", lambda v: v.tensor_copy(colp[:, 2:4], p_s1[:, 0:2]),
                         sem="DV", waits=["PE"])
                    E.do("dv", lambda v: v.tensor_scalar(
                        scr[:, :], S_all[:, :], colp[:, 2:3], None, ALU.is_gt),
                        sem="DV", waits=["DV"])
                    E.do("ac", lambda s: s.activation(
                        scr2[:, :], S_all[:, :], AF.Exp, bias=colp[:, 3:4]),
                        sem="AC", waits=[("DV", E.cnt["DV"])])
                    E.do("dv", lambda v: v.scalar_tensor_tensor(
                        scr2[:, :], scr2[:, :], 1.0, scr[:, :],
                        ALU.mult, ALU.mult, accum_out=colp[:, 0:1]),
                        sem="DV", waits=[("AC", E.cnt["AC"]), "DV"])
                    E.do("pe", lambda t: t.matmul(
                        p_s2[:, 0:1], g8[:, :], colp[:, 0:1], start=True, stop=True),
                        sem="PE", waits=[("DV", E.cnt["DV"])])
                    E.do("dv", lambda v: v.tensor_copy(sv8[:, 6:7], p_s2[:, 0:1]),
                         sem="DV", waits=["PE"])
                    E.do("dv", lambda v: v.reciprocal(sv8[:, 7:8], sv8[:, 6:7]),
                         sem="DV", waits=["DV"])
                    if "d_prm" in dbg_t:
                        E.do("gp", lambda g: g.dma_start(dbg_t["d_prm"], sv8[:, :]),
                             sem="STO", n=16, waits=[("DV", E.cnt["DV"])])

                    E.do("dv", lambda v: v.tensor_scalar(
                        mk[:, :], sT_loc[:, :], sv8[:, 2:3], None, ALU.is_gt),
                        sem="DV", waits=["DV"])
                    E.do("ac", lambda s: s.activation(
                        wl[:, :], sT_loc[:, :], AF.Exp, bias=sv8[:, 5:6]),
                        sem="AC", waits=[("DV", E.cnt["DV"])])
                    E.do("dv", lambda v: v.scalar_tensor_tensor(
                        wl[:, :], wl[:, :], sv8[:, 7:8], mk[:, :],
                        ALU.mult, ALU.mult), sem="DV",
                        waits=[("AC", E.cnt["AC"]), "DV"])
                    E.do("dv", lambda v: v.memset(on8[:, :], 1.0), sem="DV")
                    E.do("dv", lambda v: v.tensor_tensor_scan(
                        ic[:, :], mk[:, :], on8[:, :], 0.0, ALU.add, ALU.mult),
                        sem="DV", waits=["DV"])
                    E.do("dv", lambda v: v.tensor_single_scalar(
                        ok8[:, :], ic[:, :], c.CSLOT + 0.5, ALU.is_lt),
                        sem="DV", waits=["DV"])
                    E.do("dv", lambda v: v.tensor_mul(
                        ok8[:, :], ok8[:, :], mk[:, :]), sem="DV", waits=["DV"])
                    E.do("dv", lambda v: v.scalar_tensor_tensor(
                        sf8[:, :], ic[:, :], -386.0, ok8[:, :], ALU.add, ALU.mult),
                        sem="DV", waits=["DV"])
                    E.do("dv", lambda v: v.tensor_scalar_add(
                        sf8[:, :], sf8[:, :], 385.0), sem="DV", waits=["DV"])
                    E.do("gp", lambda g: g.iota(
                        tsramp[:, :], [[1, c.TQS], [0, 128]],
                        channel_multiplier=0,
                        allow_small_or_imprecise_dtypes=True), sem="GP")
                    E.do("dv", lambda v: v.scalar_tensor_tensor(
                        tsneg[:, :], ok8[:, :], -1.0, tsramp[:, :],
                        ALU.mult, ALU.mult), sem="DV",
                        waits=["DV", ("GP", E.cnt["GP"])])
                    E.do("dv", lambda v: v.tensor_add(
                        sf8[:, :], sf8[:, :], tsramp[:, :]), sem="DV",
                        waits=["DV"])
                    E.do("dv", lambda v: v.tensor_add(
                        sf8[:, :], sf8[:, :], tsneg[:, :]), sem="DV",
                        waits=["DV"])
                    E.do("dv", lambda v: v.scalar_tensor_tensor(
                        gf8[:, :], ic[:, :], -385.0, ok8[:, :], ALU.add, ALU.mult),
                        sem="DV", waits=["DV"])
                    E.do("dv", lambda v: v.tensor_scalar_add(
                        gf8[:, :], gf8[:, :], 384.0), sem="DV", waits=["DV"])
                    slots_ok = ("DV", E.cnt["DV"])
                    if "d_wl" in dbg_t:
                        E.do("gp", lambda g: g.dma_start(dbg_t["d_wl"], wl[:, :]),
                             sem="STO", n=16, waits=[slots_ok])
                    if "d_slot" in dbg_t:
                        E.do("gp", lambda g: g.dma_start(dbg_t["d_slot"], sf8[:, :]),
                             sem="STO", n=16, waits=[slots_ok])

                    for ts_ in range(c.TQS):
                        E.do("gp", lambda g, ts_=ts_: g.dma_start(
                            ln1_aug[ts_ * 128:(ts_ + 1) * 128, D + 16:c.AUGW],
                            zrow_b[:, 0:c.AUGW - D - 16]), sem="AUGC", n=16)
                    for ts_ in range(c.TQS):
                        jobs = [(sf8, "scat"), (gf8, "gath"), (sT_loc, "s"),
                                (wl, "w")]
                        for src, kindt in jobs:
                            E.do("pe", lambda t, src=src, ts_=ts_: t.transpose(
                                p_s1[:, :], src[:, ts_ * 128:(ts_ + 1) * 128],
                                ident_f[0:8, 0:8]),
                                sem="PE",
                                waits=[slots_ok, ("DV", E.cnt["DV"]),
                                       ("GP", E.cnt["GP"])])
                            if kindt == "scat":
                                E.do("dv", lambda v, ts_=ts_: v.tensor_copy(
                                    scatT[:, ts_ * 8:(ts_ + 1) * 8], p_s1[:, :]),
                                    sem="DV", waits=["PE"])
                            elif kindt == "gath":
                                E.do("dv", lambda v, ts_=ts_: v.tensor_copy(
                                    gathT[:, ts_ * 8:(ts_ + 1) * 8], p_s1[:, :]),
                                    sem="DV", waits=["PE"])
                            else:
                                colbase = 1024 if kindt == "s" else 1032
                                E.do("dv", lambda v: v.tensor_copy(
                                    trb[:, :], p_s1[:, :]), sem="DV", waits=["PE"])
                                E.do("gp", lambda g, ts_=ts_, colbase=colbase:
                                     g.dma_start(
                                         ln1_aug[ts_ * 128:(ts_ + 1) * 128,
                                                 colbase:colbase + 8], trb[:, :]),
                                     sem="AUGC", n=16, waits=[("DV", E.cnt["DV"])])
                                E.do("dv", lambda v: v.engine_nop(), sem="DV",
                                     waits=[("AUGC", E.cnt["AUGC"])])
                    trans_dv = ("DV", E.cnt["DV"])
                    aug_done = ("AUGC", E.cnt["AUGC"])

                    zero_done = ("XGZ", E.cnt["XGZ"])
                    gp_rdy = ("GP", E.cnt["GP"])

                    for ts_ in range(c.TQS):
                        w = [trans_dv, zero_done, gp_rdy]
                        for ei in range(c.E):
                            E.do("gp", lambda g, ei=ei, ts_=ts_:
                                 g.indirect_dma_start(
                                     tokslot_d,
                                     bass.IndirectOffsetOnAxis(
                                         scatT[:, ts_ * 8 + ei:ts_ * 8 + ei + 1], 0),
                                     tokrow[:, ts_:ts_ + 1], None,
                                     element_offset=ei * c.XRT),
                                 sem=f"SC{ei}", n=16, waits=w)
                em.phase(b4, p4)
        SC_ALL = [(f"SC{i}", em.cnt[f"SC{i}"]) for i in range(c.E)]
        AUG_ALL = [("AUGC", em.cnt["AUGC"]), ("LN1S", em.cnt["LN1S"])]

        stC.close()
        st5 = ExitStack()
        with st5:
            e5 = st5.enter_context
            xgt = [e5(nc.sbuf_tensor(f"xgt{i}", [128, c.CSLOT], BF16))
                   for i in range(c.DCH)]
            xg_sb = [e5(nc.sbuf_tensor(f"xg_sb{i}", [128, c.AUGW], BF16))
                     for i in range(3)]
            idxg = e5(nc.sbuf_tensor("idxg", [128, 3], I32))
            w1s = [e5(nc.sbuf_tensor(f"w1s{i}", [128, c.HID], BF16))
                   for i in range(c.DCH)]
            w3s = [e5(nc.sbuf_tensor(f"w3s{i}", [128, c.HID], BF16))
                   for i in range(c.DCH)]
            w2s = [e5(nc.sbuf_tensor(f"w2s{i}", [128, D], BF16))
                   for i in range(3)]
            hts = [e5(nc.sbuf_tensor(f"hts{i}", [128, c.CSLOT], BF16))
                   for i in range(c.HT)]
            h1f = e5(nc.sbuf_tensor("h1f", [128, c.CSLOT], F32))
            h3f = e5(nc.sbuf_tensor("h3f", [128, c.CSLOT], F32))
            y_sb = [e5(nc.sbuf_tensor(f"y_sb{i}", [128, D], BF16)) for i in range(3)]
            wcol = e5(nc.sbuf_tensor("wcol", [128, 3], F32))
            p_h = [e5(nc.psum_tensor(f"p_h{i}", [128, c.CSLOT], F32))
                   for i in range(2)]
            p_y = [e5(nc.psum_tensor(f"p_y{i}", [128, 512], F32)) for i in range(6)]

            for _s in ("W1L", "W3L", "SYC", "XGG"):
                em.sem(_s)
            with nc.Block() as b5:
                def p5(E):
                    prev_gath = None
                    yprev = None
                    xgl_val = {}
                    w1_val = {}
                    w3_val = {}
                    w2_use = [0, 0, 0]
                    W2S = ("SYA", "SYB", "SYC")

                    def xg_loads(E, ee):
                        prevg = ("XGG", E.cnt["XGG"])
                        for sl in range(3):
                            E.do("sy", lambda s, ee=ee, sl=sl: s.dma_start(
                                idxg[:, sl:sl + 1],
                                tokslot_d[ee * c.XRT + sl * 128:
                                          ee * c.XRT + (sl + 1) * 128, :]),
                                sem="XGL", n=16,
                                waits=[(f"SC{ee}", 16 * c.TQS), prevg])
                        idv = ("XGL", E.cnt["XGL"])
                        for sl in range(3):
                            E.do("gp", lambda g, sl=sl: g.indirect_dma_start(
                                xg_sb[sl][:, :], None,
                                ln1_aug,
                                bass.IndirectOffsetOnAxis(idxg[:, sl:sl + 1], 0),
                                element_offset=0),
                                sem="XGG", n=16,
                                waits=[idv, ("PE", E.cnt["PE"]),
                                       ("STO", E.cnt["STO"])] + AUG_ALL)
                        xgl_val[ee] = ("XGG", E.cnt["XGG"])

                    def w1_load(E, ee, cs, pe_guard):
                        E.do("gp", lambda g, ee=ee, cs=cs: g.dma_start(
                            w1s[cs][:, :],
                            w1T_i[ee * D + cs * 128:ee * D + (cs + 1) * 128, :]),
                            sem="W1L", n=16,
                            waits=[("PE", pe_guard),
                                   ("STO", E.cnt["STO"])] + SC_ALL)
                        w1_val[ee] = ("W1L", E.cnt["W1L"])

                    def w3_load(E, ee, cs, pe_guard):
                        E.do("gp", lambda g, ee=ee, cs=cs: g.dma_start(
                            w3s[cs][:, :],
                            w3T_i[ee * D + cs * 128:ee * D + (cs + 1) * 128, :]),
                            sem="W3L", n=16,
                            waits=[("PE", pe_guard),
                                   ("STO", E.cnt["STO"])] + SC_ALL)
                        w3_val[ee] = ("W3L", E.cnt["W3L"])

                    for ei in range(c.E):
                        if ei == 0:
                            xg_loads(E, 0)
                            for cs in range(c.DCH):
                                w1_load(E, 0, cs, E.cnt["PE"])
                                w3_load(E, 0, cs, E.cnt["PE"])
                        xgl_ok = xgl_val[ei]
                        for sl in range(3):
                            for cs in range(c.DCH):
                                E.do("pe", lambda t, sl=sl, cs=cs: t.matmul(
                                    p_y[0][:, 0:128],
                                    xg_sb[sl][:, cs * 128:(cs + 1) * 128],
                                    ident_b[:, :], start=True, stop=True),
                                    sem="PE", waits=[xgl_ok, ("DV", E.cnt["DV"])])
                                E.do("dv", lambda v, sl=sl, cs=cs: v.tensor_copy(
                                    xgt[cs][:, sl * 128:(sl + 1) * 128],
                                    p_y[0][:, 0:128]),
                                    sem="DV", waits=["PE"])
                        xg_ok = ("DV", E.cnt["DV"])
                        wld1, wld3 = w1_val[ei], w3_val[ei]
                        for sl in range(3):
                            E.do("dv", lambda v, sl=sl, ei=ei: v.tensor_copy(
                                wcol[:, sl:sl + 1],
                                xg_sb[sl][:, 1032 + ei:1033 + ei]),
                                sem="DV", waits=[xgl_ok, "DV"])
                        wcol_ok = ("DV", E.cnt["DV"])

                        h_evac = [None, None]
                        for ht in range(c.HT):
                            pb = ht % 2
                            for wsl, hf, wldx in [(w1s, h1f, wld1),
                                                  (w3s, h3f, wld3)]:
                                mw = [xg_ok, wldx]
                                if h_evac[pb] is not None:
                                    mw.append(h_evac[pb])
                                for cs in range(c.DCH):
                                    E.do("pe", lambda t, pb=pb, wsl=wsl, cs=cs, ht=ht:
                                         t.matmul(p_h[pb][:, :],
                                                  wsl[cs][:, ht * 128:(ht + 1) * 128],
                                                  xgt[cs][:, :],
                                                  start=(cs == 0),
                                                  stop=(cs == c.DCH - 1)),
                                         sem="PE", waits=mw if cs == 0 else ())
                                E.do("dv", lambda v, pb=pb, hf=hf: v.tensor_copy(
                                    hf[:, :], p_h[pb][:, :]), sem="DV",
                                    waits=["PE", ("AC", E.cnt["AC"]), "DV"])
                                h_evac[pb] = ("DV", E.cnt["DV"])
                            E.do("ac", lambda s: s.activation(
                                h1f[:, :], h1f[:, :], AF.Silu), sem="AC",
                                waits=[("DV", E.cnt["DV"])])
                            E.do("dv", lambda v, ht=ht: v.tensor_mul(
                                hts[ht][:, :], h1f[:, :], h3f[:, :]),
                                sem="DV", waits=[("AC", E.cnt["AC"]), "DV"])
                            h_evac[pb] = ("DV", E.cnt["DV"])
                        ht_ok = ("DV", E.cnt["DV"])
                        pe_after_w13 = E.cnt["PE"]
                        for hc in range(c.HT):
                            wb = hc % 3
                            wsm = W2S[wb]
                            E.do("gp", lambda g, ei=ei, hc=hc, wb=wb: g.dma_start(
                                w2s[wb][:, :],
                                w2T_i[ei * c.HID + hc * 128:
                                      ei * c.HID + (hc + 1) * 128, :]),
                                sem=wsm, n=16,
                                waits=[("PE", w2_use[wb])])
                            wldv = (wsm, E.cnt[wsm])
                            if ei + 1 < c.E:
                                if hc < c.DCH:
                                    w1_load(E, ei + 1, hc, pe_after_w13)
                                if hc % 2 == 0:
                                    w3_load(E, ei + 1, hc // 2, pe_after_w13)
                            for ch in range(2):
                                for sl in range(3):
                                    mw = [ht_ok, wldv] if (ch == 0 and sl == 0) \
                                        else ()
                                    E.do("pe", lambda t, sl=sl, ch=ch, hc=hc, wb=wb:
                                         t.matmul(p_y[ch * 3 + sl][:, :],
                                                  hts[hc][:, sl * 128:(sl + 1) * 128],
                                                  w2s[wb][:, ch * 512:(ch + 1) * 512],
                                                  start=(hc == 0),
                                                  stop=(hc == c.HT - 1)),
                                         sem="PE", waits=mw)
                            w2_use[wb] = E.cnt["PE"]
                        if ei + 1 < c.E:
                            xg_loads(E, ei + 1)
                        for sl in range(3):
                            for ch in range(2):
                                E.do("dv", lambda v, sl=sl, ch=ch:
                                     v.tensor_scalar(
                                         y_sb[sl][:, ch * 512:(ch + 1) * 512],
                                         p_y[ch * 3 + sl][:, :],
                                         wcol[:, sl:sl + 1], None, ALU.mult),
                                     sem="DV",
                                     waits=["PE", wcol_ok] +
                                           ([yprev] if yprev is not None else []))
                            E.do("gp", lambda g, ei=ei, sl=sl: g.dma_start(
                                ysl_d[ei * c.XROW + sl * 128:
                                      ei * c.XROW + (sl + 1) * 128, :],
                                y_sb[sl][:, :]),
                                sem=f"YS{ei}", n=16,
                                waits=[("DV", E.cnt["DV"])])
                        yprev = (f"YS{ei}", E.cnt[f"YS{ei}"])
                        gw = [(f"YS{ei}", 16 * 3)]
                        if prev_gath is not None:
                            gw.append(prev_gath)
                        for ts_ in range(c.TQS):
                            E.do("gp", lambda g, ei=ei, ts_=ts_:
                                 g.indirect_dma_start(
                                     resid[ts_][:, :], None,
                                     ysl_d,
                                     bass.IndirectOffsetOnAxis(
                                         gathT[:, ts_ * 8 + ei:ts_ * 8 + ei + 1], 0),
                                     element_offset=ei * c.XROW * D,
                                     compute_op=ALU.add),
                                 sem="IND", n=16, waits=gw if ts_ == 0 else ())
                        prev_gath = ("IND", E.cnt["IND"])
                em.phase(b5, p5)

            with nc.Block() as b6:
                def p6(E):
                    E.wait("IND") if False else None
                    for tt in range(c.TQS):
                        if tt == 0:
                            E.do("dv", lambda v: v.engine_nop(), sem="DV",
                                 waits=[("IND", E.cnt["IND"])])
                        ln_emit(E, resid[tt], None)
                        E.do("gp", lambda g, tt=tt: g.dma_start(
                            y_out[tt * 128:(tt + 1) * 128, :], resid[tt][:, :]),
                            sem="STO", n=16, waits=[("DV", E.cnt["DV"])])
                    E.do("gp", lambda g: g.engine_nop(), sem=None,
                         waits=[("STO", E.cnt["STO"])])
                em.phase(b6, p6)
        stB.close()

    return nc


def em_snapshot(E, sem):
    return E.cnt[sem]


_CACHE = {}


def make_in_maps(cfg, L0_attn_x, L0_attn_Wq, L0_attn_bq, L0_attn_Wk, L0_attn_bk,
                 L0_attn_Wv, L0_attn_bv, L0_attn_Wo, L0_attn_bo,
                 L0_ffn_Wg, L0_ffn_W1, L0_ffn_W2, L0_ffn_W3):
    x = np.ascontiguousarray(np.asarray(L0_attn_x, dtype=np.float32))
    B, S, D = x.shape
    c = cfg
    f32 = lambda a: np.ascontiguousarray(np.asarray(a, dtype=np.float32))
    Wq, Wk, Wv, Wo = map(f32, (L0_attn_Wq, L0_attn_Wk, L0_attn_Wv, L0_attn_Wo))
    bq, bk, bv, bo = map(f32, (L0_attn_bq, L0_attn_bk, L0_attn_bv, L0_attn_bo))
    Wg, W1, W2, W3 = map(f32, (L0_ffn_Wg, L0_ffn_W1, L0_ffn_W2, L0_ffn_W3))
    Wg16 = np.zeros((16, D), np.float32)
    Wg16[:c.E] = Wg
    WgT = np.ascontiguousarray(Wg16.T)
    WoT = np.ascontiguousarray(Wo.T)
    W1T = np.ascontiguousarray(W1.transpose(0, 2, 1)).reshape(c.E * D, c.HID)
    W3T = np.ascontiguousarray(W3.transpose(0, 2, 1)).reshape(c.E * D, c.HID)
    W2T = np.ascontiguousarray(W2.transpose(0, 2, 1)).reshape(c.E * c.HID, D)
    xT = [np.ascontiguousarray(x[b].T) for b in range(B)]

    in_maps = []
    for core in range(c.CORES):
        b = core // c.GC
        j = core % c.GC
        hsl = slice(j * c.DL, (j + 1) * c.DL)
        idx = (np.arange(D, dtype=np.int64) * S + j * c.TQ).astype(np.int32)
        idx = idx.reshape(c.DCH, 128).T.copy()
        in_maps.append(dict(
            xT_h=xT[b],
            x_q=x[b, j * c.TQ:(j + 1) * c.TQ],
            wqT_i=np.ascontiguousarray(Wq[hsl].T),
            wkT_i=np.ascontiguousarray(Wk[hsl].T),
            wvT_i=np.ascontiguousarray(Wv[hsl].T),
            bq_l=bq[hsl].reshape(2, 128), bk_l=bk[hsl].reshape(2, 128),
            bv_l=bv[hsl].reshape(2, 128), bv_r=bv[hsl].reshape(1, c.DL),
            woT_i=WoT, bo_r=bo.reshape(1, D),
            wgT_i=WgT, w1T_i=W1T, w3T_i=W3T, w2T_i=W2T,
            idxq=idx,
        ))
    return in_maps


def kernel(**inputs):
    x = np.asarray(inputs["L0_attn_x"])
    B, S, D = x.shape
    cfg = Cfg(S=S)
    c = cfg
    key = (S,)
    if key not in _CACHE:
        _CACHE[key] = build(cfg)
    nc = _CACHE[key]
    in_maps = make_in_maps(cfg, **inputs)
    res = run_bass_kernel_spmd(nc, in_maps, core_ids=list(range(c.CORES)))
    out = np.empty((B, S, D), np.float32)
    for core in range(c.CORES):
        b = core // c.GC
        j = core % c.GC
        out[b, j * c.TQ:(j + 1) * c.TQ] = np.asarray(res.results[core]["y_out"])
    return out


# version 15
# speedup vs baseline: 1.2787x; 1.2787x over previous
import numpy as np
import ml_dtypes
from contextlib import ExitStack

import concourse.bass as bass
import concourse.mybir as mybir
from concourse.bass_utils import run_bass_kernel_spmd

F32 = mybir.dt.float32
BF16 = mybir.dt.bfloat16
I32 = mybir.dt.int32
AF = mybir.ActivationFunctionType
ALU = mybir.AluOpType
AX = mybir.AxisListType


class Cfg:
    def __init__(self, S=8192):
        self.B = 2; self.S = S; self.D = 1024; self.H = 16; self.DH = 64
        self.E = 8; self.HID = 2048; self.CORES = 8
        self.GC = self.CORES // self.B
        self.HPC = self.H // self.GC
        self.DL = self.HPC * self.DH
        self.TQ = S // self.GC
        self.SW = 512
        self.NS = S // self.SW
        self.NKB = S // 128
        self.CAPT = self.B * S // self.E
        self.CSLOT = 384
        self.XROW = self.CSLOT + 2
        self.AUGW = self.D + 128
        self.NIT = 28
        self.DCH = self.D // 128
        self.TQS = self.TQ // 128
        self.XRT = self.CSLOT + 1 + self.TQS + 1
        self.GST = self.TQ // 512
        self.HT = self.HID // 128


class Emit:
    ENGS = ("gp", "sy", "pe", "ac", "dv")
    BMAP = {"gp": "gpsimd", "sy": "sync", "pe": "tensor", "ac": "scalar",
            "dv": "vector"}

    def __init__(self, nc):
        self.nc = nc
        self.cnt = {}
        self.sems = {}
        self.cur = None
        self.eng = None
        self._eobj = dict(gp=nc.gpsimd, sy=nc.sync, pe=nc.tensor,
                          ac=nc.scalar, dv=nc.vector)
        self._lastwait = {}

    def sem(self, name):
        if name not in self.sems:
            self.sems[name] = self.nc.alloc_semaphore(name)
            self.cnt[name] = 0
        return self.sems[name]

    def phase(self, block, fn):
        base = dict(self.cnt)
        final = None
        for name in self.ENGS:
            self.cnt = dict(base)
            self.cur = name
            self.eng = self._eobj[name]
            self._lastwait = {}
            getattr(block, self.BMAP[name])(lambda _e: fn(self))
            final = self.cnt
        self.cnt = final

    def wait(self, sname, val=None):
        if val is None:
            val = self.cnt[sname]
        if val <= 0:
            return
        if self._lastwait.get(sname, -1) >= val:
            return
        self._lastwait[sname] = val
        self.eng.wait_ge(self.sem(sname), val)

    SERIAL = False
    NO_XW = False

    def do(self, engname, fn, sem=None, n=1, waits=()):
        if sem is not None:
            self.sem(sem)
        if self.cur == engname:
            if Emit.SERIAL:
                for s_ in ("PE", "DV", "AC", "GP"):
                    if s_ in self.cnt:
                        self.wait(s_, self.cnt[s_])
            for w in waits:
                if isinstance(w, tuple):
                    self.wait(w[0], w[1])
                else:
                    self.wait(w)
            inst = fn(self.eng)
            if sem is not None and inst is not None:
                inst.then_inc(self.sems[sem], n)
        if sem is not None:
            self.cnt[sem] += n
        return (sem, self.cnt[sem]) if sem is not None else None


def build(cfg: Cfg, dbg=()):
    nc = bass.Bass("TRN2", target_bir_lowering=False)
    c = cfg
    S, D, TQ, DL, HPC, DH = c.S, c.D, c.TQ, c.DL, c.HPC, c.DH
    DHA = DH + 1
    NKB, NS = c.NKB, c.NS

    def T(n, shp, dt=F32, kind="ExternalInput"):
        return nc.dram_tensor(n, shp, dt, kind=kind).ap()

    xT_h = T("xT_h", [D, S])
    x_q = T("x_q", [TQ, D])
    wqT_i = T("wqT_i", [D, DL]); wkT_i = T("wkT_i", [D, DL]); wvT_i = T("wvT_i", [D, DL])
    bq_l = T("bq_l", [2, 128]); bk_l = T("bk_l", [2, 128]); bv_l = T("bv_l", [2, 128])
    bv_r = T("bv_r", [1, DL])
    woT_i = T("woT_i", [D, D]); bo_r = T("bo_r", [1, D])
    wgT_i = T("wgT_i", [D, 16])
    w1T_i = T("w1T_i", [c.E * D, c.HID]); w3T_i = T("w3T_i", [c.E * D, c.HID])
    w2T_i = T("w2T_i", [c.E * c.HID, D])
    idxq = T("idxq", [128, 8], I32)
    y_out = T("y_out", [TQ, D], F32, kind="ExternalOutput")

    xT_d = T("xT_d", [D, S], BF16, "Internal")
    wqT_bf = T("wqT_bf", [D, DL], BF16, "Internal")
    wkT_bf = T("wkT_bf", [D, DL], BF16, "Internal")
    wvT_bf = T("wvT_bf", [D, DL], BF16, "Internal")
    woT_bf = T("woT_bf", [D, D], BF16, "Internal")
    wgT_bf = T("wgT_bf", [D, 16], BF16, "Internal")
    w1T_d = T("w1T_d", [c.E * D, c.HID], BF16, "Internal")
    w3T_d = T("w3T_d", [c.E * D, c.HID], BF16, "Internal")
    w2T_d = T("w2T_d", [c.E * c.HID, D], BF16, "Internal")
    khead_d = T("khead_d", [HPC * 64, S], BF16, "Internal")
    qhat_d = T("qhat_d", [HPC * DHA, S], BF16, "Internal")
    attnT_d = T("attnT_d", [DL, S], BF16, "Internal")
    attnT_full = T("attnT_full", [D, S], BF16, "Internal")
    l_d = T("l_d", [1, S], F32, "Internal")
    rl_d = T("rl_d", [1, S], F32, "Internal")
    sT_dr = T("sT_dr", [c.E, TQ], F32, "Internal")
    s_full = T("s_full", [c.CORES * c.E, TQ], F32, "Internal")
    s_re = T("s_re", [128, c.CORES * TQ // 16], F32, "Internal")
    ln1_aug = T("ln1_aug", [TQ, c.AUGW], BF16, "Internal")
    tokslot_d = T("tokslot_d", [c.E * c.XRT, 1], I32, "Internal")
    xg_d = T("xg_d", [c.E * c.XROW, c.AUGW], BF16, "Internal")
    ysl_d = T("ysl_d", [c.E * c.XROW, D], BF16, "Internal")

    dbg_t = {}
    for name, shp, dt in [("d_attnT", [DL, S], BF16), ("d_ln1", [TQ, D], F32),
                          ("d_sT", [c.E, TQ], F32), ("d_prm", [8, 16], F32),
                          ("d_wl", [c.E, TQ], F32), ("d_slot", [c.E, TQ], F32),
                          ("d_qd", [HPC * DHA, S], BF16),
                          ("d_kd", [HPC * 64, S], BF16),
                          ("d_vd", [DL, S], BF16),
                          ("d_vtok", [128, (S // 128) * DL], BF16)]:
        if name in dbg:
            dbg_t[name] = T(name, shp, dt, "ExternalOutput")

    st = ExitStack()
    with st:
        e = st.enter_context
        sb = lambda n, shp, dt: e(nc.sbuf_tensor(n, shp, dt))

        ones_rf = sb("ones_rf", [1, 128], F32)
        ones_cb = sb("ones_cb", [128, 1], BF16)
        ident_b = sb("ident_b", [128, 128], BF16)
        ident_f = sb("ident_f", [128, 128], F32)
        wmask = sb("wmask", [128, 896], BF16)
        g8 = sb("g8", [128, 8], F32)
        g8t = sb("g8t", [8, 128], F32)
        bqT = sb("bqT", [128, 2], F32)
        bkT = sb("bkT", [128, 2], F32)
        bvT = sb("bvT", [128, 2], F32)
        bo_row = sb("bo_row", [1, D], F32)
        idxq_sb = sb("idxq_sb", [128, c.DCH], I32)
        zrow_b = sb("zrow_b", [128, c.AUGW], BF16)
        zrow_f = sb("zrow_f", [128, D], F32)
        tokrow = sb("tokrow", [128, c.TQS], I32)
        zi32 = sb("zi32", [8, c.XRT], I32)
        stats = sb("stats", [1, 64], F32)
        ln_ms = sb("ln_ms", [128, 6], F32)
        eps_c = sb("eps_c", [128, 1], F32)

        em = Emit(nc)
        for s_ in ["LDG", "XBR", "LDS", "PE", "DV", "AC", "GP", "STO", "IND",
                   "CC1", "CC2", "ATS", "LRL", "SYA", "SYB", "XQG", "XMW",
                   "PRJ", "KH0", "KH1", "CX", "CW", "COG", "BIA", "LN1S",
                   "STD", "SRE", "AGL", "XGZ", "AUGC", "XGL", "WOT", "XB0", "XB1", "XS0", "XS1", "PR0", "PR1", "PRM", "XT0", "XT1", "LDI", "LS0", "LS1", "RL0", "RL1", "AT0", "AT1"]:
            em.sem(s_)
        for i in range(c.E):
            em.sem(f"XW{i}"); em.sem(f"SC{i}"); em.sem(f"YS{i}"); em.sem(f"WC{i}")

        def ln_emit(E, slab, out_bf):
            E.do("dv", lambda v: v.reduce_sum(ln_ms[:, 0:1], slab[:, :], AX.X),
                 sem="DV", waits=[("AC", E.cnt["AC"]), ("DV", E.cnt["DV"])])
            E.do("dv", lambda v: v.tensor_scalar_mul(
                ln_ms[:, 1:2], ln_ms[:, 0:1], -1.0 / D), sem="DV", waits=["DV"])
            E.do("ac", lambda s: s.activation(
                sq_sc[:, :], slab[:, :], AF.Square, bias=ln_ms[:, 1:2],
                accum_out=ln_ms[:, 2:3]), sem="AC", waits=[("DV", E.cnt["DV"])])
            E.do("ac", lambda s: s.activation(
                ln_ms[:, 3:4], ln_ms[:, 2:3], AF.Sqrt, bias=eps_c[:, :],
                scale=1.0 / D), sem="AC", waits=["AC"])
            E.do("dv", lambda v: v.reciprocal(ln_ms[:, 4:5], ln_ms[:, 3:4]),
                 sem="DV", waits=[("AC", E.cnt["AC"])])
            E.do("dv", lambda v: v.tensor_mul(
                ln_ms[:, 5:6], ln_ms[:, 4:5], ln_ms[:, 4:5]), sem="DV", waits=["DV"])
            E.do("dv", lambda v: v.tensor_scalar(
                ln_ms[:, 0:1], ln_ms[:, 2:3], 1.0 / D, 1e-5, ALU.mult, ALU.add),
                sem="DV", waits=["DV"])
            E.do("dv", lambda v: v.tensor_mul(
                ln_ms[:, 5:6], ln_ms[:, 5:6], ln_ms[:, 0:1]), sem="DV", waits=["DV"])
            E.do("dv", lambda v: v.tensor_scalar(
                ln_ms[:, 5:6], ln_ms[:, 5:6], -0.5, 1.5, ALU.mult, ALU.add),
                sem="DV", waits=["DV"])
            E.do("dv", lambda v: v.tensor_mul(
                ln_ms[:, 4:5], ln_ms[:, 4:5], ln_ms[:, 5:6]), sem="DV", waits=["DV"])
            E.do("dv", lambda v: v.tensor_scalar(
                slab[:, :], slab[:, :], ln_ms[:, 1:2], ln_ms[:, 4:5],
                ALU.add, ALU.mult), sem="DV", waits=["DV"])
            if out_bf is not None:
                E.do("dv", lambda v: v.tensor_copy(out_bf[:, :], slab[:, :]),
                     sem="DV", waits=["DV"])

        with nc.Block() as b0:
            def p0(E):
                for t_ in (ones_rf, ones_cb, ident_b, ident_f, wmask, g8, g8t):
                    E.do("dv", lambda v, t_=t_: v.memset(t_[:, :], 1.0), sem="DV")
                E.do("dv", lambda v: v.memset(zrow_b[:, :], 0.0), sem="DV")
                E.do("dv", lambda v: v.memset(zrow_f[:, :], 0.0), sem="DV")
                E.do("dv", lambda v: v.memset(eps_c[:, :], 1e-5), sem="DV")
                gsel = [(ident_b, [[1, 128]], ALU.is_equal, 0, -1),
                        (ident_f, [[1, 128]], ALU.is_equal, 0, -1),
                        (wmask, [[1, 896]], ALU.is_ge, -384, -1),
                        (g8, [[-16, 8]], ALU.is_ge, 0, 1),
                        (g8, [[16, 8]], ALU.is_ge, 15, -1),
                        (g8t, [[1, 128]], ALU.is_ge, 0, -16),
                        (g8t, [[-1, 128]], ALU.is_ge, 15, 16)]
                for t_, pat, op, base, cm in gsel:
                    E.do("gp", lambda g, t_=t_, pat=pat, op=op, base=base, cm=cm:
                         g.affine_select(t_[:, :], t_[:, :], pat, op, 0.0,
                                         base=base, channel_multiplier=cm),
                         sem="GP", waits=["DV", "GP"])
                E.do("sy", lambda s: s.dma_start(bo_row[:, :], bo_r), sem="LDS", n=16)
                E.do("sy", lambda s: s.dma_start(
                    idxq_sb[:, :], idxq), sem="LDI", n=16)
                E.do("gp", lambda g: g.dma_start(xT_d, xT_h), sem="CX", n=16)
                for dst, src in [(wqT_bf, wqT_i), (wkT_bf, wkT_i), (wvT_bf, wvT_i)]:
                    E.do("gp", lambda g, dst=dst, src=src: g.dma_start(dst, src),
                         sem="CW", n=16)
                for dst, src in [(woT_bf, woT_i), (wgT_bf, wgT_i)]:
                    E.do("gp", lambda g, dst=dst, src=src: g.dma_start(dst, src),
                         sem="COG", n=16)
                E.do("gp", lambda g: g.memset(zi32[:, :], 0), sem="GP",
                     waits=["GP"])
                zw = [("DV", E.cnt["DV"]), ("GP", E.cnt["GP"])]
                for ei in range(c.E):
                    for rr in range(0, c.XROW, 128):
                        n_ = min(128, c.XROW - rr)
                        E.do("sy", lambda s, ei=ei, rr=rr, n_=n_: s.dma_start(
                            ysl_d[ei * c.XROW + rr:ei * c.XROW + rr + n_, :],
                            zrow_b[0:n_, 0:D]), sem="XGZ", n=16, waits=zw)

            em.phase(b0, p0)

        stA = ExitStack()
        with stA:
            eA = stA.enter_context
            v_tok = eA(nc.sbuf_tensor("v_tok", [128, (S // 128) * DL], BF16))
            st1 = ExitStack()
            with st1:
                e1 = st1.enter_context
                wqT = [e1(nc.sbuf_tensor(f"wqT{i}", [128, DL], BF16))
                       for i in range(c.DCH)]
                wkT = [e1(nc.sbuf_tensor(f"wkT{i}", [128, DL], BF16))
                       for i in range(c.DCH)]
                wvT = [e1(nc.sbuf_tensor(f"wvT{i}", [128, DL], BF16))
                       for i in range(c.DCH)]
                xts = [[e1(nc.sbuf_tensor(f"xts{i}_{j}", [128, 512], BF16))
                        for j in range(c.DCH)] for i in range(2)]
                prj = [e1(nc.sbuf_tensor(f"prj{i}", [128, 512], BF16))
                       for i in range(2)]
                btmp = [e1(nc.sbuf_tensor(f"btmp{i}", [2, 128], F32))
                        for i in range(3)]
                bv_bc = e1(nc.sbuf_tensor("bv_bc", [128, DL], F32))
                bvr_s = e1(nc.sbuf_tensor("bvr_s", [1, DL], F32))
                sqs = e1(nc.sbuf_tensor("sqs", [128, 512], BF16))
                karr = e1(nc.sbuf_tensor("karr", [1, HPC * NS], F32))
                qn_s = e1(nc.sbuf_tensor("qn_s", [1, 512], F32))
                mh_s = e1(nc.sbuf_tensor("mh_s", [1, 512], BF16))
                p_pr = [e1(nc.psum_tensor(f"p_pr{i}", [128, 512], F32))
                        for i in range(2)]
                p_sm = [e1(nc.psum_tensor(f"p_sm{i}", [1, 512], F32))
                        for i in range(2)]
                p_bT = e1(nc.psum_tensor("p_bT", [128, 2], F32))
                p_bv = [e1(nc.psum_tensor(f"p_bv{i}", [128, 256], F32))
                        for i in range(2)]

                with nc.Block() as b1:
                    def p1(E):
                        for i, bin_ in enumerate([bq_l, bk_l, bv_l]):
                            E.do("gp", lambda g, bin_=bin_, i=i: g.dma_start(
                                btmp[i][:, :], bin_), sem="BIA", n=16)
                        E.do("gp", lambda g: g.dma_start(bvr_s[:, :], bv_r),
                             sem="BIA", n=16)
                        for i, (bt, scl) in enumerate([(bqT, 0.125), (bkT, 1.0),
                                                       (bvT, 1.0)]):
                            E.do("pe", lambda t, i=i: t.transpose(
                                p_bT[:, :], btmp[i][:, :], ident_f[0:2, 0:2]),
                                sem="PE", waits=[("BIA", 64), ("GP", E.cnt["GP"]),
                                                 ("DV", E.cnt["DV"])])
                            E.do("dv", lambda v, bt=bt, scl=scl: v.tensor_scalar_mul(
                                bt[:, :], p_bT[:, :], scl), sem="DV", waits=["PE"])
                        for hf in range(2):
                            E.do("pe", lambda t, hf=hf: t.matmul(
                                p_bv[hf][:, 0:128], ones_rf[:, :],
                                bvr_s[0:1, hf * 128:(hf + 1) * 128],
                                start=True, stop=True),
                                sem="PE", waits=[("BIA", 64), ("DV", E.cnt["DV"])])
                            E.do("dv", lambda v, hf=hf: v.tensor_copy(
                                bv_bc[:, hf * 128:(hf + 1) * 128],
                                p_bv[hf][:, 0:128]), sem="DV", waits=["PE"])
                        bvbc_ok = ("DV", E.cnt["DV"])
                        for wbf, wt in [(wqT_bf, wqT), (wkT_bf, wkT),
                                        (wvT_bf, wvT)]:
                            for cc in range(c.DCH):
                                E.do("sy", lambda s, wbf=wbf, wt=wt, cc=cc:
                                     s.dma_start(wt[cc][:, :],
                                                 wbf[cc * 128:(cc + 1) * 128, :]),
                                     sem="SYA", n=16, waits=[("CW", 48)])
                        wT_ok = ("SYA", E.cnt["SYA"])

                        evac_ps = [None, None]
                        prj_free = [None, None]
                        sqs_free = None
                        psm_free = [None, None]
                        pbv_free = [bvbc_ok, bvbc_ok]
                        for wi, kind in enumerate(["k", "q", "v"]):
                            wt = {"k": wkT, "q": wqT, "v": wvT}[kind]
                            bt = {"k": bkT, "q": bqT, "v": bvT}[kind]
                            scl = 0.125 if kind == "q" else 1.0
                            pe_round = {}
                            for tc in range(NS):
                                bsel = tc % 2
                                ldsem = "SYB" if bsel else "SYA"
                                wld = [wT_ok, ("CX", 16)]
                                if tc >= 2:
                                    wld.append(("PE", pe_round[tc - 2]))
                                elif wi > 0:
                                    wld.append(("PE", E.cnt["PE"]))
                                for cc in range(c.DCH):
                                    E.do("sy", lambda s, tc=tc, cc=cc, bsel=bsel:
                                         s.dma_start(
                                             xts[bsel][cc][:, :],
                                             xT_d[cc * 128:(cc + 1) * 128,
                                                  tc * 512:(tc + 1) * 512]),
                                         sem=ldsem, n=16, waits=wld)
                                ldval = (ldsem, E.cnt[ldsem])
                                if kind == "v":
                                    for tb in range(4):
                                        pvi = tb % 2
                                        pv = p_bv[pvi]
                                        mm_w = [ldval, pbv_free[pvi]]
                                        for cc in range(c.DCH):
                                            E.do("pe", lambda t, pv=pv, cc=cc,
                                                 tb=tb, bsel=bsel:
                                                 t.matmul(
                                                     pv[:, :],
                                                     xts[bsel][cc][:, tb * 128:
                                                                   (tb + 1) * 128],
                                                     wvT[cc][:, :],
                                                     start=(cc == 0),
                                                     stop=(cc == c.DCH - 1)),
                                                 sem="PE",
                                                 waits=mm_w if cc == 0 else ())
                                        tg = tc * 4 + tb
                                        E.do("dv", lambda v, pv=pv, tg=tg:
                                             v.scalar_tensor_tensor(
                                                 v_tok[:, tg * DL:(tg + 1) * DL],
                                                 pv[:, :], 0.0, bv_bc[:, :],
                                                 ALU.add, ALU.add),
                                             sem="DV", waits=["PE", bvbc_ok])
                                        pbv_free[pvi] = ("DV", E.cnt["DV"])
                                    pe_round[tc] = E.cnt["PE"]
                                    continue
                                for dt in range(DL // 128):
                                    pidx = (tc * 2 + dt) % 2
                                    pp = p_pr[pidx]
                                    mm_w = [ldval]
                                    if evac_ps[pidx] is not None:
                                        mm_w.append(evac_ps[pidx])
                                    for cc in range(c.DCH):
                                        E.do("pe", lambda t, pp=pp, wt=wt, cc=cc,
                                             dt=dt, bsel=bsel:
                                             t.matmul(
                                                 pp[:, :],
                                                 wt[cc][:, dt * 128:(dt + 1) * 128],
                                                 xts[bsel][cc][:, :],
                                                 start=(cc == 0),
                                                 stop=(cc == c.DCH - 1)),
                                             sem="PE", waits=mm_w if cc == 0 else ())
                                    sidx = (tc * 2 + dt) % 2
                                    sg = prj[sidx]
                                    ew = ["PE"]
                                    if prj_free[sidx] is not None:
                                        ew.append(prj_free[sidx])
                                    ev = E.do("ac", lambda s, pp=pp, sg=sg, bt=bt,
                                              dt=dt, scl=scl:
                                              s.activation(sg[:, :], pp[:, :],
                                                           AF.Identity,
                                                           bias=bt[:, dt:dt + 1],
                                                           scale=scl),
                                              sem="AC", waits=ew)
                                    evac_ps[pidx] = ev
                                    if kind == "k":
                                        dmas = [(khead_d[dt * 128:(dt + 1) * 128,
                                                         tc * 512:(tc + 1) * 512],
                                                 sg[:, :])]
                                    else:
                                        dmas = []
                                        for hh in range(2):
                                            h = dt * 2 + hh
                                            dmas.append(
                                                (qhat_d[h * DHA:h * DHA + 64,
                                                        tc * 512:(tc + 1) * 512],
                                                 sg[hh * 64:(hh + 1) * 64, :]))
                                    prs = "PR0" if sidx == 0 else "PR1"
                                    for dst_, src_ in dmas:
                                        E.do("gp", lambda g, dst_=dst_, src_=src_:
                                             g.dma_start(dst_, src_),
                                             sem=prs, n=16, waits=[ev])
                                    prj_free[sidx] = (prs, E.cnt[prs])
                                    sqw = [ev]
                                    if sqs_free is not None:
                                        sqw.append(sqs_free)
                                    E.do("ac", lambda s, sg=sg: s.activation(
                                        sqs[:, :], sg[:, :], AF.Square),
                                        sem="AC", waits=sqw)
                                    sq_ok = ("AC", E.cnt["AC"])
                                    for hh in range(2):
                                        h = dt * 2 + hh
                                        pb2 = hh
                                        mmw = [sq_ok]
                                        if psm_free[pb2] is not None:
                                            mmw.append(psm_free[pb2])
                                        E.do("pe", lambda t, hh=hh, pb2=pb2:
                                             t.matmul(
                                                 p_sm[pb2][:, :],
                                                 ones_cb[hh * 64:hh * 64 + 64, :],
                                                 sqs[hh * 64:(hh + 1) * 64, :],
                                                 start=True, stop=True),
                                             sem="PE", waits=mmw)
                                        if kind == "k":
                                            E.do("dv", lambda v, h=h, tc=tc, pb2=pb2:
                                                 v.reduce_max(
                                                     karr[0:1, h * NS + tc:
                                                          h * NS + tc + 1],
                                                     p_sm[pb2][:, :], AX.X),
                                                 sem="DV", waits=["PE"])
                                            psm_free[pb2] = ("DV", E.cnt["DV"])
                                        else:
                                            E.do("ac", lambda s, pb2=pb2:
                                                 s.activation(qn_s[:, :],
                                                              p_sm[pb2][:, :],
                                                              AF.Sqrt),
                                                 sem="AC",
                                                 waits=["PE", ("DV", E.cnt["DV"])])
                                            E.do("dv", lambda v, h=h:
                                                 v.tensor_scalar(
                                                     mh_s[:, :], qn_s[:, :],
                                                     stats[0:1, h:h + 1], None,
                                                     ALU.mult),
                                                 sem="DV",
                                                 waits=[("AC", E.cnt["AC"]),
                                                        ("PRM", E.cnt["PRM"])])
                                            E.do("gp", lambda g, h=h, tc=tc:
                                                 g.dma_start(
                                                     qhat_d[h * DHA + 64:
                                                            h * DHA + 65,
                                                            tc * 512:(tc + 1) * 512],
                                                     mh_s[:, :]),
                                                 sem="PRM", n=16,
                                                 waits=[("DV", E.cnt["DV"]),
                                                        ("PRM", E.cnt["PRM"])])
                                            psm_free[pb2] = ("AC", E.cnt["AC"])
                                    sqs_free = ("PE", E.cnt["PE"])
                                pe_round[tc] = E.cnt["PE"]
                            if kind == "k":
                                for h in range(HPC):
                                    E.do("dv", lambda v, h=h: v.reduce_max(
                                        stats[0:1, 32 + h:33 + h],
                                        karr[0:1, h * NS:(h + 1) * NS], AX.X),
                                        sem="DV", waits=["DV"])
                                    E.do("ac", lambda s, h=h: s.activation(
                                        stats[0:1, 16 + h:17 + h],
                                        stats[0:1, 32 + h:33 + h], AF.Sqrt),
                                        sem="AC", waits=[("DV", E.cnt["DV"]),
                                                         ("AC", E.cnt["AC"])])
                                    E.do("dv", lambda v, h=h: v.tensor_scalar_mul(
                                        stats[0:1, h:h + 1],
                                        stats[0:1, 16 + h:17 + h], -1.0),
                                        sem="DV", waits=[("AC", E.cnt["AC"])])
                    em.phase(b1, p1)
            PRJ_ALL = [("PR0", em.cnt["PR0"]), ("PR1", em.cnt["PR1"]),
                       ("PRM", em.cnt["PRM"])]

            st2 = ExitStack()
            with st2:
                e2 = st2.enter_context
                kh_buf = [e2(nc.sbuf_tensor(f"khb{i}", [DHA, S], BF16))
                          for i in range(2)]
                qh_buf = [e2(nc.sbuf_tensor(f"qhb{i}", [DHA, S], BF16))
                          for i in range(2)]
                v_aug = [e2(nc.sbuf_tensor(f"vaug{i}", [128, NKB * DHA], BF16))
                         for i in range(2)]
                pt = [e2(nc.sbuf_tensor(f"pt{i}", [128, 1024], BF16))
                      for i in range(4)]
                raw_h = e2(nc.sbuf_tensor("raw_h", [64, S], F32))
                rlb = [e2(nc.sbuf_tensor(f"rlb{i}", [64, 512], F32))
                       for i in range(2)]
                ltile = e2(nc.sbuf_tensor("ltile", [NS, c.SW], F32))
                l_st = [e2(nc.sbuf_tensor(f"l_st{i}", [1, 512], F32))
                        for i in range(2)]
                rl_st = [e2(nc.sbuf_tensor(f"rlst{i}", [1, 512], F32))
                         for i in range(2)]
                att_st = [e2(nc.sbuf_tensor(f"attst{i}", [64, 512], BF16))
                          for i in range(2)]
                s_ps = [e2(nc.psum_tensor(f"s_ps{i}", [128, 1024], F32))
                        for i in range(3)]
                ot_ps = [e2(nc.psum_tensor(f"ot_ps{i}", [DHA, 512], F32))
                         for i in range(2)]


                with nc.Block() as b2:
                    def p2(E):
                        for i in range(2):
                            E.do("dv", lambda v, i=i: v.memset(
                                v_aug[i][:, :].rearrange("p (k a) -> p k a", a=DHA)
                                [:, :, 64:65], 1.0), sem="DV")
                            E.do("dv", lambda v, i=i: v.memset(
                                kh_buf[i][64:65, :], 1.0), sem="DV")

                        vab_ready = [None, None]
                        ot_evac = [None, None]
                        exp_of_sp = [None, None, None]
                        av_of_pt = [None, None, None, None]
                        attn_of_hb = [None, None]
                        kq_val = {}
                        for h in range(HPC):
                            hb = h % 2
                            vb = h % 2
                            lrl_hs = ("LRL", E.cnt["LRL"])
                            rl_hs = [("RL0", E.cnt["RL0"]), ("RL1", E.cnt["RL1"])]
                            def kq_load(E, hh):
                                hbb = hh % 2
                                ks = "KH0" if hbb == 0 else "KH1"
                                lw = PRJ_ALL + [("DV", E.cnt["DV"])]
                                if attn_of_hb[hbb] is not None:
                                    lw.append(attn_of_hb[hbb])
                                E.do("gp", lambda g, hh=hh, hbb=hbb: g.dma_start(
                                    kh_buf[hbb][0:64, :],
                                    khead_d[hh * 64:(hh + 1) * 64, :]),
                                    sem=ks, n=16, waits=lw)
                                E.do("gp", lambda g, hh=hh, hbb=hbb: g.dma_start(
                                    qh_buf[hbb][:, :],
                                    qhat_d[hh * DHA:(hh + 1) * DHA, :]),
                                    sem=ks, n=16)
                                kq_val[hh] = (ks, E.cnt[ks])
                            if h == 0:
                                kq_load(E, 0)
                                kq_load(E, 1)
                            kq_ok = kq_val[h]
                            cast_q = []
                            for ei in range(h * 2, h * 2 + 2):
                                for dst, srt, rows in [(w1T_d, w1T_i, D),
                                                       (w3T_d, w3T_i, D),
                                                       (w2T_d, w2T_i, c.HID)]:
                                    nch = 8
                                    crows = rows // nch
                                    for chk in range(nch):
                                        r0 = ei * rows + chk * crows
                                        cast_q.append((dst[r0:r0 + crows, :],
                                                       srt[r0:r0 + crows, :], ei))
                            wts = [("DV", E.cnt["DV"])]
                            if vab_ready[vb] is not None:
                                wts.append(vab_ready[vb])
                            E.do("dv", lambda v, h=h, vb=vb: v.tensor_copy(
                                v_aug[vb][:, :].rearrange("p (k a) -> p k a", a=DHA)
                                [:, :, 0:64],
                                v_tok[:, :].rearrange("p (k a) -> p k a", a=DL)
                                [:, :, h * 64:(h + 1) * 64]),
                                sem="DV", waits=wts)
                            va_ok = ("DV", E.cnt["DV"])
                            for s_ in range(NS):
                                ot = ot_ps[s_ % 2]
                                npair = (s_ + 1) * 2
                                LAG = 2
                                exp_rec = {}
                                mask_rec = {}

                                def emit_av(kp, E=None):
                                    pb = kp % 4
                                    av_w = [exp_rec[kp]]
                                    if mask_rec.get(kp) is not None:
                                        av_w.append(mask_rec[kp])
                                    if kp == 0 and ot_evac[s_ % 2] is not None:
                                        av_w.append(ot_evac[s_ % 2])
                                    for half in range(2):
                                        kb = 2 * kp + half
                                        E.do("pe", lambda t, kb=kb, pb=pb, half=half,
                                             kp=kp:
                                             t.matmul(
                                                 ot[:, :],
                                                 v_aug[vb][:, kb * DHA:(kb + 1) * DHA],
                                                 pt[pb][:, half * 512:(half + 1) * 512],
                                                 start=(kp == 0 and half == 0),
                                                 stop=(kp == npair - 1 and half == 1)),
                                             sem="PE", waits=av_w if half == 0 else ())
                                    av_of_pt[pb] = ("PE", E.cnt["PE"])

                                for kp in range(npair):
                                    pb = kp % 4
                                    spb = kp % 3
                                    sp = s_ps[spb]
                                    mm_w = [kq_ok, va_ok]
                                    if exp_of_sp[spb] is not None:
                                        mm_w.append(exp_of_sp[spb])
                                    for half in range(2):
                                        kb = 2 * kp + half
                                        E.do("pe", lambda t, sp=sp, hb=hb, kb=kb,
                                             s_=s_, half=half:
                                             t.matmul(sp[:, half * 512:(half + 1) * 512],
                                                      kh_buf[hb][:, kb * 128:(kb + 1) * 128],
                                                      qh_buf[hb][:, s_ * 512:(s_ + 1) * 512],
                                                      start=True, stop=True),
                                             sem="PE", waits=mm_w if half == 0 else ())
                                    smm = ("PE", E.cnt["PE"])
                                    ew = [smm]
                                    if av_of_pt[pb] is not None:
                                        ew.append(av_of_pt[pb])
                                    E.do("ac", lambda s, pb=pb, spb=spb: s.activation(
                                        pt[pb][:, :], s_ps[spb][:, :], AF.Exp),
                                        sem="AC", waits=ew)
                                    exp_of_sp[spb] = ("AC", E.cnt["AC"])
                                    exp_rec[kp] = ("AC", E.cnt["AC"])
                                    md = None
                                    for half in range(2):
                                        kb = 2 * kp + half
                                        m = kb - 4 * s_
                                        if 0 <= m < 4:
                                            E.do("dv", lambda v, pb=pb, half=half, m=m:
                                                 v.tensor_mul(
                                                     pt[pb][:, half * 512:(half + 1) * 512],
                                                     pt[pb][:, half * 512:(half + 1) * 512],
                                                     wmask[:, 384 - 128 * m:
                                                           896 - 128 * m]),
                                                 sem="DV", waits=[exp_rec[kp]])
                                            md = ("DV", E.cnt["DV"])
                                    mask_rec[kp] = md
                                    if kp >= LAG:
                                        emit_av(kp - LAG, E)
                                for kp in range(max(0, npair - LAG), npair):
                                    emit_av(kp, E)
                                otmm = ("PE", E.cnt["PE"])
                                lsm = "LS0" if s_ % 2 == 0 else "LS1"
                                E.do("dv", lambda v, ot=ot, s_=s_: v.tensor_copy(
                                    l_st[s_ % 2][:, :], ot[64:65, :]),
                                    sem="DV", waits=[otmm, (lsm, E.cnt[lsm])])
                                E.do("gp", lambda g, s_=s_: g.dma_start(
                                    l_d[0:1, s_ * 512:(s_ + 1) * 512],
                                    l_st[s_ % 2][:, :]),
                                    sem=lsm, n=16,
                                    waits=[("DV", E.cnt["DV"]), lrl_hs])
                                E.do("dv", lambda v, ot=ot, s_=s_: v.tensor_copy(
                                    raw_h[:, s_ * 512:(s_ + 1) * 512], ot[0:64, :]),
                                    sem="DV", waits=["DV"])
                                ot_evac[s_ % 2] = ("DV", E.cnt["DV"])
                                for _ in range(3):
                                    if cast_q:
                                        dst_, srt_, ei_ = cast_q.pop(0)
                                        E.do("gp", lambda g, dst_=dst_, srt_=srt_:
                                             g.dma_start(dst_, srt_),
                                             sem=f"WC{ei_}", n=16)
                            E.do("gp", lambda g: g.dma_start(
                                ltile[:, :],
                                l_d.rearrange("o (a b) -> (o a) b", b=c.SW)),
                                sem="LRL", n=16,
                                waits=["LRL", "LS0", "LS1", ("DV", E.cnt["DV"])])
                            E.do("dv", lambda v: v.reciprocal(
                                ltile[:, :], ltile[:, :]), sem="DV", waits=["LRL"])
                            E.do("gp", lambda g: g.dma_start(
                                rl_d.rearrange("o (a b) -> (o a) b", b=c.SW),
                                ltile[:, :]), sem="LRL", n=16,
                                waits=[("DV", E.cnt["DV"])] + rl_hs)
                            rl_done = ("LRL", E.cnt["LRL"])
                            while cast_q:
                                dst_, srt_, ei_ = cast_q.pop(0)
                                E.do("gp", lambda g, dst_=dst_, srt_=srt_:
                                     g.dma_start(dst_, srt_),
                                     sem=f"WC{ei_}", n=16)
                            for s_ in range(NS):
                                rb = s_ % 2
                                rsm = "RL0" if rb == 0 else "RL1"
                                E.do("gp", lambda g, s_=s_, rb=rb: g.dma_start(
                                    rlb[rb][:, :],
                                    bass.AP(rl_d.tensor, s_ * 512,
                                            [[0, 64], [1, 512]])),
                                    sem=rsm, n=16,
                                    waits=[rl_done, ("DV", E.cnt["DV"])])
                                rlv = (rsm, E.cnt[rsm])
                                ab = att_st[s_ % 2]
                                asm_ = "AT0" if s_ % 2 == 0 else "AT1"
                                E.do("dv", lambda v, s_=s_, ab=ab, rb=rb:
                                     v.tensor_mul(
                                         ab[:, :],
                                         raw_h[:, s_ * 512:(s_ + 1) * 512],
                                         rlb[rb][:, :]),
                                    sem="DV",
                                    waits=["DV", rlv, (asm_, E.cnt[asm_])])
                                E.do("gp", lambda g, h=h, s_=s_, ab=ab: g.dma_start(
                                    attnT_d[h * 64:(h + 1) * 64,
                                            s_ * 512:(s_ + 1) * 512], ab[:, :]),
                                    sem=asm_, n=16, waits=[("DV", E.cnt["DV"])])
                            vab_ready[vb] = ("PE", E.cnt["PE"])
                            attn_of_hb[hb] = ("PE", E.cnt["PE"])
                            if h + 2 < HPC:
                                kq_load(E, h + 2)
                        if "d_attnT" in dbg_t:
                            E.do("gp", lambda g: g.dma_start(
                                dbg_t["d_attnT"], attnT_d), sem="STO", n=16,
                                waits=["AT0", "AT1"])
                        E.do("gp", lambda g: g.collective_compute(
                            "AllGather", ALU.bypass,
                            replica_groups=[[0, 1, 2, 3], [4, 5, 6, 7]],
                            ins=[attnT_d.opt()], outs=[attnT_full.opt()]),
                            sem="CC1", n=1, waits=["AT0", "AT1"])
                    em.phase(b2, p2)

        stB = ExitStack()
        eB = stB.enter_context
        resid = [eB(nc.sbuf_tensor(f"resid{i}", [128, D], F32))
                 for i in range(c.TQS)]
        sq_sc = eB(nc.sbuf_tensor("sq_sc", [128, D], F32))
        scatT = eB(nc.sbuf_tensor("scatT", [128, c.E * c.TQS], F32))
        gathT = eB(nc.sbuf_tensor("gathT", [128, c.E * c.TQS], I32))
        idx_all = eB(nc.sbuf_tensor("idx_all", [128, c.E * 3], I32))
        stC = ExitStack()
        sT_loc = stC.enter_context(nc.sbuf_tensor("sT_loc", [c.E, TQ], F32))

        st3 = ExitStack()
        with st3:
            e3 = st3.enter_context
            attnQ = [e3(nc.sbuf_tensor(f"attnQ{i}", [128, TQ], BF16))
                     for i in range(c.DCH)]
            woT = [e3(nc.sbuf_tensor(f"woT{i}", [128, D], BF16))
                   for i in range(c.DCH)]
            wgT = [e3(nc.sbuf_tensor(f"wgT{i}", [128, 16], BF16))
                   for i in range(c.DCH)]
            lt_sb = [e3(nc.sbuf_tensor(f"lt_sb{i}", [128, TQ], BF16))
                     for i in range(c.DCH)]
            ln1b = e3(nc.sbuf_tensor("ln1b", [128, D], BF16))
            xq_t = [e3(nc.sbuf_tensor(f"xq_t{i}", [128, D], F32)) for i in range(2)]
            bo_bc = e3(nc.sbuf_tensor("bo_bc", [128, D], F32))
            p_o = [e3(nc.psum_tensor(f"p_o{i}", [128, 512], F32)) for i in range(2)]
            p_g = e3(nc.psum_tensor("p_g", [16, 512], F32))
            p_lt = e3(nc.psum_tensor("p_lt", [128, 128], BF16))

            with nc.Block() as b3:
                def p3(E):
                    for cc in range(c.DCH):
                        E.do("sy", lambda s, cc=cc: s.dma_start(
                            woT[cc][:, :], woT_bf[cc * 128:(cc + 1) * 128, :]),
                            sem="WOT", n=16, waits=[("COG", 32)])
                        E.do("sy", lambda s, cc=cc: s.dma_start(
                            wgT[cc][:, 0:16], wgT_bf[cc * 128:(cc + 1) * 128, :]),
                            sem="WOT", n=16, waits=[("COG", 32)])
                    xbr_o = ("WOT", E.cnt["WOT"])
                    for half in range(2):
                        E.do("pe", lambda t, half=half: t.matmul(
                            p_o[half][:, :], ones_rf[:, :],
                            bo_row[0:1, half * 512:(half + 1) * 512],
                            start=True, stop=True), sem="PE", waits=[("LDS", 16)])
                        E.do("dv", lambda v, half=half: v.tensor_copy(
                            bo_bc[:, half * 512:(half + 1) * 512], p_o[half][:, :]),
                            sem="DV", waits=["PE"])
                    bo_ok = ("DV", E.cnt["DV"])
                    for cc in range(c.DCH):
                        E.do("gp", lambda g, cc=cc: g.indirect_dma_start(
                            attnQ[cc][:, :], None,
                            attnT_full.rearrange("(o a) b -> o (a b)", o=1),
                            bass.IndirectOffsetOnAxis(idxq_sb[:, cc:cc + 1], 1)),
                            sem="XQG", n=16, waits=[("CC1", 1), ("LDI", 16),
                                                    ("STO", E.cnt["STO"])])
                    xqg = ("XQG", E.cnt["XQG"])

                    evac_po = [bo_ok, bo_ok]
                    for tt in range(c.TQS):
                        xb = tt % 2
                        xsem = "SYB" if xb else "SYA"
                        E.do("sy", lambda s, tt=tt, xb=xb: s.dma_start(
                            xq_t[xb][:, :], x_q[tt * 128:(tt + 1) * 128, :]),
                            sem=xsem, n=16,
                            waits=[("DV", E.cnt["DV"])] if tt >= 2 else [])
                        xqv = (xsem, E.cnt[xsem])
                        for ch in range(2):
                            pp = p_o[ch]
                            for cc in range(c.DCH):
                                E.do("pe", lambda t, pp=pp, cc=cc, tt=tt, ch=ch:
                                     t.matmul(pp[:, :],
                                              attnQ[cc][:, tt * 128:(tt + 1) * 128],
                                              woT[cc][:, ch * 512:(ch + 1) * 512],
                                              start=(cc == 0), stop=(cc == c.DCH - 1)),
                                     sem="PE",
                                     waits=[xqg, xbr_o, evac_po[ch]] if cc == 0 else ())
                            E.do("dv", lambda v, pp=pp, tt=tt, ch=ch, xb=xb:
                                 v.scalar_tensor_tensor(
                                     resid[tt][:, ch * 512:(ch + 1) * 512], pp[:, :],
                                     0.0, xq_t[xb][:, ch * 512:(ch + 1) * 512],
                                     ALU.add, ALU.add), sem="DV",
                                 waits=["PE", xqv])
                            E.do("dv", lambda v, tt=tt, ch=ch: v.tensor_add(
                                resid[tt][:, ch * 512:(ch + 1) * 512],
                                resid[tt][:, ch * 512:(ch + 1) * 512],
                                bo_bc[:, ch * 512:(ch + 1) * 512]), sem="DV",
                                waits=["DV"])
                            evac_po[ch] = ("DV", E.cnt["DV"])
                        ln_emit(E, resid[tt], ln1b)
                        E.do("gp", lambda g, tt=tt: g.dma_start(
                            ln1_aug[tt * 128:(tt + 1) * 128, 0:D], ln1b[:, :]),
                            sem="LN1S", n=16, waits=[("DV", E.cnt["DV"])])
                        for cc in range(c.DCH):
                            E.do("pe", lambda t, cc=cc: t.transpose(
                                p_lt[:, :], ln1b[:, cc * 128:(cc + 1) * 128],
                                ident_b[:, :]),
                                sem="PE", waits=[("DV", E.cnt["DV"]),
                                                 ("DV", E.cnt["DV"])])
                            E.do("dv", lambda v, cc=cc, tt=tt: v.tensor_copy(
                                lt_sb[cc][:, tt * 128:(tt + 1) * 128], p_lt[:, :]),
                                sem="DV", waits=["PE"])
                        E.do("dv", lambda v: v.engine_nop(), sem="DV",
                             waits=[("LN1S", E.cnt["LN1S"])])
                        if "d_ln1" in dbg_t:
                            E.do("gp", lambda g, tt=tt: g.dma_start(
                                dbg_t["d_ln1"][tt * 128:(tt + 1) * 128, :],
                                resid[tt][:, :]), sem="STO", n=16,
                                waits=[("DV", E.cnt["DV"])])
                    ln1_st = ("LN1S", E.cnt["LN1S"])

                    lt_ok = ("DV", E.cnt["DV"])
                    for gs in range(c.GST):
                        for cc in range(c.DCH):
                            E.do("pe", lambda t, gs=gs, cc=cc: t.matmul(
                                p_g[:, :], wgT[cc][:, 0:16],
                                lt_sb[cc][:, gs * 512:(gs + 1) * 512],
                                start=(cc == 0), stop=(cc == c.DCH - 1)),
                                sem="PE",
                                waits=[lt_ok, ("DV", E.cnt["DV"])] if cc == 0 else ())
                        E.do("dv", lambda v, gs=gs: v.tensor_copy(
                            sT_loc[:, gs * 512:(gs + 1) * 512], p_g[0:8, :]),
                            sem="DV", waits=["PE"])
                    E.do("gp", lambda g: g.dma_start(sT_dr, sT_loc[:, :]),
                         sem="STD", n=16, waits=[("DV", E.cnt["DV"])])
                    E.do("gp", lambda g: g.collective_compute(
                        "AllGather", ALU.bypass,
                        replica_groups=[[0, 1, 2, 3, 4, 5, 6, 7]],
                        ins=[sT_dr.opt()], outs=[s_full.opt()]),
                        sem="CC2", n=1, waits=["STD"])
                    if "d_sT" in dbg_t:
                        E.do("gp", lambda g: g.dma_start(dbg_t["d_sT"], sT_loc[:, :]),
                             sem="STO", n=16, waits=[("DV", E.cnt["DV"])])
                em.phase(b3, p3)

        NW = c.CORES * TQ // 16
        st4 = ExitStack()
        with st4:
            e4 = st4.enter_context
            S_all = e4(nc.sbuf_tensor("S_all", [128, NW], F32))
            scr = e4(nc.sbuf_tensor("scr", [128, NW], F32))
            scr2 = e4(nc.sbuf_tensor("scr2", [128, NW], F32))
            sv8 = e4(nc.sbuf_tensor("sv8", [8, 16], F32))
            prow = e4(nc.sbuf_tensor("prow", [1, 16], F32))
            colp = e4(nc.sbuf_tensor("colp", [128, 4], F32))
            r1x = e4(nc.sbuf_tensor("r1x", [1, 128], F32))
            mk = e4(nc.sbuf_tensor("mk", [c.E, TQ], F32))
            wl = e4(nc.sbuf_tensor("wl", [c.E, TQ], F32))
            ic = e4(nc.sbuf_tensor("ic", [c.E, TQ], F32))
            on8 = e4(nc.sbuf_tensor("on8", [c.E, TQ], F32))
            ok8 = e4(nc.sbuf_tensor("ok8", [c.E, TQ], F32))
            sf8 = e4(nc.sbuf_tensor("sf8", [c.E, TQ], F32))
            gf8 = e4(nc.sbuf_tensor("gf8", [c.E, TQ], F32))
            trb = e4(nc.sbuf_tensor("trb", [128, 8], BF16))
            tsramp = e4(nc.sbuf_tensor("tsramp", [c.E, TQ], F32))
            tsneg = e4(nc.sbuf_tensor("tsneg", [c.E, TQ], F32))
            rampJ = e4(nc.sbuf_tensor("rampJ", [128, 128], F32))
            c3 = e4(nc.sbuf_tensor("c3", [128, 3], F32))
            rhs2 = e4(nc.sbuf_tensor("rhs2", [128, 2 * c.TQS], BF16))
            mt_bf = [e4(nc.sbuf_tensor(f"mt_bf{i}", [128, 128], BF16))
                     for i in range(4)]
            tok_all_f = e4(nc.sbuf_tensor("tok_all_f", [128, c.E * 3], F32))
            pall = e4(nc.sbuf_tensor("pall", [128, c.E * 6], F32))
            p_s1 = e4(nc.psum_tensor("p_s1", [128, 8], F32))
            p_s2 = e4(nc.psum_tensor("p_s2", [8, 8], F32))
            p_tr = e4(nc.psum_tensor("p_tr", [1, 128], F32))
            p_inv = e4(nc.psum_tensor("p_inv", [128, 48], F32))

            with nc.Block() as b4:
                def p4(E):
                    for hh in range(2):
                        E.do("gp", lambda g, hh=hh: g.dma_start(
                            s_re.rearrange("(e r h) q -> e r h q", e=c.E,
                                           r=c.CORES, h=2)[:, :, hh, :],
                            s_full.rearrange("(r e) (h q) -> r e h q", r=c.CORES,
                                             e=c.E, h=2).transpose([1, 0, 2, 3])
                            [:, :, hh, :]),
                            sem="SRE", n=16, waits=[("CC2", 1)])
                    sal = E.do("gp", lambda g: g.dma_start(
                        S_all[:, :], s_re), sem="SRE", n=16, waits=["SRE"])

                    E.do("dv", lambda v: v.memset(sv8[:, :], 0.0), sem="DV")
                    E.do("dv", lambda v: v.memset(prow[:, :], 0.0), sem="DV")
                    E.do("dv", lambda v: v.memset(colp[:, :], 0.0), sem="DV")
                    E.do("dv", lambda v: v.memset(r1x[:, :], 0.0), sem="DV")

                    def expert_reduce(src_col, dst_slice, op):
                        E.do("pe", lambda t, src_col=src_col: t.transpose(
                            p_tr[:, :], src_col, ident_f[:, :]),
                            sem="PE", waits=[("DV", E.cnt["DV"])])
                        E.do("dv", lambda v: v.tensor_copy(r1x[:, :], p_tr[:, :]),
                             sem="DV", waits=["PE"])
                        E.do("dv", lambda v, op=op: v.tensor_reduce(
                            prow[0:1, 0:8],
                            r1x[0:1, :].rearrange("o (e s) -> o e s", s=16),
                            AX.X, op), sem="DV", waits=["DV"])
                        E.do("pe", lambda t: t.transpose(
                            p_s2[:, 0:1], prow[0:1, 0:8], ident_f[0:1, 0:1]),
                            sem="PE", waits=[("DV", E.cnt["DV"])])
                        E.do("dv", lambda v, dst_slice=dst_slice: v.tensor_copy(
                            dst_slice, p_s2[:, 0:1]), sem="DV", waits=["PE"])

                    E.do("dv", lambda v: v.reduce_max(
                        colp[:, 0:1], S_all[:, :], AX.X), sem="DV",
                        waits=[sal, "DV"])
                    expert_reduce(colp[:, 0:1], sv8[:, 1:2], ALU.max)
                    E.do("dv", lambda v: v.tensor_reduce(
                        colp[:, 0:1], S_all[:, :], AX.X, ALU.min), sem="DV",
                        waits=[sal, ("PE", E.cnt["PE"])])
                    expert_reduce(colp[:, 0:1], sv8[:, 0:1], ALU.min)
                    E.do("dv", lambda v: v.tensor_copy(sv8[:, 4:5], sv8[:, 1:2]),
                         sem="DV", waits=["DV"])
                    E.do("dv", lambda v: v.tensor_scalar_mul(
                        sv8[:, 5:6], sv8[:, 4:5], -1.0), sem="DV", waits=["DV"])

                    for it in range(c.NIT):
                        E.do("dv", lambda v: v.tensor_add(
                            sv8[:, 2:3], sv8[:, 0:1], sv8[:, 1:2]), sem="DV",
                            waits=["DV", ("PE", E.cnt["PE"])])
                        E.do("dv", lambda v: v.tensor_scalar_mul(
                            sv8[:, 2:3], sv8[:, 2:3], 0.5), sem="DV", waits=["DV"])
                        E.do("pe", lambda t: t.matmul(
                            p_s1[:, 0:1], g8t[:, :], sv8[:, 2:3],
                            start=True, stop=True), sem="PE",
                            waits=[("DV", E.cnt["DV"])])
                        E.do("dv", lambda v: v.tensor_copy(
                            colp[:, 2:3], p_s1[:, 0:1]), sem="DV", waits=["PE"])
                        E.do("dv", lambda v: v.tensor_scalar(
                            scr[:, :], S_all[:, :], colp[:, 2:3], 0.0,
                            ALU.is_gt, ALU.add, accum_out=colp[:, 3:4]),
                            sem="DV", waits=["DV"])
                        E.do("pe", lambda t: t.matmul(
                            p_s2[:, 0:1], g8[:, :], colp[:, 3:4],
                            start=True, stop=True), sem="PE",
                            waits=[("DV", E.cnt["DV"])])
                        E.do("dv", lambda v: v.tensor_copy(
                            sv8[:, 8:9], p_s2[:, 0:1]), sem="DV", waits=["PE"])
                        E.do("dv", lambda v: v.tensor_single_scalar(
                            sv8[:, 3:4], sv8[:, 8:9], c.CAPT - 0.5, ALU.is_gt),
                            sem="DV", waits=["DV"])
                        E.do("dv", lambda v: v.tensor_sub(
                            sv8[:, 9:10], sv8[:, 2:3], sv8[:, 0:1]),
                            sem="DV", waits=["DV"])
                        E.do("dv", lambda v: v.tensor_sub(
                            sv8[:, 10:11], sv8[:, 1:2], sv8[:, 2:3]),
                            sem="DV", waits=["DV"])
                        E.do("dv", lambda v: v.tensor_mul(
                            sv8[:, 9:10], sv8[:, 9:10], sv8[:, 3:4]),
                            sem="DV", waits=["DV"])
                        E.do("dv", lambda v: v.tensor_mul(
                            sv8[:, 10:11], sv8[:, 10:11], sv8[:, 3:4]),
                            sem="DV", waits=["DV"])
                        E.do("dv", lambda v: v.tensor_add(
                            sv8[:, 0:1], sv8[:, 0:1], sv8[:, 9:10]),
                            sem="DV", waits=["DV"])
                        E.do("dv", lambda v: v.tensor_add(
                            sv8[:, 1:2], sv8[:, 2:3], sv8[:, 10:11]),
                            sem="DV", waits=["DV"])
                    E.do("dv", lambda v: v.tensor_add(
                        sv8[:, 2:3], sv8[:, 0:1], sv8[:, 1:2]), sem="DV",
                        waits=["DV", ("PE", E.cnt["PE"])])
                    E.do("dv", lambda v: v.tensor_scalar_mul(
                        sv8[:, 2:3], sv8[:, 2:3], 0.5), sem="DV", waits=["DV"])
                    E.do("pe", lambda t: t.matmul(
                        p_s1[:, 0:1], g8t[:, :], sv8[:, 2:3], start=True, stop=True),
                        sem="PE", waits=[("DV", E.cnt["DV"])])
                    E.do("pe", lambda t: t.matmul(
                        p_s1[:, 1:2], g8t[:, :], sv8[:, 5:6], start=True, stop=True),
                        sem="PE", waits=[("DV", E.cnt["DV"])])
                    E.do("dv", lambda v: v.tensor_copy(colp[:, 2:4], p_s1[:, 0:2]),
                         sem="DV", waits=["PE"])
                    E.do("dv", lambda v: v.tensor_scalar(
                        scr[:, :], S_all[:, :], colp[:, 2:3], None, ALU.is_gt),
                        sem="DV", waits=["DV"])
                    E.do("ac", lambda s: s.activation(
                        scr2[:, :], S_all[:, :], AF.Exp, bias=colp[:, 3:4]),
                        sem="AC", waits=[("DV", E.cnt["DV"])])
                    E.do("dv", lambda v: v.scalar_tensor_tensor(
                        scr2[:, :], scr2[:, :], 1.0, scr[:, :],
                        ALU.mult, ALU.mult, accum_out=colp[:, 0:1]),
                        sem="DV", waits=[("AC", E.cnt["AC"]), "DV"])
                    E.do("pe", lambda t: t.matmul(
                        p_s2[:, 0:1], g8[:, :], colp[:, 0:1], start=True, stop=True),
                        sem="PE", waits=[("DV", E.cnt["DV"])])
                    E.do("dv", lambda v: v.tensor_copy(sv8[:, 6:7], p_s2[:, 0:1]),
                         sem="DV", waits=["PE"])
                    E.do("dv", lambda v: v.reciprocal(sv8[:, 7:8], sv8[:, 6:7]),
                         sem="DV", waits=["DV"])
                    if "d_prm" in dbg_t:
                        E.do("gp", lambda g: g.dma_start(dbg_t["d_prm"], sv8[:, :]),
                             sem="STO", n=16, waits=[("DV", E.cnt["DV"])])

                    E.do("dv", lambda v: v.tensor_scalar(
                        mk[:, :], sT_loc[:, :], sv8[:, 2:3], None, ALU.is_gt),
                        sem="DV", waits=["DV"])
                    E.do("ac", lambda s: s.activation(
                        wl[:, :], sT_loc[:, :], AF.Exp, bias=sv8[:, 5:6]),
                        sem="AC", waits=[("DV", E.cnt["DV"])])
                    E.do("dv", lambda v: v.scalar_tensor_tensor(
                        wl[:, :], wl[:, :], sv8[:, 7:8], mk[:, :],
                        ALU.mult, ALU.mult), sem="DV",
                        waits=[("AC", E.cnt["AC"]), "DV"])
                    E.do("dv", lambda v: v.memset(on8[:, :], 1.0), sem="DV")
                    E.do("dv", lambda v: v.tensor_tensor_scan(
                        ic[:, :], mk[:, :], on8[:, :], 0.0, ALU.add, ALU.mult),
                        sem="DV", waits=["DV"])
                    E.do("dv", lambda v: v.tensor_single_scalar(
                        ok8[:, :], ic[:, :], c.CSLOT + 0.5, ALU.is_lt),
                        sem="DV", waits=["DV"])
                    E.do("dv", lambda v: v.tensor_mul(
                        ok8[:, :], ok8[:, :], mk[:, :]), sem="DV", waits=["DV"])
                    E.do("dv", lambda v: v.scalar_tensor_tensor(
                        sf8[:, :], ic[:, :], -386.0, ok8[:, :], ALU.add, ALU.mult),
                        sem="DV", waits=["DV"])
                    E.do("dv", lambda v: v.tensor_scalar_add(
                        sf8[:, :], sf8[:, :], 385.0), sem="DV", waits=["DV"])
                    E.do("gp", lambda g: g.iota(
                        tsramp[:, :], [[1, c.TQS], [0, 128]],
                        channel_multiplier=0,
                        allow_small_or_imprecise_dtypes=True), sem="GP")
                    E.do("dv", lambda v: v.scalar_tensor_tensor(
                        tsneg[:, :], ok8[:, :], -1.0, tsramp[:, :],
                        ALU.mult, ALU.mult), sem="DV",
                        waits=["DV", ("GP", E.cnt["GP"])])
                    E.do("dv", lambda v: v.tensor_add(
                        sf8[:, :], sf8[:, :], tsramp[:, :]), sem="DV",
                        waits=["DV"])
                    E.do("dv", lambda v: v.tensor_add(
                        sf8[:, :], sf8[:, :], tsneg[:, :]), sem="DV",
                        waits=["DV"])
                    E.do("dv", lambda v: v.scalar_tensor_tensor(
                        gf8[:, :], ic[:, :], -385.0, ok8[:, :], ALU.add, ALU.mult),
                        sem="DV", waits=["DV"])
                    E.do("dv", lambda v: v.tensor_scalar_add(
                        gf8[:, :], gf8[:, :], 384.0), sem="DV", waits=["DV"])
                    slots_ok = ("DV", E.cnt["DV"])
                    if "d_wl" in dbg_t:
                        E.do("gp", lambda g: g.dma_start(dbg_t["d_wl"], wl[:, :]),
                             sem="STO", n=16, waits=[slots_ok])
                    if "d_slot" in dbg_t:
                        E.do("gp", lambda g: g.dma_start(dbg_t["d_slot"], sf8[:, :]),
                             sem="STO", n=16, waits=[slots_ok])

                    for ts_ in range(c.TQS):
                        E.do("gp", lambda g, ts_=ts_: g.dma_start(
                            ln1_aug[ts_ * 128:(ts_ + 1) * 128, D + 16:c.AUGW],
                            zrow_b[:, 0:c.AUGW - D - 16]), sem="AUGC", n=16)
                    for ts_ in range(c.TQS):
                        jobs = [(sf8, "scat"), (gf8, "gath"), (sT_loc, "s"),
                                (wl, "w")]
                        for src, kindt in jobs:
                            E.do("pe", lambda t, src=src, ts_=ts_: t.transpose(
                                p_s1[:, :], src[:, ts_ * 128:(ts_ + 1) * 128],
                                ident_f[0:8, 0:8]),
                                sem="PE",
                                waits=[slots_ok, ("DV", E.cnt["DV"]),
                                       ("GP", E.cnt["GP"])])
                            if kindt == "scat":
                                E.do("dv", lambda v, ts_=ts_: v.tensor_copy(
                                    scatT[:, ts_ * 8:(ts_ + 1) * 8], p_s1[:, :]),
                                    sem="DV", waits=["PE"])
                            elif kindt == "gath":
                                E.do("dv", lambda v, ts_=ts_: v.tensor_copy(
                                    gathT[:, ts_ * 8:(ts_ + 1) * 8], p_s1[:, :]),
                                    sem="DV", waits=["PE"])
                            else:
                                colbase = 1024 if kindt == "s" else 1032
                                E.do("dv", lambda v: v.tensor_copy(
                                    trb[:, :], p_s1[:, :]), sem="DV", waits=["PE"])
                                E.do("gp", lambda g, ts_=ts_, colbase=colbase:
                                     g.dma_start(
                                         ln1_aug[ts_ * 128:(ts_ + 1) * 128,
                                                 colbase:colbase + 8], trb[:, :]),
                                     sem="AUGC", n=16, waits=[("DV", E.cnt["DV"])])
                                E.do("dv", lambda v: v.engine_nop(), sem="DV",
                                     waits=[("AUGC", E.cnt["AUGC"])])
                    trans_dv = ("DV", E.cnt["DV"])
                    aug_done = ("AUGC", E.cnt["AUGC"])

                    E.do("gp", lambda g: g.iota(
                        rampJ[:, :], [[1, 128]], channel_multiplier=0,
                        allow_small_or_imprecise_dtypes=True), sem="GP")
                    E.do("gp", lambda g: g.iota(
                        rhs2[:, :], [[0, 2 * c.TQS]], channel_multiplier=1,
                        allow_small_or_imprecise_dtypes=True), sem="GP")
                    gp_inv = ("GP", E.cnt["GP"])
                    for t_ in range(c.TQS):
                        E.do("dv", lambda v, t_=t_: v.memset(
                            rhs2[:, 2 * t_ + 1:2 * t_ + 2], float(t_)),
                            sem="DV", waits=[gp_inv, "DV"])
                    for chx in range(3):
                        E.do("dv", lambda v, chx=chx: v.memset(
                            c3[:, chx:chx + 1], -128.0 * chx), sem="DV",
                            waits=["DV"])
                    inv_pre = ("DV", E.cnt["DV"])
                    mt_hist = []
                    it = 0
                    for ei in range(c.E):
                        for chx in range(3):
                            for t_ in range(c.TQS):
                                bu = it % 4
                                wv = [trans_dv, inv_pre, "DV"]
                                if it >= 4:
                                    wv.append(("PE", mt_hist[it - 4]))
                                E.do("dv", lambda v, ei=ei, chx=chx, t_=t_, bu=bu:
                                     v.tensor_scalar(
                                         mt_bf[bu][:, :], rampJ[:, :],
                                         scatT[:, t_ * 8 + ei:t_ * 8 + ei + 1],
                                         c3[:, chx:chx + 1],
                                         ALU.subtract, ALU.is_equal),
                                     sem="DV", waits=wv)
                                kcol = (ei * 3 + chx) * 2
                                E.do("pe", lambda t, bu=bu, kcol=kcol, t_=t_:
                                     t.matmul(
                                         p_inv[:, kcol:kcol + 2],
                                         mt_bf[bu][:, :],
                                         rhs2[:, 2 * t_:2 * t_ + 2],
                                         start=(t_ == 0),
                                         stop=(t_ == c.TQS - 1)),
                                     sem="PE", waits=[("DV", E.cnt["DV"])])
                                mt_hist.append(E.cnt["PE"])
                                it += 1
                    pe_inv = ("PE", E.cnt["PE"])
                    E.do("dv", lambda v: v.tensor_copy(
                        pall[:, :], p_inv[:, :]), sem="DV",
                        waits=[pe_inv, "DV"])
                    for ei in range(c.E):
                        for chx in range(3):
                            kcol = (ei * 3 + chx) * 2
                            E.do("dv", lambda v, kcol=kcol, ei=ei, chx=chx:
                                 v.scalar_tensor_tensor(
                                     tok_all_f[:, ei * 3 + chx:
                                               ei * 3 + chx + 1],
                                     pall[:, kcol + 1:kcol + 2], 128.0,
                                     pall[:, kcol:kcol + 1],
                                     ALU.mult, ALU.add),
                                 sem="DV", waits=["DV"])
                    E.do("dv", lambda v: v.tensor_copy(
                        idx_all[:, :], tok_all_f[:, :]), sem="DV",
                        waits=["DV"])
                em.phase(b4, p4)
        SC_ALL = [(f"SC{i}", em.cnt[f"SC{i}"]) for i in range(c.E)]
        AUG_ALL = [("AUGC", em.cnt["AUGC"]), ("LN1S", em.cnt["LN1S"])]

        stC.close()
        st5 = ExitStack()
        with st5:
            e5 = st5.enter_context
            xgt = [e5(nc.sbuf_tensor(f"xgt{i}", [128, c.CSLOT], BF16))
                   for i in range(c.DCH)]
            xg_sb = [e5(nc.sbuf_tensor(f"xg_sb{i}", [128, c.AUGW], BF16))
                     for i in range(3)]
            idxg = e5(nc.sbuf_tensor("idxg", [128, 3], I32))
            w1s = [e5(nc.sbuf_tensor(f"w1s{i}", [128, c.HID], BF16))
                   for i in range(c.DCH)]
            w3s = [e5(nc.sbuf_tensor(f"w3s{i}", [128, c.HID], BF16))
                   for i in range(c.DCH)]
            w2s = [e5(nc.sbuf_tensor(f"w2s{i}", [128, D], BF16))
                   for i in range(3)]
            hts = [e5(nc.sbuf_tensor(f"hts{i}", [128, c.CSLOT], BF16))
                   for i in range(c.HT)]
            h1f = e5(nc.sbuf_tensor("h1f", [128, c.CSLOT], F32))
            h3f = e5(nc.sbuf_tensor("h3f", [128, c.CSLOT], F32))
            y_sb = [e5(nc.sbuf_tensor(f"y_sb{i}", [128, D], BF16)) for i in range(3)]
            wcol = e5(nc.sbuf_tensor("wcol", [128, 3], F32))
            p_h = [e5(nc.psum_tensor(f"p_h{i}", [128, c.CSLOT], F32))
                   for i in range(2)]
            p_y = [e5(nc.psum_tensor(f"p_y{i}", [128, 512], F32)) for i in range(6)]

            for _s in ("W1L", "W3L", "SYC", "XGG"):
                em.sem(_s)
            with nc.Block() as b5:
                def p5(E):
                    prev_gath = None
                    yprev = None
                    xgl_val = {}
                    w1_val = {}
                    w3_val = {}
                    w2_use = [0, 0, 0]
                    W2S = ("SYA", "SYB", "SYC")

                    def xg_loads(E, ee):
                        prevg = ("XGG", E.cnt["XGG"])
                        for sl in range(3):
                            E.do("gp", lambda g, ee=ee, sl=sl:
                                 g.indirect_dma_start(
                                     xg_sb[sl][:, :], None,
                                     ln1_aug,
                                     bass.IndirectOffsetOnAxis(
                                         idx_all[:, ee * 3 + sl:
                                                 ee * 3 + sl + 1], 0),
                                     element_offset=0),
                                 sem="XGG", n=16,
                                 waits=[prevg, ("PE", E.cnt["PE"]),
                                        ("STO", E.cnt["STO"])] + AUG_ALL)
                        xgl_val[ee] = ("XGG", E.cnt["XGG"])

                    def w1_load(E, ee, cs, pe_guard):
                        E.do("gp", lambda g, ee=ee, cs=cs: g.dma_start(
                            w1s[cs][:, :],
                            w1T_d[ee * D + cs * 128:ee * D + (cs + 1) * 128, :]),
                            sem="W1L", n=16,
                            waits=[(f"WC{ee}", 16 * 24), ("PE", pe_guard),
                                   ("STO", E.cnt["STO"])] + SC_ALL)
                        w1_val[ee] = ("W1L", E.cnt["W1L"])

                    def w3_load(E, ee, cs, pe_guard):
                        E.do("sy", lambda s, ee=ee, cs=cs: s.dma_start(
                            w3s[cs][:, :],
                            w3T_d[ee * D + cs * 128:ee * D + (cs + 1) * 128, :]),
                            sem="W3L", n=16,
                            waits=[(f"WC{ee}", 16 * 24), ("PE", pe_guard),
                                   ("STO", E.cnt["STO"])] + SC_ALL)
                        w3_val[ee] = ("W3L", E.cnt["W3L"])

                    for ei in range(c.E):
                        if ei == 0:
                            xg_loads(E, 0)
                            for cs in range(c.DCH):
                                w1_load(E, 0, cs, E.cnt["PE"])
                                w3_load(E, 0, cs, E.cnt["PE"])
                        xgl_ok = xgl_val[ei]
                        for sl in range(3):
                            for cs in range(c.DCH):
                                E.do("pe", lambda t, sl=sl, cs=cs: t.matmul(
                                    p_y[0][:, 0:128],
                                    xg_sb[sl][:, cs * 128:(cs + 1) * 128],
                                    ident_b[:, :], start=True, stop=True),
                                    sem="PE", waits=[xgl_ok, ("DV", E.cnt["DV"])])
                                E.do("dv", lambda v, sl=sl, cs=cs: v.tensor_copy(
                                    xgt[cs][:, sl * 128:(sl + 1) * 128],
                                    p_y[0][:, 0:128]),
                                    sem="DV", waits=["PE"])
                        xg_ok = ("DV", E.cnt["DV"])
                        wld1, wld3 = w1_val[ei], w3_val[ei]
                        for sl in range(3):
                            E.do("dv", lambda v, sl=sl, ei=ei: v.tensor_copy(
                                wcol[:, sl:sl + 1],
                                xg_sb[sl][:, 1032 + ei:1033 + ei]),
                                sem="DV", waits=[xgl_ok, "DV"])
                        wcol_ok = ("DV", E.cnt["DV"])

                        h_evac = [None, None]
                        for ht in range(c.HT):
                            pb = ht % 2
                            for wsl, hf, wldx in [(w1s, h1f, wld1),
                                                  (w3s, h3f, wld3)]:
                                mw = [xg_ok, wldx]
                                if h_evac[pb] is not None:
                                    mw.append(h_evac[pb])
                                for cs in range(c.DCH):
                                    E.do("pe", lambda t, pb=pb, wsl=wsl, cs=cs, ht=ht:
                                         t.matmul(p_h[pb][:, :],
                                                  wsl[cs][:, ht * 128:(ht + 1) * 128],
                                                  xgt[cs][:, :],
                                                  start=(cs == 0),
                                                  stop=(cs == c.DCH - 1)),
                                         sem="PE", waits=mw if cs == 0 else ())
                                E.do("dv", lambda v, pb=pb, hf=hf: v.tensor_copy(
                                    hf[:, :], p_h[pb][:, :]), sem="DV",
                                    waits=["PE", ("AC", E.cnt["AC"]), "DV"])
                                h_evac[pb] = ("DV", E.cnt["DV"])
                            E.do("ac", lambda s: s.activation(
                                h1f[:, :], h1f[:, :], AF.Silu), sem="AC",
                                waits=[("DV", E.cnt["DV"])])
                            E.do("dv", lambda v, ht=ht: v.tensor_mul(
                                hts[ht][:, :], h1f[:, :], h3f[:, :]),
                                sem="DV", waits=[("AC", E.cnt["AC"]), "DV"])
                            h_evac[pb] = ("DV", E.cnt["DV"])
                        ht_ok = ("DV", E.cnt["DV"])
                        pe_after_w13 = E.cnt["PE"]
                        for hc in range(c.HT):
                            wb = hc % 3
                            wsm = W2S[wb]
                            E.do("sy", lambda s, ei=ei, hc=hc, wb=wb: s.dma_start(
                                w2s[wb][:, :],
                                w2T_d[ei * c.HID + hc * 128:
                                      ei * c.HID + (hc + 1) * 128, :]),
                                sem=wsm, n=16,
                                waits=[(f"WC{ei}", 16 * 24), ("PE", w2_use[wb])])
                            wldv = (wsm, E.cnt[wsm])
                            if ei + 1 < c.E:
                                if hc < c.DCH:
                                    w1_load(E, ei + 1, hc, pe_after_w13)
                                if hc % 2 == 0:
                                    w3_load(E, ei + 1, hc // 2, pe_after_w13)
                            for ch in range(2):
                                for sl in range(3):
                                    mw = [ht_ok, wldv] if (ch == 0 and sl == 0) \
                                        else ()
                                    E.do("pe", lambda t, sl=sl, ch=ch, hc=hc, wb=wb:
                                         t.matmul(p_y[ch * 3 + sl][:, :],
                                                  hts[hc][:, sl * 128:(sl + 1) * 128],
                                                  w2s[wb][:, ch * 512:(ch + 1) * 512],
                                                  start=(hc == 0),
                                                  stop=(hc == c.HT - 1)),
                                         sem="PE", waits=mw)
                            w2_use[wb] = E.cnt["PE"]
                        if ei + 1 < c.E:
                            xg_loads(E, ei + 1)
                        for sl in range(3):
                            for ch in range(2):
                                E.do("dv", lambda v, sl=sl, ch=ch:
                                     v.tensor_scalar(
                                         y_sb[sl][:, ch * 512:(ch + 1) * 512],
                                         p_y[ch * 3 + sl][:, :],
                                         wcol[:, sl:sl + 1], None, ALU.mult),
                                     sem="DV",
                                     waits=["PE", wcol_ok] +
                                           ([yprev] if yprev is not None else []))
                            E.do("gp", lambda g, ei=ei, sl=sl: g.dma_start(
                                ysl_d[ei * c.XROW + sl * 128:
                                      ei * c.XROW + (sl + 1) * 128, :],
                                y_sb[sl][:, :]),
                                sem=f"YS{ei}", n=16,
                                waits=[("DV", E.cnt["DV"])])
                        yprev = (f"YS{ei}", E.cnt[f"YS{ei}"])
                        gw = [(f"YS{ei}", 16 * 3)]
                        if prev_gath is not None:
                            gw.append(prev_gath)
                        for ts_ in range(c.TQS):
                            E.do("gp", lambda g, ei=ei, ts_=ts_:
                                 g.indirect_dma_start(
                                     resid[ts_][:, :], None,
                                     ysl_d,
                                     bass.IndirectOffsetOnAxis(
                                         gathT[:, ts_ * 8 + ei:ts_ * 8 + ei + 1], 0),
                                     element_offset=ei * c.XROW * D,
                                     compute_op=ALU.add),
                                 sem="IND", n=16, waits=gw if ts_ == 0 else ())
                        prev_gath = ("IND", E.cnt["IND"])
                em.phase(b5, p5)

            with nc.Block() as b6:
                def p6(E):
                    E.wait("IND") if False else None
                    for tt in range(c.TQS):
                        if tt == 0:
                            E.do("dv", lambda v: v.engine_nop(), sem="DV",
                                 waits=[("IND", E.cnt["IND"])])
                        ln_emit(E, resid[tt], None)
                        E.do("gp", lambda g, tt=tt: g.dma_start(
                            y_out[tt * 128:(tt + 1) * 128, :], resid[tt][:, :]),
                            sem="STO", n=16, waits=[("DV", E.cnt["DV"])])
                    E.do("gp", lambda g: g.engine_nop(), sem=None,
                         waits=[("STO", E.cnt["STO"])])
                em.phase(b6, p6)
        stB.close()

    return nc


def em_snapshot(E, sem):
    return E.cnt[sem]


_CACHE = {}


def make_in_maps(cfg, L0_attn_x, L0_attn_Wq, L0_attn_bq, L0_attn_Wk, L0_attn_bk,
                 L0_attn_Wv, L0_attn_bv, L0_attn_Wo, L0_attn_bo,
                 L0_ffn_Wg, L0_ffn_W1, L0_ffn_W2, L0_ffn_W3):
    x = np.ascontiguousarray(np.asarray(L0_attn_x, dtype=np.float32))
    B, S, D = x.shape
    c = cfg
    f32 = lambda a: np.ascontiguousarray(np.asarray(a, dtype=np.float32))
    Wq, Wk, Wv, Wo = map(f32, (L0_attn_Wq, L0_attn_Wk, L0_attn_Wv, L0_attn_Wo))
    bq, bk, bv, bo = map(f32, (L0_attn_bq, L0_attn_bk, L0_attn_bv, L0_attn_bo))
    Wg, W1, W2, W3 = map(f32, (L0_ffn_Wg, L0_ffn_W1, L0_ffn_W2, L0_ffn_W3))
    Wg16 = np.zeros((16, D), np.float32)
    Wg16[:c.E] = Wg
    WgT = np.ascontiguousarray(Wg16.T)
    WoT = np.ascontiguousarray(Wo.T)
    W1T = np.ascontiguousarray(W1.transpose(0, 2, 1)).reshape(c.E * D, c.HID)
    W3T = np.ascontiguousarray(W3.transpose(0, 2, 1)).reshape(c.E * D, c.HID)
    W2T = np.ascontiguousarray(W2.transpose(0, 2, 1)).reshape(c.E * c.HID, D)
    xT = [np.ascontiguousarray(x[b].T) for b in range(B)]

    in_maps = []
    for core in range(c.CORES):
        b = core // c.GC
        j = core % c.GC
        hsl = slice(j * c.DL, (j + 1) * c.DL)
        idx = (np.arange(D, dtype=np.int64) * S + j * c.TQ).astype(np.int32)
        idx = idx.reshape(c.DCH, 128).T.copy()
        in_maps.append(dict(
            xT_h=xT[b],
            x_q=x[b, j * c.TQ:(j + 1) * c.TQ],
            wqT_i=np.ascontiguousarray(Wq[hsl].T),
            wkT_i=np.ascontiguousarray(Wk[hsl].T),
            wvT_i=np.ascontiguousarray(Wv[hsl].T),
            bq_l=bq[hsl].reshape(2, 128), bk_l=bk[hsl].reshape(2, 128),
            bv_l=bv[hsl].reshape(2, 128), bv_r=bv[hsl].reshape(1, c.DL),
            woT_i=WoT, bo_r=bo.reshape(1, D),
            wgT_i=WgT, w1T_i=W1T, w3T_i=W3T, w2T_i=W2T,
            idxq=idx,
        ))
    return in_maps


def kernel(**inputs):
    x = np.asarray(inputs["L0_attn_x"])
    B, S, D = x.shape
    cfg = Cfg(S=S)
    c = cfg
    key = (S,)
    if key not in _CACHE:
        _CACHE[key] = build(cfg)
    nc = _CACHE[key]
    in_maps = make_in_maps(cfg, **inputs)
    res = run_bass_kernel_spmd(nc, in_maps, core_ids=list(range(c.CORES)))
    out = np.empty((B, S, D), np.float32)
    for core in range(c.CORES):
        b = core // c.GC
        j = core % c.GC
        out[b, j * c.TQ:(j + 1) * c.TQ] = np.asarray(res.results[core]["y_out"])
    return out


# version 16 (final)
# speedup vs baseline: 1.3188x; 1.0314x over previous
import numpy as np
import ml_dtypes
from contextlib import ExitStack

import concourse.bass as bass
import concourse.mybir as mybir
from concourse.bass_utils import run_bass_kernel_spmd

F32 = mybir.dt.float32
BF16 = mybir.dt.bfloat16
I32 = mybir.dt.int32
AF = mybir.ActivationFunctionType
ALU = mybir.AluOpType
AX = mybir.AxisListType


class Cfg:
    def __init__(self, S=8192):
        self.B = 2; self.S = S; self.D = 1024; self.H = 16; self.DH = 64
        self.E = 8; self.HID = 2048; self.CORES = 8
        self.GC = self.CORES // self.B
        self.HPC = self.H // self.GC
        self.DL = self.HPC * self.DH
        self.TQ = S // self.GC
        self.SW = 512
        self.NS = S // self.SW
        self.NKB = S // 128
        self.CAPT = self.B * S // self.E
        self.CSLOT = 384
        self.XROW = self.CSLOT + 2
        self.AUGW = self.D + 128
        self.NIT = 28
        self.DCH = self.D // 128
        self.TQS = self.TQ // 128
        self.XRT = self.CSLOT + 1 + self.TQS + 1
        self.GST = self.TQ // 512
        self.HT = self.HID // 128


class Emit:
    ENGS = ("gp", "sy", "pe", "ac", "dv")
    BMAP = {"gp": "gpsimd", "sy": "sync", "pe": "tensor", "ac": "scalar",
            "dv": "vector"}

    def __init__(self, nc):
        self.nc = nc
        self.cnt = {}
        self.sems = {}
        self.cur = None
        self.eng = None
        self._eobj = dict(gp=nc.gpsimd, sy=nc.sync, pe=nc.tensor,
                          ac=nc.scalar, dv=nc.vector)
        self._lastwait = {}

    def sem(self, name):
        if name not in self.sems:
            self.sems[name] = self.nc.alloc_semaphore(name)
            self.cnt[name] = 0
        return self.sems[name]

    def phase(self, block, fn):
        base = dict(self.cnt)
        final = None
        for name in self.ENGS:
            self.cnt = dict(base)
            self.cur = name
            self.eng = self._eobj[name]
            self._lastwait = {}
            getattr(block, self.BMAP[name])(lambda _e: fn(self))
            final = self.cnt
        self.cnt = final

    def wait(self, sname, val=None):
        if val is None:
            val = self.cnt[sname]
        if val <= 0:
            return
        if self._lastwait.get(sname, -1) >= val:
            return
        self._lastwait[sname] = val
        self.eng.wait_ge(self.sem(sname), val)

    SERIAL = False
    NO_XW = False

    def do(self, engname, fn, sem=None, n=1, waits=()):
        if sem is not None:
            self.sem(sem)
        if self.cur == engname:
            if Emit.SERIAL:
                for s_ in ("PE", "DV", "AC", "GP"):
                    if s_ in self.cnt:
                        self.wait(s_, self.cnt[s_])
            for w in waits:
                if isinstance(w, tuple):
                    self.wait(w[0], w[1])
                else:
                    self.wait(w)
            inst = fn(self.eng)
            if sem is not None and inst is not None:
                inst.then_inc(self.sems[sem], n)
        if sem is not None:
            self.cnt[sem] += n
        return (sem, self.cnt[sem]) if sem is not None else None


def build(cfg: Cfg, dbg=()):
    nc = bass.Bass("TRN2", target_bir_lowering=False)
    c = cfg
    S, D, TQ, DL, HPC, DH = c.S, c.D, c.TQ, c.DL, c.HPC, c.DH
    DHA = DH + 1
    NKB, NS = c.NKB, c.NS

    def T(n, shp, dt=F32, kind="ExternalInput"):
        return nc.dram_tensor(n, shp, dt, kind=kind).ap()

    xT_h = T("xT_h", [D, S])
    x_q = T("x_q", [TQ, D])
    wqT_i = T("wqT_i", [D, DL]); wkT_i = T("wkT_i", [D, DL]); wvT_i = T("wvT_i", [D, DL])
    bq_l = T("bq_l", [2, 128]); bk_l = T("bk_l", [2, 128]); bv_l = T("bv_l", [2, 128])
    bv_r = T("bv_r", [1, DL])
    woT_i = T("woT_i", [D, D]); bo_r = T("bo_r", [1, D])
    wgT_i = T("wgT_i", [D, 16])
    w1T_i = T("w1T_i", [c.E * D, c.HID]); w3T_i = T("w3T_i", [c.E * D, c.HID])
    w2T_i = T("w2T_i", [c.E * c.HID, D])
    idxq = T("idxq", [128, 8], I32)
    y_out = T("y_out", [TQ, D], F32, kind="ExternalOutput")

    xT_d = T("xT_d", [D, S], BF16, "Internal")
    wqT_bf = T("wqT_bf", [D, DL], BF16, "Internal")
    wkT_bf = T("wkT_bf", [D, DL], BF16, "Internal")
    wvT_bf = T("wvT_bf", [D, DL], BF16, "Internal")
    woT_bf = T("woT_bf", [D, D], BF16, "Internal")
    wgT_bf = T("wgT_bf", [D, 16], BF16, "Internal")
    w1T_d = T("w1T_d", [c.E * D, c.HID], BF16, "Internal")
    w3T_d = T("w3T_d", [c.E * D, c.HID], BF16, "Internal")
    w2T_d = T("w2T_d", [c.E * c.HID, D], BF16, "Internal")
    khead_d = T("khead_d", [HPC * 64, S], BF16, "Internal")
    qhat_d = T("qhat_d", [HPC * DHA, S], BF16, "Internal")
    attnT_d = T("attnT_d", [DL, S], BF16, "Internal")
    attnT_full = T("attnT_full", [D, S], BF16, "Internal")
    l_d = T("l_d", [1, S], F32, "Internal")
    rl_d = T("rl_d", [1, S], F32, "Internal")
    sT_dr = T("sT_dr", [c.E, TQ], F32, "Internal")
    s_full = T("s_full", [c.CORES * c.E, TQ], F32, "Internal")
    s_re = T("s_re", [128, c.CORES * TQ // 16], F32, "Internal")
    ln1_aug = T("ln1_aug", [TQ, c.AUGW], BF16, "Internal")
    tokslot_d = T("tokslot_d", [c.E * c.XRT, 1], I32, "Internal")
    xg_d = T("xg_d", [c.E * c.XROW, c.AUGW], BF16, "Internal")
    ysl_d = T("ysl_d", [c.E * c.XROW, D], BF16, "Internal")

    dbg_t = {}
    for name, shp, dt in [("d_attnT", [DL, S], BF16), ("d_ln1", [TQ, D], F32),
                          ("d_sT", [c.E, TQ], F32), ("d_prm", [8, 16], F32),
                          ("d_wl", [c.E, TQ], F32), ("d_slot", [c.E, TQ], F32),
                          ("d_qd", [HPC * DHA, S], BF16),
                          ("d_kd", [HPC * 64, S], BF16),
                          ("d_vd", [DL, S], BF16),
                          ("d_vtok", [128, (S // 128) * DL], BF16)]:
        if name in dbg:
            dbg_t[name] = T(name, shp, dt, "ExternalOutput")

    st = ExitStack()
    with st:
        e = st.enter_context
        sb = lambda n, shp, dt: e(nc.sbuf_tensor(n, shp, dt))

        ones_rf = sb("ones_rf", [1, 128], F32)
        ones_cb = sb("ones_cb", [128, 1], BF16)
        ident_b = sb("ident_b", [128, 128], BF16)
        ident_f = sb("ident_f", [128, 128], F32)
        wmask = sb("wmask", [128, 896], BF16)
        g8 = sb("g8", [128, 8], F32)
        g8t = sb("g8t", [8, 128], F32)
        bqT = sb("bqT", [128, 2], F32)
        bkT = sb("bkT", [128, 2], F32)
        bvT = sb("bvT", [128, 2], F32)
        bo_row = sb("bo_row", [1, D], F32)
        idxq_sb = sb("idxq_sb", [128, c.DCH], I32)
        zrow_b = sb("zrow_b", [128, c.AUGW], BF16)
        zrow_f = sb("zrow_f", [128, D], F32)
        tokrow = sb("tokrow", [128, c.TQS], I32)
        zi32 = sb("zi32", [8, c.XRT], I32)
        stats = sb("stats", [1, 64], F32)
        ln_ms = sb("ln_ms", [128, 6], F32)
        eps_c = sb("eps_c", [128, 1], F32)

        em = Emit(nc)
        for s_ in ["LDG", "XBR", "LDS", "PE", "DV", "AC", "GP", "STO", "IND",
                   "CC1", "CC2", "ATS", "LRL", "SYA", "SYB", "XQG", "XMW",
                   "PRJ", "KH0", "KH1", "CX", "CW", "COG", "BIA", "LN1S",
                   "STD", "SRE", "AGL", "XGZ", "AUGC", "XGL", "WOT", "XB0", "XB1", "XS0", "XS1", "PR0", "PR1", "PRM", "XT0", "XT1", "LDI", "LS0", "LS1", "RL0", "RL1", "AT0", "AT1"]:
            em.sem(s_)
        for i in range(c.E):
            em.sem(f"XW{i}"); em.sem(f"SC{i}"); em.sem(f"YS{i}"); em.sem(f"WC{i}")

        def ln_emit(E, slab, out_bf):
            E.do("dv", lambda v: v.reduce_sum(ln_ms[:, 0:1], slab[:, :], AX.X),
                 sem="DV", waits=[("AC", E.cnt["AC"]), ("DV", E.cnt["DV"])])
            E.do("dv", lambda v: v.tensor_scalar_mul(
                ln_ms[:, 1:2], ln_ms[:, 0:1], -1.0 / D), sem="DV", waits=["DV"])
            E.do("ac", lambda s: s.activation(
                sq_sc[:, :], slab[:, :], AF.Square, bias=ln_ms[:, 1:2],
                accum_out=ln_ms[:, 2:3]), sem="AC", waits=[("DV", E.cnt["DV"])])
            E.do("ac", lambda s: s.activation(
                ln_ms[:, 3:4], ln_ms[:, 2:3], AF.Sqrt, bias=eps_c[:, :],
                scale=1.0 / D), sem="AC", waits=["AC"])
            E.do("dv", lambda v: v.reciprocal(ln_ms[:, 4:5], ln_ms[:, 3:4]),
                 sem="DV", waits=[("AC", E.cnt["AC"])])
            E.do("dv", lambda v: v.tensor_mul(
                ln_ms[:, 5:6], ln_ms[:, 4:5], ln_ms[:, 4:5]), sem="DV", waits=["DV"])
            E.do("dv", lambda v: v.tensor_scalar(
                ln_ms[:, 0:1], ln_ms[:, 2:3], 1.0 / D, 1e-5, ALU.mult, ALU.add),
                sem="DV", waits=["DV"])
            E.do("dv", lambda v: v.tensor_mul(
                ln_ms[:, 5:6], ln_ms[:, 5:6], ln_ms[:, 0:1]), sem="DV", waits=["DV"])
            E.do("dv", lambda v: v.tensor_scalar(
                ln_ms[:, 5:6], ln_ms[:, 5:6], -0.5, 1.5, ALU.mult, ALU.add),
                sem="DV", waits=["DV"])
            E.do("dv", lambda v: v.tensor_mul(
                ln_ms[:, 4:5], ln_ms[:, 4:5], ln_ms[:, 5:6]), sem="DV", waits=["DV"])
            E.do("dv", lambda v: v.tensor_scalar(
                slab[:, :], slab[:, :], ln_ms[:, 1:2], ln_ms[:, 4:5],
                ALU.add, ALU.mult), sem="DV", waits=["DV"])
            if out_bf is not None:
                E.do("dv", lambda v: v.tensor_copy(out_bf[:, :], slab[:, :]),
                     sem="DV", waits=["DV"])

        with nc.Block() as b0:
            def p0(E):
                for t_ in (ones_rf, ones_cb, ident_b, ident_f, wmask, g8, g8t):
                    E.do("dv", lambda v, t_=t_: v.memset(t_[:, :], 1.0), sem="DV")
                E.do("dv", lambda v: v.memset(zrow_b[:, :], 0.0), sem="DV")
                E.do("dv", lambda v: v.memset(zrow_f[:, :], 0.0), sem="DV")
                E.do("dv", lambda v: v.memset(eps_c[:, :], 1e-5), sem="DV")
                gsel = [(ident_b, [[1, 128]], ALU.is_equal, 0, -1),
                        (ident_f, [[1, 128]], ALU.is_equal, 0, -1),
                        (wmask, [[1, 896]], ALU.is_ge, -384, -1),
                        (g8, [[-16, 8]], ALU.is_ge, 0, 1),
                        (g8, [[16, 8]], ALU.is_ge, 15, -1),
                        (g8t, [[1, 128]], ALU.is_ge, 0, -16),
                        (g8t, [[-1, 128]], ALU.is_ge, 15, 16)]
                for t_, pat, op, base, cm in gsel:
                    E.do("gp", lambda g, t_=t_, pat=pat, op=op, base=base, cm=cm:
                         g.affine_select(t_[:, :], t_[:, :], pat, op, 0.0,
                                         base=base, channel_multiplier=cm),
                         sem="GP", waits=["DV", "GP"])
                E.do("sy", lambda s: s.dma_start(bo_row[:, :], bo_r), sem="LDS", n=16)
                E.do("sy", lambda s: s.dma_start(
                    idxq_sb[:, :], idxq), sem="LDI", n=16)
                E.do("gp", lambda g: g.dma_start(xT_d, xT_h), sem="CX", n=16)
                for dst, src in [(wqT_bf, wqT_i), (wkT_bf, wkT_i), (wvT_bf, wvT_i)]:
                    E.do("gp", lambda g, dst=dst, src=src: g.dma_start(dst, src),
                         sem="CW", n=16)
                for dst, src in [(woT_bf, woT_i), (wgT_bf, wgT_i)]:
                    E.do("gp", lambda g, dst=dst, src=src: g.dma_start(dst, src),
                         sem="COG", n=16)
                E.do("gp", lambda g: g.memset(zi32[:, :], 0), sem="GP",
                     waits=["GP"])
                zw = [("DV", E.cnt["DV"]), ("GP", E.cnt["GP"])]
                for ei in range(c.E):
                    for rr in range(0, c.XROW, 128):
                        n_ = min(128, c.XROW - rr)
                        E.do("sy", lambda s, ei=ei, rr=rr, n_=n_: s.dma_start(
                            ysl_d[ei * c.XROW + rr:ei * c.XROW + rr + n_, :],
                            zrow_b[0:n_, 0:D]), sem="XGZ", n=16, waits=zw)

            em.phase(b0, p0)

        stA = ExitStack()
        with stA:
            eA = stA.enter_context
            v_tok = eA(nc.sbuf_tensor("v_tok", [128, (S // 128) * DL], BF16))
            st1 = ExitStack()
            with st1:
                e1 = st1.enter_context
                wqT = [e1(nc.sbuf_tensor(f"wqT{i}", [128, DL], BF16))
                       for i in range(c.DCH)]
                wkT = [e1(nc.sbuf_tensor(f"wkT{i}", [128, DL], BF16))
                       for i in range(c.DCH)]
                wvT = [e1(nc.sbuf_tensor(f"wvT{i}", [128, DL], BF16))
                       for i in range(c.DCH)]
                xts = [[e1(nc.sbuf_tensor(f"xts{i}_{j}", [128, 512], BF16))
                        for j in range(c.DCH)] for i in range(2)]
                prj = [e1(nc.sbuf_tensor(f"prj{i}", [128, 512], BF16))
                       for i in range(2)]
                btmp = [e1(nc.sbuf_tensor(f"btmp{i}", [2, 128], F32))
                        for i in range(3)]
                bv_bc = e1(nc.sbuf_tensor("bv_bc", [128, DL], F32))
                bvr_s = e1(nc.sbuf_tensor("bvr_s", [1, DL], F32))
                sqs = e1(nc.sbuf_tensor("sqs", [128, 512], BF16))
                karr = e1(nc.sbuf_tensor("karr", [1, HPC * NS], F32))
                qn_s = e1(nc.sbuf_tensor("qn_s", [1, 512], F32))
                mh_s = e1(nc.sbuf_tensor("mh_s", [1, 512], BF16))
                p_pr = [e1(nc.psum_tensor(f"p_pr{i}", [128, 512], F32))
                        for i in range(2)]
                p_sm = [e1(nc.psum_tensor(f"p_sm{i}", [1, 512], F32))
                        for i in range(2)]
                p_bT = e1(nc.psum_tensor("p_bT", [128, 2], F32))
                p_bv = [e1(nc.psum_tensor(f"p_bv{i}", [128, 256], F32))
                        for i in range(2)]

                with nc.Block() as b1:
                    def p1(E):
                        for i, bin_ in enumerate([bq_l, bk_l, bv_l]):
                            E.do("gp", lambda g, bin_=bin_, i=i: g.dma_start(
                                btmp[i][:, :], bin_), sem="BIA", n=16)
                        E.do("gp", lambda g: g.dma_start(bvr_s[:, :], bv_r),
                             sem="BIA", n=16)
                        for i, (bt, scl) in enumerate([(bqT, 0.125), (bkT, 1.0),
                                                       (bvT, 1.0)]):
                            E.do("pe", lambda t, i=i: t.transpose(
                                p_bT[:, :], btmp[i][:, :], ident_f[0:2, 0:2]),
                                sem="PE", waits=[("BIA", 64), ("GP", E.cnt["GP"]),
                                                 ("DV", E.cnt["DV"])])
                            E.do("dv", lambda v, bt=bt, scl=scl: v.tensor_scalar_mul(
                                bt[:, :], p_bT[:, :], scl), sem="DV", waits=["PE"])
                        for hf in range(2):
                            E.do("pe", lambda t, hf=hf: t.matmul(
                                p_bv[hf][:, 0:128], ones_rf[:, :],
                                bvr_s[0:1, hf * 128:(hf + 1) * 128],
                                start=True, stop=True),
                                sem="PE", waits=[("BIA", 64), ("DV", E.cnt["DV"])])
                            E.do("dv", lambda v, hf=hf: v.tensor_copy(
                                bv_bc[:, hf * 128:(hf + 1) * 128],
                                p_bv[hf][:, 0:128]), sem="DV", waits=["PE"])
                        bvbc_ok = ("DV", E.cnt["DV"])
                        for wbf, wt in [(wqT_bf, wqT), (wkT_bf, wkT),
                                        (wvT_bf, wvT)]:
                            for cc in range(c.DCH):
                                E.do("sy", lambda s, wbf=wbf, wt=wt, cc=cc:
                                     s.dma_start(wt[cc][:, :],
                                                 wbf[cc * 128:(cc + 1) * 128, :]),
                                     sem="SYA", n=16, waits=[("CW", 48)])
                        wT_ok = ("SYA", E.cnt["SYA"])

                        evac_ps = [None, None]
                        prj_free = [None, None]
                        sqs_free = None
                        psm_free = [None, None]
                        pbv_free = [bvbc_ok, bvbc_ok]
                        for wi, kind in enumerate(["k", "q", "v"]):
                            wt = {"k": wkT, "q": wqT, "v": wvT}[kind]
                            bt = {"k": bkT, "q": bqT, "v": bvT}[kind]
                            scl = 0.125 if kind == "q" else 1.0
                            pe_round = {}
                            for tc in range(NS):
                                bsel = tc % 2
                                ldsem = "SYB" if bsel else "SYA"
                                wld = [wT_ok, ("CX", 16)]
                                if tc >= 2:
                                    wld.append(("PE", pe_round[tc - 2]))
                                elif wi > 0:
                                    wld.append(("PE", E.cnt["PE"]))
                                for cc in range(c.DCH):
                                    E.do("sy", lambda s, tc=tc, cc=cc, bsel=bsel:
                                         s.dma_start(
                                             xts[bsel][cc][:, :],
                                             xT_d[cc * 128:(cc + 1) * 128,
                                                  tc * 512:(tc + 1) * 512]),
                                         sem=ldsem, n=16, waits=wld)
                                ldval = (ldsem, E.cnt[ldsem])
                                if kind == "v":
                                    for tb in range(4):
                                        pvi = tb % 2
                                        pv = p_bv[pvi]
                                        mm_w = [ldval, pbv_free[pvi]]
                                        for cc in range(c.DCH):
                                            E.do("pe", lambda t, pv=pv, cc=cc,
                                                 tb=tb, bsel=bsel:
                                                 t.matmul(
                                                     pv[:, :],
                                                     xts[bsel][cc][:, tb * 128:
                                                                   (tb + 1) * 128],
                                                     wvT[cc][:, :],
                                                     start=(cc == 0),
                                                     stop=(cc == c.DCH - 1)),
                                                 sem="PE",
                                                 waits=mm_w if cc == 0 else ())
                                        tg = tc * 4 + tb
                                        E.do("dv", lambda v, pv=pv, tg=tg:
                                             v.scalar_tensor_tensor(
                                                 v_tok[:, tg * DL:(tg + 1) * DL],
                                                 pv[:, :], 0.0, bv_bc[:, :],
                                                 ALU.add, ALU.add),
                                             sem="DV", waits=["PE", bvbc_ok])
                                        pbv_free[pvi] = ("DV", E.cnt["DV"])
                                    pe_round[tc] = E.cnt["PE"]
                                    continue
                                for dt in range(DL // 128):
                                    pidx = (tc * 2 + dt) % 2
                                    pp = p_pr[pidx]
                                    mm_w = [ldval]
                                    if evac_ps[pidx] is not None:
                                        mm_w.append(evac_ps[pidx])
                                    for cc in range(c.DCH):
                                        E.do("pe", lambda t, pp=pp, wt=wt, cc=cc,
                                             dt=dt, bsel=bsel:
                                             t.matmul(
                                                 pp[:, :],
                                                 wt[cc][:, dt * 128:(dt + 1) * 128],
                                                 xts[bsel][cc][:, :],
                                                 start=(cc == 0),
                                                 stop=(cc == c.DCH - 1)),
                                             sem="PE", waits=mm_w if cc == 0 else ())
                                    sidx = (tc * 2 + dt) % 2
                                    sg = prj[sidx]
                                    ew = ["PE"]
                                    if prj_free[sidx] is not None:
                                        ew.append(prj_free[sidx])
                                    ev = E.do("ac", lambda s, pp=pp, sg=sg, bt=bt,
                                              dt=dt, scl=scl:
                                              s.activation(sg[:, :], pp[:, :],
                                                           AF.Identity,
                                                           bias=bt[:, dt:dt + 1],
                                                           scale=scl),
                                              sem="AC", waits=ew)
                                    evac_ps[pidx] = ev
                                    if kind == "k":
                                        dmas = [(khead_d[dt * 128:(dt + 1) * 128,
                                                         tc * 512:(tc + 1) * 512],
                                                 sg[:, :])]
                                    else:
                                        dmas = []
                                        for hh in range(2):
                                            h = dt * 2 + hh
                                            dmas.append(
                                                (qhat_d[h * DHA:h * DHA + 64,
                                                        tc * 512:(tc + 1) * 512],
                                                 sg[hh * 64:(hh + 1) * 64, :]))
                                    prs = "PR0" if sidx == 0 else "PR1"
                                    for dst_, src_ in dmas:
                                        E.do("gp", lambda g, dst_=dst_, src_=src_:
                                             g.dma_start(dst_, src_),
                                             sem=prs, n=16, waits=[ev])
                                    prj_free[sidx] = (prs, E.cnt[prs])
                                    sqw = [ev]
                                    if sqs_free is not None:
                                        sqw.append(sqs_free)
                                    E.do("ac", lambda s, sg=sg: s.activation(
                                        sqs[:, :], sg[:, :], AF.Square),
                                        sem="AC", waits=sqw)
                                    sq_ok = ("AC", E.cnt["AC"])
                                    for hh in range(2):
                                        h = dt * 2 + hh
                                        pb2 = hh
                                        mmw = [sq_ok]
                                        if psm_free[pb2] is not None:
                                            mmw.append(psm_free[pb2])
                                        E.do("pe", lambda t, hh=hh, pb2=pb2:
                                             t.matmul(
                                                 p_sm[pb2][:, :],
                                                 ones_cb[hh * 64:hh * 64 + 64, :],
                                                 sqs[hh * 64:(hh + 1) * 64, :],
                                                 start=True, stop=True),
                                             sem="PE", waits=mmw)
                                        if kind == "k":
                                            E.do("dv", lambda v, h=h, tc=tc, pb2=pb2:
                                                 v.reduce_max(
                                                     karr[0:1, h * NS + tc:
                                                          h * NS + tc + 1],
                                                     p_sm[pb2][:, :], AX.X),
                                                 sem="DV", waits=["PE"])
                                            psm_free[pb2] = ("DV", E.cnt["DV"])
                                        else:
                                            E.do("ac", lambda s, pb2=pb2:
                                                 s.activation(qn_s[:, :],
                                                              p_sm[pb2][:, :],
                                                              AF.Sqrt),
                                                 sem="AC",
                                                 waits=["PE", ("DV", E.cnt["DV"])])
                                            E.do("dv", lambda v, h=h:
                                                 v.tensor_scalar(
                                                     mh_s[:, :], qn_s[:, :],
                                                     stats[0:1, h:h + 1], None,
                                                     ALU.mult),
                                                 sem="DV",
                                                 waits=[("AC", E.cnt["AC"]),
                                                        ("PRM", E.cnt["PRM"])])
                                            E.do("gp", lambda g, h=h, tc=tc:
                                                 g.dma_start(
                                                     qhat_d[h * DHA + 64:
                                                            h * DHA + 65,
                                                            tc * 512:(tc + 1) * 512],
                                                     mh_s[:, :]),
                                                 sem="PRM", n=16,
                                                 waits=[("DV", E.cnt["DV"]),
                                                        ("PRM", E.cnt["PRM"])])
                                            psm_free[pb2] = ("AC", E.cnt["AC"])
                                    sqs_free = ("PE", E.cnt["PE"])
                                pe_round[tc] = E.cnt["PE"]
                            if kind == "k":
                                for h in range(HPC):
                                    E.do("dv", lambda v, h=h: v.reduce_max(
                                        stats[0:1, 32 + h:33 + h],
                                        karr[0:1, h * NS:(h + 1) * NS], AX.X),
                                        sem="DV", waits=["DV"])
                                    E.do("ac", lambda s, h=h: s.activation(
                                        stats[0:1, 16 + h:17 + h],
                                        stats[0:1, 32 + h:33 + h], AF.Sqrt),
                                        sem="AC", waits=[("DV", E.cnt["DV"]),
                                                         ("AC", E.cnt["AC"])])
                                    E.do("dv", lambda v, h=h: v.tensor_scalar_mul(
                                        stats[0:1, h:h + 1],
                                        stats[0:1, 16 + h:17 + h], -1.0),
                                        sem="DV", waits=[("AC", E.cnt["AC"])])
                    em.phase(b1, p1)
            PRJ_ALL = [("PR0", em.cnt["PR0"]), ("PR1", em.cnt["PR1"]),
                       ("PRM", em.cnt["PRM"])]

            st2 = ExitStack()
            with st2:
                e2 = st2.enter_context
                kh_buf = [e2(nc.sbuf_tensor(f"khb{i}", [DHA, S], BF16))
                          for i in range(2)]
                qh_buf = [e2(nc.sbuf_tensor(f"qhb{i}", [DHA, S], BF16))
                          for i in range(2)]
                v_aug = [e2(nc.sbuf_tensor(f"vaug{i}", [128, NKB * DHA], BF16))
                         for i in range(2)]
                pt = [e2(nc.sbuf_tensor(f"pt{i}", [128, 1024], BF16))
                      for i in range(4)]
                raw_h = e2(nc.sbuf_tensor("raw_h", [64, S], F32))
                rlb = [e2(nc.sbuf_tensor(f"rlb{i}", [64, 512], F32))
                       for i in range(2)]
                ltile = e2(nc.sbuf_tensor("ltile", [NS, c.SW], F32))
                l_st = [e2(nc.sbuf_tensor(f"l_st{i}", [1, 512], F32))
                        for i in range(2)]
                rl_st = [e2(nc.sbuf_tensor(f"rlst{i}", [1, 512], F32))
                         for i in range(2)]
                att_st = [e2(nc.sbuf_tensor(f"attst{i}", [64, 512], BF16))
                          for i in range(2)]
                s_ps = [e2(nc.psum_tensor(f"s_ps{i}", [128, 1024], F32))
                        for i in range(3)]
                ot_ps = [e2(nc.psum_tensor(f"ot_ps{i}", [DHA, 512], F32))
                         for i in range(2)]


                with nc.Block() as b2:
                    def p2(E):
                        for i in range(2):
                            E.do("dv", lambda v, i=i: v.memset(
                                v_aug[i][:, :].rearrange("p (k a) -> p k a", a=DHA)
                                [:, :, 64:65], 1.0), sem="DV")
                            E.do("dv", lambda v, i=i: v.memset(
                                kh_buf[i][64:65, :], 1.0), sem="DV")

                        vab_ready = [None, None]
                        ot_evac = [None, None]
                        exp_of_sp = [None, None, None]
                        av_of_pt = [None, None, None, None]
                        attn_of_hb = [None, None]
                        kq_val = {}
                        for h in range(HPC):
                            hb = h % 2
                            vb = h % 2
                            lrl_hs = ("LRL", E.cnt["LRL"])
                            rl_hs = [("RL0", E.cnt["RL0"]), ("RL1", E.cnt["RL1"])]
                            def kq_load(E, hh):
                                hbb = hh % 2
                                ks = "KH0" if hbb == 0 else "KH1"
                                lw = PRJ_ALL + [("DV", E.cnt["DV"])]
                                if attn_of_hb[hbb] is not None:
                                    lw.append(attn_of_hb[hbb])
                                E.do("gp", lambda g, hh=hh, hbb=hbb: g.dma_start(
                                    kh_buf[hbb][0:64, :],
                                    khead_d[hh * 64:(hh + 1) * 64, :]),
                                    sem=ks, n=16, waits=lw)
                                E.do("gp", lambda g, hh=hh, hbb=hbb: g.dma_start(
                                    qh_buf[hbb][:, :],
                                    qhat_d[hh * DHA:(hh + 1) * DHA, :]),
                                    sem=ks, n=16)
                                kq_val[hh] = (ks, E.cnt[ks])
                            if h == 0:
                                kq_load(E, 0)
                                kq_load(E, 1)
                            kq_ok = kq_val[h]
                            cast_q = []
                            for ei in range(h * 2, h * 2 + 2):
                                for dst, srt, rows in [(w1T_d, w1T_i, D),
                                                       (w3T_d, w3T_i, D),
                                                       (w2T_d, w2T_i, c.HID)]:
                                    nch = 8
                                    crows = rows // nch
                                    for chk in range(nch):
                                        r0 = ei * rows + chk * crows
                                        cast_q.append((dst[r0:r0 + crows, :],
                                                       srt[r0:r0 + crows, :], ei))
                            wts = [("DV", E.cnt["DV"])]
                            if vab_ready[vb] is not None:
                                wts.append(vab_ready[vb])
                            E.do("dv", lambda v, h=h, vb=vb: v.tensor_copy(
                                v_aug[vb][:, :].rearrange("p (k a) -> p k a", a=DHA)
                                [:, :, 0:64],
                                v_tok[:, :].rearrange("p (k a) -> p k a", a=DL)
                                [:, :, h * 64:(h + 1) * 64]),
                                sem="DV", waits=wts)
                            va_ok = ("DV", E.cnt["DV"])
                            for s_ in range(NS):
                                ot = ot_ps[s_ % 2]
                                npair = (s_ + 1) * 2
                                LAG = 2
                                exp_rec = {}
                                mask_rec = {}

                                def emit_av(kp, E=None):
                                    pb = kp % 4
                                    av_w = [exp_rec[kp]]
                                    if mask_rec.get(kp) is not None:
                                        av_w.append(mask_rec[kp])
                                    if kp == 0 and ot_evac[s_ % 2] is not None:
                                        av_w.append(ot_evac[s_ % 2])
                                    for half in range(2):
                                        kb = 2 * kp + half
                                        E.do("pe", lambda t, kb=kb, pb=pb, half=half,
                                             kp=kp:
                                             t.matmul(
                                                 ot[:, :],
                                                 v_aug[vb][:, kb * DHA:(kb + 1) * DHA],
                                                 pt[pb][:, half * 512:(half + 1) * 512],
                                                 start=(kp == 0 and half == 0),
                                                 stop=(kp == npair - 1 and half == 1)),
                                             sem="PE", waits=av_w if half == 0 else ())
                                    av_of_pt[pb] = ("PE", E.cnt["PE"])

                                for kp in range(npair):
                                    pb = kp % 4
                                    spb = kp % 3
                                    sp = s_ps[spb]
                                    mm_w = [kq_ok, va_ok]
                                    if exp_of_sp[spb] is not None:
                                        mm_w.append(exp_of_sp[spb])
                                    for half in range(2):
                                        kb = 2 * kp + half
                                        E.do("pe", lambda t, sp=sp, hb=hb, kb=kb,
                                             s_=s_, half=half:
                                             t.matmul(sp[:, half * 512:(half + 1) * 512],
                                                      kh_buf[hb][:, kb * 128:(kb + 1) * 128],
                                                      qh_buf[hb][:, s_ * 512:(s_ + 1) * 512],
                                                      start=True, stop=True),
                                             sem="PE", waits=mm_w if half == 0 else ())
                                    smm = ("PE", E.cnt["PE"])
                                    ew = [smm]
                                    if av_of_pt[pb] is not None:
                                        ew.append(av_of_pt[pb])
                                    E.do("ac", lambda s, pb=pb, spb=spb: s.activation(
                                        pt[pb][:, :], s_ps[spb][:, :], AF.Exp),
                                        sem="AC", waits=ew)
                                    exp_of_sp[spb] = ("AC", E.cnt["AC"])
                                    exp_rec[kp] = ("AC", E.cnt["AC"])
                                    md = None
                                    for half in range(2):
                                        kb = 2 * kp + half
                                        m = kb - 4 * s_
                                        if 0 <= m < 4:
                                            E.do("dv", lambda v, pb=pb, half=half, m=m:
                                                 v.tensor_mul(
                                                     pt[pb][:, half * 512:(half + 1) * 512],
                                                     pt[pb][:, half * 512:(half + 1) * 512],
                                                     wmask[:, 384 - 128 * m:
                                                           896 - 128 * m]),
                                                 sem="DV", waits=[exp_rec[kp]])
                                            md = ("DV", E.cnt["DV"])
                                    mask_rec[kp] = md
                                    if kp >= LAG:
                                        emit_av(kp - LAG, E)
                                for kp in range(max(0, npair - LAG), npair):
                                    emit_av(kp, E)
                                otmm = ("PE", E.cnt["PE"])
                                lsm = "LS0" if s_ % 2 == 0 else "LS1"
                                E.do("dv", lambda v, ot=ot, s_=s_: v.tensor_copy(
                                    l_st[s_ % 2][:, :], ot[64:65, :]),
                                    sem="DV", waits=[otmm, (lsm, E.cnt[lsm])])
                                E.do("gp", lambda g, s_=s_: g.dma_start(
                                    l_d[0:1, s_ * 512:(s_ + 1) * 512],
                                    l_st[s_ % 2][:, :]),
                                    sem=lsm, n=16,
                                    waits=[("DV", E.cnt["DV"]), lrl_hs])
                                E.do("dv", lambda v, ot=ot, s_=s_: v.tensor_copy(
                                    raw_h[:, s_ * 512:(s_ + 1) * 512], ot[0:64, :]),
                                    sem="DV", waits=["DV"])
                                ot_evac[s_ % 2] = ("DV", E.cnt["DV"])
                                for _ in range(3):
                                    if cast_q:
                                        dst_, srt_, ei_ = cast_q.pop(0)
                                        E.do("gp", lambda g, dst_=dst_, srt_=srt_:
                                             g.dma_start(dst_, srt_),
                                             sem=f"WC{ei_}", n=16)
                            E.do("gp", lambda g: g.dma_start(
                                ltile[:, :],
                                l_d.rearrange("o (a b) -> (o a) b", b=c.SW)),
                                sem="LRL", n=16,
                                waits=["LRL", "LS0", "LS1", ("DV", E.cnt["DV"])])
                            E.do("dv", lambda v: v.reciprocal(
                                ltile[:, :], ltile[:, :]), sem="DV", waits=["LRL"])
                            E.do("gp", lambda g: g.dma_start(
                                rl_d.rearrange("o (a b) -> (o a) b", b=c.SW),
                                ltile[:, :]), sem="LRL", n=16,
                                waits=[("DV", E.cnt["DV"])] + rl_hs)
                            rl_done = ("LRL", E.cnt["LRL"])
                            while cast_q:
                                dst_, srt_, ei_ = cast_q.pop(0)
                                E.do("gp", lambda g, dst_=dst_, srt_=srt_:
                                     g.dma_start(dst_, srt_),
                                     sem=f"WC{ei_}", n=16)
                            for s_ in range(NS):
                                rb = s_ % 2
                                rsm = "RL0" if rb == 0 else "RL1"
                                E.do("gp", lambda g, s_=s_, rb=rb: g.dma_start(
                                    rlb[rb][:, :],
                                    bass.AP(rl_d.tensor, s_ * 512,
                                            [[0, 64], [1, 512]])),
                                    sem=rsm, n=16,
                                    waits=[rl_done, ("DV", E.cnt["DV"])])
                                rlv = (rsm, E.cnt[rsm])
                                ab = att_st[s_ % 2]
                                asm_ = "AT0" if s_ % 2 == 0 else "AT1"
                                E.do("dv", lambda v, s_=s_, ab=ab, rb=rb:
                                     v.tensor_mul(
                                         ab[:, :],
                                         raw_h[:, s_ * 512:(s_ + 1) * 512],
                                         rlb[rb][:, :]),
                                    sem="DV",
                                    waits=["DV", rlv, (asm_, E.cnt[asm_])])
                                E.do("gp", lambda g, h=h, s_=s_, ab=ab: g.dma_start(
                                    attnT_d[h * 64:(h + 1) * 64,
                                            s_ * 512:(s_ + 1) * 512], ab[:, :]),
                                    sem=asm_, n=16, waits=[("DV", E.cnt["DV"])])
                            vab_ready[vb] = ("PE", E.cnt["PE"])
                            attn_of_hb[hb] = ("PE", E.cnt["PE"])
                            if h + 2 < HPC:
                                kq_load(E, h + 2)
                        if "d_attnT" in dbg_t:
                            E.do("gp", lambda g: g.dma_start(
                                dbg_t["d_attnT"], attnT_d), sem="STO", n=16,
                                waits=["AT0", "AT1"])
                        E.do("gp", lambda g: g.collective_compute(
                            "AllGather", ALU.bypass,
                            replica_groups=[[0, 1, 2, 3], [4, 5, 6, 7]],
                            ins=[attnT_d.opt()], outs=[attnT_full.opt()]),
                            sem="CC1", n=1, waits=["AT0", "AT1"])
                    em.phase(b2, p2)

        stB = ExitStack()
        eB = stB.enter_context
        resid = [eB(nc.sbuf_tensor(f"resid{i}", [128, D], F32))
                 for i in range(c.TQS)]
        sq_sc = eB(nc.sbuf_tensor("sq_sc", [128, D], F32))
        scatT = eB(nc.sbuf_tensor("scatT", [128, c.E * c.TQS], F32))
        gathT = eB(nc.sbuf_tensor("gathT", [128, c.E * c.TQS], I32))
        idx_all = eB(nc.sbuf_tensor("idx_all", [128, c.E * 3], I32))
        stC = ExitStack()
        sT_loc = stC.enter_context(nc.sbuf_tensor("sT_loc", [c.E, TQ], F32))

        st3 = ExitStack()
        with st3:
            e3 = st3.enter_context
            attnQ = [e3(nc.sbuf_tensor(f"attnQ{i}", [128, TQ], BF16))
                     for i in range(c.DCH)]
            woT = [e3(nc.sbuf_tensor(f"woT{i}", [128, D], BF16))
                   for i in range(c.DCH)]
            wgT = [e3(nc.sbuf_tensor(f"wgT{i}", [128, 16], BF16))
                   for i in range(c.DCH)]
            lt_sb = [e3(nc.sbuf_tensor(f"lt_sb{i}", [128, TQ], BF16))
                     for i in range(c.DCH)]
            ln1b = e3(nc.sbuf_tensor("ln1b", [128, D], BF16))
            xq_t = [e3(nc.sbuf_tensor(f"xq_t{i}", [128, D], F32)) for i in range(2)]
            bo_bc = e3(nc.sbuf_tensor("bo_bc", [128, D], F32))
            p_o = [e3(nc.psum_tensor(f"p_o{i}", [128, 512], F32)) for i in range(2)]
            p_g = e3(nc.psum_tensor("p_g", [16, 512], F32))
            p_lt = e3(nc.psum_tensor("p_lt", [128, 128], BF16))

            with nc.Block() as b3:
                def p3(E):
                    for cc in range(c.DCH):
                        E.do("sy", lambda s, cc=cc: s.dma_start(
                            woT[cc][:, :], woT_bf[cc * 128:(cc + 1) * 128, :]),
                            sem="WOT", n=16, waits=[("COG", 32)])
                        E.do("sy", lambda s, cc=cc: s.dma_start(
                            wgT[cc][:, 0:16], wgT_bf[cc * 128:(cc + 1) * 128, :]),
                            sem="WOT", n=16, waits=[("COG", 32)])
                    xbr_o = ("WOT", E.cnt["WOT"])
                    for half in range(2):
                        E.do("pe", lambda t, half=half: t.matmul(
                            p_o[half][:, :], ones_rf[:, :],
                            bo_row[0:1, half * 512:(half + 1) * 512],
                            start=True, stop=True), sem="PE", waits=[("LDS", 16)])
                        E.do("dv", lambda v, half=half: v.tensor_copy(
                            bo_bc[:, half * 512:(half + 1) * 512], p_o[half][:, :]),
                            sem="DV", waits=["PE"])
                    bo_ok = ("DV", E.cnt["DV"])
                    for cc in range(c.DCH):
                        E.do("gp", lambda g, cc=cc: g.indirect_dma_start(
                            attnQ[cc][:, :], None,
                            attnT_full.rearrange("(o a) b -> o (a b)", o=1),
                            bass.IndirectOffsetOnAxis(idxq_sb[:, cc:cc + 1], 1)),
                            sem="XQG", n=16, waits=[("CC1", 1), ("LDI", 16),
                                                    ("STO", E.cnt["STO"])])
                    xqg = ("XQG", E.cnt["XQG"])

                    evac_po = [bo_ok, bo_ok]
                    for tt in range(c.TQS):
                        xb = tt % 2
                        xsem = "SYB" if xb else "SYA"
                        E.do("sy", lambda s, tt=tt, xb=xb: s.dma_start(
                            xq_t[xb][:, :], x_q[tt * 128:(tt + 1) * 128, :]),
                            sem=xsem, n=16,
                            waits=[("DV", E.cnt["DV"])] if tt >= 2 else [])
                        xqv = (xsem, E.cnt[xsem])
                        for ch in range(2):
                            pp = p_o[ch]
                            for cc in range(c.DCH):
                                E.do("pe", lambda t, pp=pp, cc=cc, tt=tt, ch=ch:
                                     t.matmul(pp[:, :],
                                              attnQ[cc][:, tt * 128:(tt + 1) * 128],
                                              woT[cc][:, ch * 512:(ch + 1) * 512],
                                              start=(cc == 0), stop=(cc == c.DCH - 1)),
                                     sem="PE",
                                     waits=[xqg, xbr_o, evac_po[ch]] if cc == 0 else ())
                            E.do("dv", lambda v, pp=pp, tt=tt, ch=ch, xb=xb:
                                 v.scalar_tensor_tensor(
                                     resid[tt][:, ch * 512:(ch + 1) * 512], pp[:, :],
                                     0.0, xq_t[xb][:, ch * 512:(ch + 1) * 512],
                                     ALU.add, ALU.add), sem="DV",
                                 waits=["PE", xqv])
                            E.do("dv", lambda v, tt=tt, ch=ch: v.tensor_add(
                                resid[tt][:, ch * 512:(ch + 1) * 512],
                                resid[tt][:, ch * 512:(ch + 1) * 512],
                                bo_bc[:, ch * 512:(ch + 1) * 512]), sem="DV",
                                waits=["DV"])
                            evac_po[ch] = ("DV", E.cnt["DV"])
                        ln_emit(E, resid[tt], ln1b)
                        E.do("gp", lambda g, tt=tt: g.dma_start(
                            ln1_aug[tt * 128:(tt + 1) * 128, 0:D], ln1b[:, :]),
                            sem="LN1S", n=16, waits=[("DV", E.cnt["DV"])])
                        for cc in range(c.DCH):
                            E.do("pe", lambda t, cc=cc: t.transpose(
                                p_lt[:, :], ln1b[:, cc * 128:(cc + 1) * 128],
                                ident_b[:, :]),
                                sem="PE", waits=[("DV", E.cnt["DV"]),
                                                 ("DV", E.cnt["DV"])])
                            E.do("dv", lambda v, cc=cc, tt=tt: v.tensor_copy(
                                lt_sb[cc][:, tt * 128:(tt + 1) * 128], p_lt[:, :]),
                                sem="DV", waits=["PE"])
                        E.do("dv", lambda v: v.engine_nop(), sem="DV",
                             waits=[("LN1S", E.cnt["LN1S"])])
                        if "d_ln1" in dbg_t:
                            E.do("gp", lambda g, tt=tt: g.dma_start(
                                dbg_t["d_ln1"][tt * 128:(tt + 1) * 128, :],
                                resid[tt][:, :]), sem="STO", n=16,
                                waits=[("DV", E.cnt["DV"])])
                    ln1_st = ("LN1S", E.cnt["LN1S"])

                    lt_ok = ("DV", E.cnt["DV"])
                    for gs in range(c.GST):
                        for cc in range(c.DCH):
                            E.do("pe", lambda t, gs=gs, cc=cc: t.matmul(
                                p_g[:, :], wgT[cc][:, 0:16],
                                lt_sb[cc][:, gs * 512:(gs + 1) * 512],
                                start=(cc == 0), stop=(cc == c.DCH - 1)),
                                sem="PE",
                                waits=[lt_ok, ("DV", E.cnt["DV"])] if cc == 0 else ())
                        E.do("dv", lambda v, gs=gs: v.tensor_copy(
                            sT_loc[:, gs * 512:(gs + 1) * 512], p_g[0:8, :]),
                            sem="DV", waits=["PE"])
                    E.do("gp", lambda g: g.dma_start(sT_dr, sT_loc[:, :]),
                         sem="STD", n=16, waits=[("DV", E.cnt["DV"])])
                    E.do("gp", lambda g: g.collective_compute(
                        "AllGather", ALU.bypass,
                        replica_groups=[[0, 1, 2, 3, 4, 5, 6, 7]],
                        ins=[sT_dr.opt()], outs=[s_full.opt()]),
                        sem="CC2", n=1, waits=["STD"])
                    if "d_sT" in dbg_t:
                        E.do("gp", lambda g: g.dma_start(dbg_t["d_sT"], sT_loc[:, :]),
                             sem="STO", n=16, waits=[("DV", E.cnt["DV"])])
                em.phase(b3, p3)

        NW = c.CORES * TQ // 16
        st4 = ExitStack()
        with st4:
            e4 = st4.enter_context
            S_all = e4(nc.sbuf_tensor("S_all", [128, NW], F32))
            scr = e4(nc.sbuf_tensor("scr", [128, NW], F32))
            scr2 = e4(nc.sbuf_tensor("scr2", [128, NW], F32))
            sv8 = e4(nc.sbuf_tensor("sv8", [8, 16], F32))
            prow = e4(nc.sbuf_tensor("prow", [1, 16], F32))
            colp = e4(nc.sbuf_tensor("colp", [128, 4], F32))
            r1x = e4(nc.sbuf_tensor("r1x", [1, 128], F32))
            mk = e4(nc.sbuf_tensor("mk", [c.E, TQ], F32))
            wl = e4(nc.sbuf_tensor("wl", [c.E, TQ], F32))
            ic = e4(nc.sbuf_tensor("ic", [c.E, TQ], F32))
            on8 = e4(nc.sbuf_tensor("on8", [c.E, TQ], F32))
            ok8 = e4(nc.sbuf_tensor("ok8", [c.E, TQ], F32))
            sf8 = e4(nc.sbuf_tensor("sf8", [c.E, TQ], F32))
            gf8 = e4(nc.sbuf_tensor("gf8", [c.E, TQ], F32))
            trb = e4(nc.sbuf_tensor("trb", [128, 8], BF16))
            tsramp = e4(nc.sbuf_tensor("tsramp", [c.E, TQ], F32))
            tsneg = e4(nc.sbuf_tensor("tsneg", [c.E, TQ], F32))
            rampJ = e4(nc.sbuf_tensor("rampJ", [128, 384], F32))
            rhs2 = e4(nc.sbuf_tensor("rhs2", [128, 2 * c.TQS], BF16))
            mt_bf = [e4(nc.sbuf_tensor(f"mt_bf{i}", [128, 384], BF16))
                     for i in range(4)]
            tok_all_f = e4(nc.sbuf_tensor("tok_all_f", [128, c.E * 3], F32))
            pall = e4(nc.sbuf_tensor("pall", [128, c.E * 6], F32))
            p_s1 = e4(nc.psum_tensor("p_s1", [128, 8], F32))
            p_s2 = e4(nc.psum_tensor("p_s2", [8, 8], F32))
            p_tr = e4(nc.psum_tensor("p_tr", [1, 128], F32))
            p_inv = e4(nc.psum_tensor("p_inv", [128, 48], F32))

            with nc.Block() as b4:
                def p4(E):
                    for hh in range(2):
                        E.do("gp", lambda g, hh=hh: g.dma_start(
                            s_re.rearrange("(e r h) q -> e r h q", e=c.E,
                                           r=c.CORES, h=2)[:, :, hh, :],
                            s_full.rearrange("(r e) (h q) -> r e h q", r=c.CORES,
                                             e=c.E, h=2).transpose([1, 0, 2, 3])
                            [:, :, hh, :]),
                            sem="SRE", n=16, waits=[("CC2", 1)])
                    sal = E.do("gp", lambda g: g.dma_start(
                        S_all[:, :], s_re), sem="SRE", n=16, waits=["SRE"])

                    E.do("dv", lambda v: v.memset(sv8[:, :], 0.0), sem="DV")
                    E.do("dv", lambda v: v.memset(prow[:, :], 0.0), sem="DV")
                    E.do("dv", lambda v: v.memset(colp[:, :], 0.0), sem="DV")
                    E.do("dv", lambda v: v.memset(r1x[:, :], 0.0), sem="DV")

                    def expert_reduce(src_col, dst_slice, op):
                        E.do("pe", lambda t, src_col=src_col: t.transpose(
                            p_tr[:, :], src_col, ident_f[:, :]),
                            sem="PE", waits=[("DV", E.cnt["DV"])])
                        E.do("dv", lambda v: v.tensor_copy(r1x[:, :], p_tr[:, :]),
                             sem="DV", waits=["PE"])
                        E.do("dv", lambda v, op=op: v.tensor_reduce(
                            prow[0:1, 0:8],
                            r1x[0:1, :].rearrange("o (e s) -> o e s", s=16),
                            AX.X, op), sem="DV", waits=["DV"])
                        E.do("pe", lambda t: t.transpose(
                            p_s2[:, 0:1], prow[0:1, 0:8], ident_f[0:1, 0:1]),
                            sem="PE", waits=[("DV", E.cnt["DV"])])
                        E.do("dv", lambda v, dst_slice=dst_slice: v.tensor_copy(
                            dst_slice, p_s2[:, 0:1]), sem="DV", waits=["PE"])

                    E.do("dv", lambda v: v.reduce_max(
                        colp[:, 0:1], S_all[:, :], AX.X), sem="DV",
                        waits=[sal, "DV"])
                    expert_reduce(colp[:, 0:1], sv8[:, 1:2], ALU.max)
                    E.do("dv", lambda v: v.tensor_reduce(
                        colp[:, 0:1], S_all[:, :], AX.X, ALU.min), sem="DV",
                        waits=[sal, ("PE", E.cnt["PE"])])
                    expert_reduce(colp[:, 0:1], sv8[:, 0:1], ALU.min)
                    E.do("dv", lambda v: v.tensor_copy(sv8[:, 4:5], sv8[:, 1:2]),
                         sem="DV", waits=["DV"])
                    E.do("dv", lambda v: v.tensor_scalar_mul(
                        sv8[:, 5:6], sv8[:, 4:5], -1.0), sem="DV", waits=["DV"])

                    for it in range(c.NIT):
                        E.do("dv", lambda v: v.tensor_add(
                            sv8[:, 2:3], sv8[:, 0:1], sv8[:, 1:2]), sem="DV",
                            waits=["DV", ("PE", E.cnt["PE"])])
                        E.do("dv", lambda v: v.tensor_scalar_mul(
                            sv8[:, 2:3], sv8[:, 2:3], 0.5), sem="DV", waits=["DV"])
                        E.do("pe", lambda t: t.matmul(
                            p_s1[:, 0:1], g8t[:, :], sv8[:, 2:3],
                            start=True, stop=True), sem="PE",
                            waits=[("DV", E.cnt["DV"])])
                        E.do("dv", lambda v: v.tensor_copy(
                            colp[:, 2:3], p_s1[:, 0:1]), sem="DV", waits=["PE"])
                        E.do("dv", lambda v: v.tensor_scalar(
                            scr[:, :], S_all[:, :], colp[:, 2:3], 0.0,
                            ALU.is_gt, ALU.add, accum_out=colp[:, 3:4]),
                            sem="DV", waits=["DV"])
                        E.do("pe", lambda t: t.matmul(
                            p_s2[:, 0:1], g8[:, :], colp[:, 3:4],
                            start=True, stop=True), sem="PE",
                            waits=[("DV", E.cnt["DV"])])
                        E.do("dv", lambda v: v.tensor_copy(
                            sv8[:, 8:9], p_s2[:, 0:1]), sem="DV", waits=["PE"])
                        E.do("dv", lambda v: v.tensor_single_scalar(
                            sv8[:, 3:4], sv8[:, 8:9], c.CAPT - 0.5, ALU.is_gt),
                            sem="DV", waits=["DV"])
                        E.do("dv", lambda v: v.tensor_sub(
                            sv8[:, 9:10], sv8[:, 2:3], sv8[:, 0:1]),
                            sem="DV", waits=["DV"])
                        E.do("dv", lambda v: v.tensor_sub(
                            sv8[:, 10:11], sv8[:, 1:2], sv8[:, 2:3]),
                            sem="DV", waits=["DV"])
                        E.do("dv", lambda v: v.tensor_mul(
                            sv8[:, 9:10], sv8[:, 9:10], sv8[:, 3:4]),
                            sem="DV", waits=["DV"])
                        E.do("dv", lambda v: v.tensor_mul(
                            sv8[:, 10:11], sv8[:, 10:11], sv8[:, 3:4]),
                            sem="DV", waits=["DV"])
                        E.do("dv", lambda v: v.tensor_add(
                            sv8[:, 0:1], sv8[:, 0:1], sv8[:, 9:10]),
                            sem="DV", waits=["DV"])
                        E.do("dv", lambda v: v.tensor_add(
                            sv8[:, 1:2], sv8[:, 2:3], sv8[:, 10:11]),
                            sem="DV", waits=["DV"])
                    E.do("dv", lambda v: v.tensor_add(
                        sv8[:, 2:3], sv8[:, 0:1], sv8[:, 1:2]), sem="DV",
                        waits=["DV", ("PE", E.cnt["PE"])])
                    E.do("dv", lambda v: v.tensor_scalar_mul(
                        sv8[:, 2:3], sv8[:, 2:3], 0.5), sem="DV", waits=["DV"])
                    E.do("pe", lambda t: t.matmul(
                        p_s1[:, 0:1], g8t[:, :], sv8[:, 2:3], start=True, stop=True),
                        sem="PE", waits=[("DV", E.cnt["DV"])])
                    E.do("pe", lambda t: t.matmul(
                        p_s1[:, 1:2], g8t[:, :], sv8[:, 5:6], start=True, stop=True),
                        sem="PE", waits=[("DV", E.cnt["DV"])])
                    E.do("dv", lambda v: v.tensor_copy(colp[:, 2:4], p_s1[:, 0:2]),
                         sem="DV", waits=["PE"])
                    E.do("dv", lambda v: v.tensor_scalar(
                        scr[:, :], S_all[:, :], colp[:, 2:3], None, ALU.is_gt),
                        sem="DV", waits=["DV"])
                    E.do("ac", lambda s: s.activation(
                        scr2[:, :], S_all[:, :], AF.Exp, bias=colp[:, 3:4]),
                        sem="AC", waits=[("DV", E.cnt["DV"])])
                    E.do("dv", lambda v: v.scalar_tensor_tensor(
                        scr2[:, :], scr2[:, :], 1.0, scr[:, :],
                        ALU.mult, ALU.mult, accum_out=colp[:, 0:1]),
                        sem="DV", waits=[("AC", E.cnt["AC"]), "DV"])
                    E.do("pe", lambda t: t.matmul(
                        p_s2[:, 0:1], g8[:, :], colp[:, 0:1], start=True, stop=True),
                        sem="PE", waits=[("DV", E.cnt["DV"])])
                    E.do("dv", lambda v: v.tensor_copy(sv8[:, 6:7], p_s2[:, 0:1]),
                         sem="DV", waits=["PE"])
                    E.do("dv", lambda v: v.reciprocal(sv8[:, 7:8], sv8[:, 6:7]),
                         sem="DV", waits=["DV"])
                    if "d_prm" in dbg_t:
                        E.do("gp", lambda g: g.dma_start(dbg_t["d_prm"], sv8[:, :]),
                             sem="STO", n=16, waits=[("DV", E.cnt["DV"])])

                    E.do("dv", lambda v: v.tensor_scalar(
                        mk[:, :], sT_loc[:, :], sv8[:, 2:3], None, ALU.is_gt),
                        sem="DV", waits=["DV"])
                    E.do("ac", lambda s: s.activation(
                        wl[:, :], sT_loc[:, :], AF.Exp, bias=sv8[:, 5:6]),
                        sem="AC", waits=[("DV", E.cnt["DV"])])
                    E.do("dv", lambda v: v.scalar_tensor_tensor(
                        wl[:, :], wl[:, :], sv8[:, 7:8], mk[:, :],
                        ALU.mult, ALU.mult), sem="DV",
                        waits=[("AC", E.cnt["AC"]), "DV"])
                    E.do("dv", lambda v: v.memset(on8[:, :], 1.0), sem="DV")
                    E.do("dv", lambda v: v.tensor_tensor_scan(
                        ic[:, :], mk[:, :], on8[:, :], 0.0, ALU.add, ALU.mult),
                        sem="DV", waits=["DV"])
                    E.do("dv", lambda v: v.tensor_single_scalar(
                        ok8[:, :], ic[:, :], c.CSLOT + 0.5, ALU.is_lt),
                        sem="DV", waits=["DV"])
                    E.do("dv", lambda v: v.tensor_mul(
                        ok8[:, :], ok8[:, :], mk[:, :]), sem="DV", waits=["DV"])
                    E.do("dv", lambda v: v.scalar_tensor_tensor(
                        sf8[:, :], ic[:, :], -386.0, ok8[:, :], ALU.add, ALU.mult),
                        sem="DV", waits=["DV"])
                    E.do("dv", lambda v: v.tensor_scalar_add(
                        sf8[:, :], sf8[:, :], 385.0), sem="DV", waits=["DV"])
                    E.do("gp", lambda g: g.iota(
                        tsramp[:, :], [[1, c.TQS], [0, 128]],
                        channel_multiplier=0,
                        allow_small_or_imprecise_dtypes=True), sem="GP")
                    E.do("dv", lambda v: v.scalar_tensor_tensor(
                        tsneg[:, :], ok8[:, :], -1.0, tsramp[:, :],
                        ALU.mult, ALU.mult), sem="DV",
                        waits=["DV", ("GP", E.cnt["GP"])])
                    E.do("dv", lambda v: v.tensor_add(
                        sf8[:, :], sf8[:, :], tsramp[:, :]), sem="DV",
                        waits=["DV"])
                    E.do("dv", lambda v: v.tensor_add(
                        sf8[:, :], sf8[:, :], tsneg[:, :]), sem="DV",
                        waits=["DV"])
                    E.do("dv", lambda v: v.scalar_tensor_tensor(
                        gf8[:, :], ic[:, :], -385.0, ok8[:, :], ALU.add, ALU.mult),
                        sem="DV", waits=["DV"])
                    E.do("dv", lambda v: v.tensor_scalar_add(
                        gf8[:, :], gf8[:, :], 384.0), sem="DV", waits=["DV"])
                    slots_ok = ("DV", E.cnt["DV"])
                    if "d_wl" in dbg_t:
                        E.do("gp", lambda g: g.dma_start(dbg_t["d_wl"], wl[:, :]),
                             sem="STO", n=16, waits=[slots_ok])
                    if "d_slot" in dbg_t:
                        E.do("gp", lambda g: g.dma_start(dbg_t["d_slot"], sf8[:, :]),
                             sem="STO", n=16, waits=[slots_ok])

                    for ts_ in range(c.TQS):
                        E.do("gp", lambda g, ts_=ts_: g.dma_start(
                            ln1_aug[ts_ * 128:(ts_ + 1) * 128, D + 16:c.AUGW],
                            zrow_b[:, 0:c.AUGW - D - 16]), sem="AUGC", n=16)
                    for ts_ in range(c.TQS):
                        jobs = [(sf8, "scat"), (gf8, "gath"), (sT_loc, "s"),
                                (wl, "w")]
                        for src, kindt in jobs:
                            E.do("pe", lambda t, src=src, ts_=ts_: t.transpose(
                                p_s1[:, :], src[:, ts_ * 128:(ts_ + 1) * 128],
                                ident_f[0:8, 0:8]),
                                sem="PE",
                                waits=[slots_ok, ("DV", E.cnt["DV"]),
                                       ("GP", E.cnt["GP"])])
                            if kindt == "scat":
                                E.do("dv", lambda v, ts_=ts_: v.tensor_copy(
                                    scatT[:, ts_ * 8:(ts_ + 1) * 8], p_s1[:, :]),
                                    sem="DV", waits=["PE"])
                            elif kindt == "gath":
                                E.do("dv", lambda v, ts_=ts_: v.tensor_copy(
                                    gathT[:, ts_ * 8:(ts_ + 1) * 8], p_s1[:, :]),
                                    sem="DV", waits=["PE"])
                            else:
                                colbase = 1024 if kindt == "s" else 1032
                                E.do("dv", lambda v: v.tensor_copy(
                                    trb[:, :], p_s1[:, :]), sem="DV", waits=["PE"])
                                E.do("gp", lambda g, ts_=ts_, colbase=colbase:
                                     g.dma_start(
                                         ln1_aug[ts_ * 128:(ts_ + 1) * 128,
                                                 colbase:colbase + 8], trb[:, :]),
                                     sem="AUGC", n=16, waits=[("DV", E.cnt["DV"])])
                                E.do("dv", lambda v: v.engine_nop(), sem="DV",
                                     waits=[("AUGC", E.cnt["AUGC"])])
                    trans_dv = ("DV", E.cnt["DV"])
                    aug_done = ("AUGC", E.cnt["AUGC"])

                    E.do("gp", lambda g: g.iota(
                        rampJ[:, :], [[1, 384]], channel_multiplier=0,
                        allow_small_or_imprecise_dtypes=True), sem="GP")
                    E.do("gp", lambda g: g.iota(
                        rhs2[:, :], [[0, 2 * c.TQS]], channel_multiplier=1,
                        allow_small_or_imprecise_dtypes=True), sem="GP")
                    gp_inv = ("GP", E.cnt["GP"])
                    for t_ in range(c.TQS):
                        E.do("dv", lambda v, t_=t_: v.memset(
                            rhs2[:, 2 * t_ + 1:2 * t_ + 2], float(t_)),
                            sem="DV", waits=[gp_inv, "DV"])
                    inv_pre = ("DV", E.cnt["DV"])
                    mt_hist = []
                    it = 0
                    for ei in range(c.E):
                        for t_ in range(c.TQS):
                            bu = it % 4
                            wv = [trans_dv, inv_pre, "DV"]
                            if it >= 4:
                                wv.append(("PE", mt_hist[it - 4]))
                            E.do("dv", lambda v, ei=ei, t_=t_, bu=bu:
                                 v.tensor_scalar(
                                     mt_bf[bu][:, :], rampJ[:, :],
                                     scatT[:, t_ * 8 + ei:t_ * 8 + ei + 1],
                                     0.0,
                                     ALU.subtract, ALU.is_equal),
                                 sem="DV", waits=wv)
                            dvv = ("DV", E.cnt["DV"])
                            for chx in range(3):
                                kcol = (ei * 3 + chx) * 2
                                E.do("pe", lambda t, bu=bu, kcol=kcol, t_=t_,
                                     chx=chx: t.matmul(
                                         p_inv[:, kcol:kcol + 2],
                                         mt_bf[bu][:, chx * 128:
                                                    (chx + 1) * 128],
                                         rhs2[:, 2 * t_:2 * t_ + 2],
                                         start=(t_ == 0),
                                         stop=(t_ == c.TQS - 1)),
                                     sem="PE", waits=[dvv])
                            mt_hist.append(E.cnt["PE"])
                            it += 1
                    pe_inv = ("PE", E.cnt["PE"])
                    E.do("dv", lambda v: v.tensor_copy(
                        pall[:, :], p_inv[:, :]), sem="DV",
                        waits=[pe_inv, "DV"])
                    for ei in range(c.E):
                        for chx in range(3):
                            kcol = (ei * 3 + chx) * 2
                            E.do("dv", lambda v, kcol=kcol, ei=ei, chx=chx:
                                 v.scalar_tensor_tensor(
                                     tok_all_f[:, ei * 3 + chx:
                                               ei * 3 + chx + 1],
                                     pall[:, kcol + 1:kcol + 2], 128.0,
                                     pall[:, kcol:kcol + 1],
                                     ALU.mult, ALU.add),
                                 sem="DV", waits=["DV"])
                    E.do("dv", lambda v: v.tensor_copy(
                        idx_all[:, :], tok_all_f[:, :]), sem="DV",
                        waits=["DV"])
                em.phase(b4, p4)
        SC_ALL = [(f"SC{i}", em.cnt[f"SC{i}"]) for i in range(c.E)]
        AUG_ALL = [("AUGC", em.cnt["AUGC"]), ("LN1S", em.cnt["LN1S"])]

        stC.close()
        st5 = ExitStack()
        with st5:
            e5 = st5.enter_context
            xgt = [e5(nc.sbuf_tensor(f"xgt{i}", [128, c.CSLOT], BF16))
                   for i in range(c.DCH)]
            xg_sb = [e5(nc.sbuf_tensor(f"xg_sb{i}", [128, c.AUGW], BF16))
                     for i in range(3)]
            idxg = e5(nc.sbuf_tensor("idxg", [128, 3], I32))
            w1s = [e5(nc.sbuf_tensor(f"w1s{i}", [128, c.HID], BF16))
                   for i in range(c.DCH)]
            w3s = [e5(nc.sbuf_tensor(f"w3s{i}", [128, c.HID], BF16))
                   for i in range(c.DCH)]
            w2s = [e5(nc.sbuf_tensor(f"w2s{i}", [128, D], BF16))
                   for i in range(3)]
            hts = [e5(nc.sbuf_tensor(f"hts{i}", [128, c.CSLOT], BF16))
                   for i in range(c.HT)]
            h1f = e5(nc.sbuf_tensor("h1f", [128, c.CSLOT], F32))
            h3f = e5(nc.sbuf_tensor("h3f", [128, c.CSLOT], F32))
            y_sb = [e5(nc.sbuf_tensor(f"y_sb{i}", [128, D], BF16)) for i in range(3)]
            wcol = e5(nc.sbuf_tensor("wcol", [128, 3], F32))
            p_h = [e5(nc.psum_tensor(f"p_h{i}", [128, c.CSLOT], F32))
                   for i in range(2)]
            p_y = [e5(nc.psum_tensor(f"p_y{i}", [128, 512], F32)) for i in range(6)]

            for _s in ("W1L", "W3L", "SYC", "XGG"):
                em.sem(_s)
            with nc.Block() as b5:
                def p5(E):
                    prev_gath = None
                    yprev = None
                    xgl_val = {}
                    w1_val = {}
                    w3_val = {}
                    w2_use = [0, 0, 0]
                    W2S = ("SYA", "SYB", "SYC")

                    def xg_loads(E, ee):
                        prevg = ("XGG", E.cnt["XGG"])
                        for sl in range(3):
                            E.do("gp", lambda g, ee=ee, sl=sl:
                                 g.indirect_dma_start(
                                     xg_sb[sl][:, :], None,
                                     ln1_aug,
                                     bass.IndirectOffsetOnAxis(
                                         idx_all[:, ee * 3 + sl:
                                                 ee * 3 + sl + 1], 0),
                                     element_offset=0),
                                 sem="XGG", n=16,
                                 waits=[prevg, ("PE", E.cnt["PE"]),
                                        ("STO", E.cnt["STO"])] + AUG_ALL)
                        xgl_val[ee] = ("XGG", E.cnt["XGG"])

                    def w1_load(E, ee, cs, pe_guard):
                        E.do("gp", lambda g, ee=ee, cs=cs: g.dma_start(
                            w1s[cs][:, :],
                            w1T_d[ee * D + cs * 128:ee * D + (cs + 1) * 128, :]),
                            sem="W1L", n=16,
                            waits=[(f"WC{ee}", 16 * 24), ("PE", pe_guard),
                                   ("STO", E.cnt["STO"])] + SC_ALL)
                        w1_val[ee] = ("W1L", E.cnt["W1L"])

                    def w3_load(E, ee, cs, pe_guard):
                        E.do("sy", lambda s, ee=ee, cs=cs: s.dma_start(
                            w3s[cs][:, :],
                            w3T_d[ee * D + cs * 128:ee * D + (cs + 1) * 128, :]),
                            sem="W3L", n=16,
                            waits=[(f"WC{ee}", 16 * 24), ("PE", pe_guard),
                                   ("STO", E.cnt["STO"])] + SC_ALL)
                        w3_val[ee] = ("W3L", E.cnt["W3L"])

                    for ei in range(c.E):
                        if ei == 0:
                            xg_loads(E, 0)
                            for cs in range(c.DCH):
                                w1_load(E, 0, cs, E.cnt["PE"])
                                w3_load(E, 0, cs, E.cnt["PE"])
                        xgl_ok = xgl_val[ei]
                        for sl in range(3):
                            for cs in range(c.DCH):
                                E.do("pe", lambda t, sl=sl, cs=cs: t.matmul(
                                    p_y[0][:, 0:128],
                                    xg_sb[sl][:, cs * 128:(cs + 1) * 128],
                                    ident_b[:, :], start=True, stop=True),
                                    sem="PE", waits=[xgl_ok, ("DV", E.cnt["DV"])])
                                E.do("dv", lambda v, sl=sl, cs=cs: v.tensor_copy(
                                    xgt[cs][:, sl * 128:(sl + 1) * 128],
                                    p_y[0][:, 0:128]),
                                    sem="DV", waits=["PE"])
                        xg_ok = ("DV", E.cnt["DV"])
                        wld1, wld3 = w1_val[ei], w3_val[ei]
                        for sl in range(3):
                            E.do("dv", lambda v, sl=sl, ei=ei: v.tensor_copy(
                                wcol[:, sl:sl + 1],
                                xg_sb[sl][:, 1032 + ei:1033 + ei]),
                                sem="DV", waits=[xgl_ok, "DV"])
                        wcol_ok = ("DV", E.cnt["DV"])

                        h_evac = [None, None]
                        for ht in range(c.HT):
                            pb = ht % 2
                            for wsl, hf, wldx in [(w1s, h1f, wld1),
                                                  (w3s, h3f, wld3)]:
                                mw = [xg_ok, wldx]
                                if h_evac[pb] is not None:
                                    mw.append(h_evac[pb])
                                for cs in range(c.DCH):
                                    E.do("pe", lambda t, pb=pb, wsl=wsl, cs=cs, ht=ht:
                                         t.matmul(p_h[pb][:, :],
                                                  wsl[cs][:, ht * 128:(ht + 1) * 128],
                                                  xgt[cs][:, :],
                                                  start=(cs == 0),
                                                  stop=(cs == c.DCH - 1)),
                                         sem="PE", waits=mw if cs == 0 else ())
                                E.do("dv", lambda v, pb=pb, hf=hf: v.tensor_copy(
                                    hf[:, :], p_h[pb][:, :]), sem="DV",
                                    waits=["PE", ("AC", E.cnt["AC"]), "DV"])
                                h_evac[pb] = ("DV", E.cnt["DV"])
                            E.do("ac", lambda s: s.activation(
                                h1f[:, :], h1f[:, :], AF.Silu), sem="AC",
                                waits=[("DV", E.cnt["DV"])])
                            E.do("dv", lambda v, ht=ht: v.tensor_mul(
                                hts[ht][:, :], h1f[:, :], h3f[:, :]),
                                sem="DV", waits=[("AC", E.cnt["AC"]), "DV"])
                            h_evac[pb] = ("DV", E.cnt["DV"])
                        ht_ok = ("DV", E.cnt["DV"])
                        pe_after_w13 = E.cnt["PE"]
                        for hc in range(c.HT):
                            wb = hc % 3
                            wsm = W2S[wb]
                            E.do("sy", lambda s, ei=ei, hc=hc, wb=wb: s.dma_start(
                                w2s[wb][:, :],
                                w2T_d[ei * c.HID + hc * 128:
                                      ei * c.HID + (hc + 1) * 128, :]),
                                sem=wsm, n=16,
                                waits=[(f"WC{ei}", 16 * 24), ("PE", w2_use[wb])])
                            wldv = (wsm, E.cnt[wsm])
                            if ei + 1 < c.E:
                                if hc < c.DCH:
                                    w1_load(E, ei + 1, hc, pe_after_w13)
                                if hc % 2 == 0:
                                    w3_load(E, ei + 1, hc // 2, pe_after_w13)
                            for ch in range(2):
                                for sl in range(3):
                                    mw = [ht_ok, wldv] if (ch == 0 and sl == 0) \
                                        else ()
                                    E.do("pe", lambda t, sl=sl, ch=ch, hc=hc, wb=wb:
                                         t.matmul(p_y[ch * 3 + sl][:, :],
                                                  hts[hc][:, sl * 128:(sl + 1) * 128],
                                                  w2s[wb][:, ch * 512:(ch + 1) * 512],
                                                  start=(hc == 0),
                                                  stop=(hc == c.HT - 1)),
                                         sem="PE", waits=mw)
                            w2_use[wb] = E.cnt["PE"]
                        if ei + 1 < c.E:
                            xg_loads(E, ei + 1)
                        for sl in range(3):
                            for ch in range(2):
                                E.do("dv", lambda v, sl=sl, ch=ch:
                                     v.tensor_scalar(
                                         y_sb[sl][:, ch * 512:(ch + 1) * 512],
                                         p_y[ch * 3 + sl][:, :],
                                         wcol[:, sl:sl + 1], None, ALU.mult),
                                     sem="DV",
                                     waits=["PE", wcol_ok] +
                                           ([yprev] if yprev is not None else []))
                            E.do("gp", lambda g, ei=ei, sl=sl: g.dma_start(
                                ysl_d[ei * c.XROW + sl * 128:
                                      ei * c.XROW + (sl + 1) * 128, :],
                                y_sb[sl][:, :]),
                                sem=f"YS{ei}", n=16,
                                waits=[("DV", E.cnt["DV"])])
                        yprev = (f"YS{ei}", E.cnt[f"YS{ei}"])
                        gw = [(f"YS{ei}", 16 * 3)]
                        if prev_gath is not None:
                            gw.append(prev_gath)
                        for ts_ in range(c.TQS):
                            E.do("gp", lambda g, ei=ei, ts_=ts_:
                                 g.indirect_dma_start(
                                     resid[ts_][:, :], None,
                                     ysl_d,
                                     bass.IndirectOffsetOnAxis(
                                         gathT[:, ts_ * 8 + ei:ts_ * 8 + ei + 1], 0),
                                     element_offset=ei * c.XROW * D,
                                     compute_op=ALU.add),
                                 sem="IND", n=16, waits=gw if ts_ == 0 else ())
                        prev_gath = ("IND", E.cnt["IND"])
                em.phase(b5, p5)

            with nc.Block() as b6:
                def p6(E):
                    E.wait("IND") if False else None
                    for tt in range(c.TQS):
                        if tt == 0:
                            E.do("dv", lambda v: v.engine_nop(), sem="DV",
                                 waits=[("IND", E.cnt["IND"])])
                        ln_emit(E, resid[tt], None)
                        E.do("gp", lambda g, tt=tt: g.dma_start(
                            y_out[tt * 128:(tt + 1) * 128, :], resid[tt][:, :]),
                            sem="STO", n=16, waits=[("DV", E.cnt["DV"])])
                    E.do("gp", lambda g: g.engine_nop(), sem=None,
                         waits=[("STO", E.cnt["STO"])])
                em.phase(b6, p6)
        stB.close()

    return nc


def em_snapshot(E, sem):
    return E.cnt[sem]


_CACHE = {}


def make_in_maps(cfg, L0_attn_x, L0_attn_Wq, L0_attn_bq, L0_attn_Wk, L0_attn_bk,
                 L0_attn_Wv, L0_attn_bv, L0_attn_Wo, L0_attn_bo,
                 L0_ffn_Wg, L0_ffn_W1, L0_ffn_W2, L0_ffn_W3):
    x = np.ascontiguousarray(np.asarray(L0_attn_x, dtype=np.float32))
    B, S, D = x.shape
    c = cfg
    f32 = lambda a: np.ascontiguousarray(np.asarray(a, dtype=np.float32))
    Wq, Wk, Wv, Wo = map(f32, (L0_attn_Wq, L0_attn_Wk, L0_attn_Wv, L0_attn_Wo))
    bq, bk, bv, bo = map(f32, (L0_attn_bq, L0_attn_bk, L0_attn_bv, L0_attn_bo))
    Wg, W1, W2, W3 = map(f32, (L0_ffn_Wg, L0_ffn_W1, L0_ffn_W2, L0_ffn_W3))
    Wg16 = np.zeros((16, D), np.float32)
    Wg16[:c.E] = Wg
    WgT = np.ascontiguousarray(Wg16.T)
    WoT = np.ascontiguousarray(Wo.T)
    W1T = np.ascontiguousarray(W1.transpose(0, 2, 1)).reshape(c.E * D, c.HID)
    W3T = np.ascontiguousarray(W3.transpose(0, 2, 1)).reshape(c.E * D, c.HID)
    W2T = np.ascontiguousarray(W2.transpose(0, 2, 1)).reshape(c.E * c.HID, D)
    xT = [np.ascontiguousarray(x[b].T) for b in range(B)]

    in_maps = []
    for core in range(c.CORES):
        b = core // c.GC
        j = core % c.GC
        hsl = slice(j * c.DL, (j + 1) * c.DL)
        idx = (np.arange(D, dtype=np.int64) * S + j * c.TQ).astype(np.int32)
        idx = idx.reshape(c.DCH, 128).T.copy()
        in_maps.append(dict(
            xT_h=xT[b],
            x_q=x[b, j * c.TQ:(j + 1) * c.TQ],
            wqT_i=np.ascontiguousarray(Wq[hsl].T),
            wkT_i=np.ascontiguousarray(Wk[hsl].T),
            wvT_i=np.ascontiguousarray(Wv[hsl].T),
            bq_l=bq[hsl].reshape(2, 128), bk_l=bk[hsl].reshape(2, 128),
            bv_l=bv[hsl].reshape(2, 128), bv_r=bv[hsl].reshape(1, c.DL),
            woT_i=WoT, bo_r=bo.reshape(1, D),
            wgT_i=WgT, w1T_i=W1T, w3T_i=W3T, w2T_i=W2T,
            idxq=idx,
        ))
    return in_maps


def kernel(**inputs):
    x = np.asarray(inputs["L0_attn_x"])
    B, S, D = x.shape
    cfg = Cfg(S=S)
    c = cfg
    key = (S,)
    if key not in _CACHE:
        _CACHE[key] = build(cfg)
    nc = _CACHE[key]
    in_maps = make_in_maps(cfg, **inputs)
    res = run_bass_kernel_spmd(nc, in_maps, core_ids=list(range(c.CORES)))
    out = np.empty((B, S, D), np.float32)
    for core in range(c.CORES):
        b = core // c.GC
        j = core % c.GC
        out[b, j * c.TQ:(j + 1) * c.TQ] = np.asarray(res.results[core]["y_out"])
    return out
